# version 13
# baseline (speedup 1.0000x reference)
import contextlib
import numpy as np
import concourse.bass as bass
import concourse.mybir as mybir
from concourse.bass_utils import run_bass_kernel_spmd
from concourse.ap import AP

F32 = mybir.dt.float32
BF16 = mybir.dt.bfloat16
U8 = mybir.dt.uint8
AF = mybir.ActivationFunctionType
ALU = mybir.AluOpType
AX = mybir.AxisListType

S = 2048
D = 1024
NH = 16
HD = 64
E = 8
FF = 4096
NT = S // 128
NCH = D // 128
ALPHA = 0.125

_BUILD_PHASES = None
_N_CORES = 8

CAP = 212736
OFF_FSCR = 0
OFF_SM = 12288
OFF_SMB = 16384
OFF_REL = 20480
OFF_EYE = 28672
OFF_ONESR = 28928
OFF_ONESC = 29184
OFF_CM = 29248
OFF_CZ = 30272
OFF_WS = 30720
OFF_ATTN = 47104
OFF_ATTNT = 55296
OFF_RELB = 63488
OFF_SLAB = 71680
OFF_WTEB = 80384
OFF_BCA = 88576
OFF_PAIR = 98816
OFF_BIGA = 111360
OFF_BIGB = 176896


class Sem:
    def __init__(self, handle):
        self.h = handle
        self.n = 0

    def inc(self, kk=1):
        self.n += kk
        return self.n


class K:
    pass


def build_program():
    nc = bass.Bass("TRN2", target_bir_lowering=False)
    k = K()
    k.nc = nc
    k.stack = contextlib.ExitStack()

    di = {}

    def inp(name, shape):
        di[name] = nc.dram_tensor(name, list(shape), F32, kind="ExternalInput")

    inp("x", (S, D))
    inp("l0_wqT", (D, D)); inp("l0_wkT", (D, D)); inp("l0_wvT", (D, D)); inp("l0_woT", (D, D))
    inp("l0_bq", (D,)); inp("l0_bk", (D,)); inp("l0_bv", (D,)); inp("l0_bo", (D,))
    inp("l0_relextT", (HD, 4095))
    inp("l0_M", (128, NH // 2 * HD))
    inp("l0_z", (128, NH // 2))
    inp("l0_gate_alpha", (1, 1))
    inp("l0_gwT", (D, E))
    inp("l0_w1P", (E * 32, 128, 1024)); inp("l0_w3P", (E * 32, 128, 1024))
    inp("l0_w2P", (E * 16, 128, 2048))
    inp("l1_wqT", (D, D)); inp("l1_wkT", (D, D)); inp("l1_wvT", (D, D))
    inp("l1_bq", (D,)); inp("l1_bk", (D,)); inp("l1_bv", (D,))
    inp("l1_gamma", (1, 1))
    inp("f1_ln1_g", (D,)); inp("f1_ln1_b", (D,)); inp("f1_ln2_g", (D,)); inp("f1_ln2_b", (D,))
    inp("f1_wq", (D, D)); inp("f1_wk", (D, D)); inp("f1_wv", (D, D)); inp("f1_wo", (D, D))
    inp("f1_bq", (D,)); inp("f1_bk", (D,)); inp("f1_bv", (D,)); inp("f1_bo", (D,))
    inp("f1_gwT", (D, E)); inp("f1_gb", (E,))
    inp("f1_w1P", (E * 32, 128, 1024)); inp("f1_b1", (E, FF))
    inp("f1_w2P", (E * 16, 128, 2048)); inp("f1_b2", (E, D))
    inp("l0_bqc", (128, 8)); inp("l0_bkc", (128, 8))
    inp("f1_b1c", (128, 256)); inp("f1_b2c", (128, 64))
    inp("f1_bqc", (128, 8)); inp("f1_bkc", (128, 8)); inp("f1_bvc", (128, 8))
    inp("l1_bvc", (128, 8))
    inp("f1_ln2_gc", (128, 8)); inp("f1_ln2_bc", (128, 8))
    k.di = di

    k.out = nc.dram_tensor("out", [S, D], F32, kind="ExternalOutput")
    k.sc_x1 = nc.dram_tensor("sc_x1", [S, D], F32)
    k.sc_xn1 = nc.dram_tensor("sc_xn1", [S, D], BF16)
    k.sc_xpr = nc.dram_tensor("sc_xpr", [S, D], F32)
    k.sc_acc0 = nc.dram_tensor("sc_acc0", [NCH, 128, S], BF16)
    k.sc_satt = nc.dram_tensor("sc_satt", [S, D], BF16)
    k.sc_sl1 = nc.dram_tensor("sc_sl1", [S, D], BF16)
    k.sc_moe1 = nc.dram_tensor("sc_moe1", [NCH, 128, S], BF16)
    k.sc_slab = nc.dram_tensor("sc_slab", [2, 2, NT, 128, 2176], BF16)

    k.sb = nc.alloc_sbuf_tensor("arena", [128, CAP], U8)
    ar = k.sb

    def V(off, n_elem, dt=BF16, p0=0, p1=128):
        esz = 2 if dt in (BF16,) else 4
        return ar[p0:p1, off:off + n_elem * esz].bitcast(dt)

    k.V = V
    k.fscr = V(OFF_FSCR, 3072, F32)
    k.sm = V(OFF_SM, 1024, F32)
    k.smb = V(OFF_SMB, 2048)
    k.relextT = V(OFF_REL, 4095)
    k.eye = V(OFF_EYE, 128)
    k.ones_row = V(OFF_ONESR, 128, BF16, 0, 1)
    k.ones_all = V(OFF_ONESR, 128, BF16, 0, 128)
    k.ones_col = V(OFF_ONESC, 1)
    k.cM = V(OFF_CM, 512)
    k.cz = V(OFF_CZ, 8, F32)
    k.bca = [V(OFF_BCA + i * 2048, 1024) for i in range(4)]
    k.onesel = V(OFF_BCA + 4 * 2048, 1024, BF16, 0, 8)
    k.eyef = V(OFF_SM + 896 * 4, 128, F32)
    k.onesf_row = V(OFF_SM + 768 * 4, 128, F32, 0, 1)
    k.onesf_col = V(OFF_SM + 767 * 4, 1, F32)
    k.gwTf = V(OFF_SM + 576 * 4, 64, F32)
    k.csb = V(OFF_SM + 640 * 4, 8, F32)
    k.Kb = V(OFF_SM + 648 * 4, 8, F32)
    k.xfT = [V(OFF_WTEB, 2048), V(OFF_WTEB + 4096, 2048)]
    k.gwh = V(OFF_SMB + 2048, 64)
    k.gwl = V(OFF_SMB + 2176, 64)
    k.xhi = [V(OFF_RELB, 1024), V(OFF_RELB + 4096, 1024)]
    k.xlo = [V(OFF_RELB + 2048, 1024), V(OFF_RELB + 6144, 1024)]

    k.P = [nc.alloc_psum_tensor(f"P{i}", [128, 1024], F32) for i in range(4)]
    k.Pb = [p.ap().bitcast(BF16) for p in k.P]

    k.sems = {}
    for n in ["ld", "wq", "tn", "vc", "sc", "gp", "st", "dt", "ax", "ax2", "ax3", "ax4"]:
        k.sems[n] = Sem(k.stack.enter_context(nc.semaphore(n)))
    k.block = k.stack.enter_context(nc.Block())
    return nc, k



def prepare_host_inputs(inputs):
    g = {}
    x = np.ascontiguousarray(np.asarray(inputs["L0_attn_hidden_states"], dtype=np.float32))
    g["__x_full"] = x
    t = lambda a: np.ascontiguousarray(np.asarray(a, dtype=np.float32).T)
    cp = lambda a: np.ascontiguousarray(np.asarray(a, dtype=np.float32))
    g["l0_wqT"] = t(inputs["L0_attn_Wq"]); g["l0_wkT"] = t(inputs["L0_attn_Wk"])
    g["l0_wvT"] = t(inputs["L0_attn_Wv"]); g["l0_woT"] = t(inputs["L0_attn_Wo"])
    g["l0_bq"] = cp(inputs["L0_attn_bq"]); g["l0_bk"] = cp(inputs["L0_attn_bk"])
    g["l0_bv"] = cp(inputs["L0_attn_bv"]); g["l0_bo"] = cp(inputs["L0_attn_bo"])
    rel = np.asarray(inputs["L0_attn_rel_emb"], dtype=np.float32)
    relf = rel[::-1]
    idx = np.clip(np.arange(4095) - 1023, 0, 2048)
    g["l0_relextT"] = np.ascontiguousarray(relf[idx].T)
    M = np.asarray(inputs["L0_attn_M"], dtype=np.float32)
    z = np.asarray(inputs["L0_attn_z"], dtype=np.float32)
    Mp = np.zeros((128, NH // 2 * HD), np.float32)
    zp = np.zeros((128, NH // 2), np.float32)
    for p in range(NH // 2):
        Mp[0:64, p * 64:(p + 1) * 64] = M[2 * p]
        Mp[64:128, p * 64:(p + 1) * 64] = M[2 * p + 1]
        zp[0:64, p] = z[2 * p]
        zp[64:128, p] = z[2 * p + 1]
    g["l0_M"] = Mp; g["l0_z"] = zp
    g["l0_gate_alpha"] = np.asarray(inputs["L0_attn_gate_alpha"], dtype=np.float32).reshape(1, 1)
    g["l0_gwT"] = t(inputs["L0_ffn_gate_w"])
    packh = lambda wT: np.ascontiguousarray(
        wT.reshape(E, NCH, 128, 2, 16, 128).transpose(0, 3, 4, 2, 1, 5)
        .reshape(E * 32, 128, NCH * 128))
    packo = lambda w2T: np.ascontiguousarray(
        w2T.reshape(E, 2, 16, 128, 8, 128).transpose(0, 1, 4, 3, 2, 5)
        .reshape(E * 16, 128, 16 * 128))
    g["l0_w1P"] = packh(np.asarray(inputs["L0_ffn_w1"], np.float32).transpose(0, 2, 1))
    g["l0_w3P"] = packh(np.asarray(inputs["L0_ffn_w3"], np.float32).transpose(0, 2, 1))
    g["l0_w2P"] = packo(np.asarray(inputs["L0_ffn_w2"], np.float32).transpose(0, 2, 1))
    g["l1_wqT"] = t(inputs["L1_attn_Wq"]); g["l1_wkT"] = t(inputs["L1_attn_Wk"])
    g["l1_wvT"] = t(inputs["L1_attn_Wv"])
    g["l1_bq"] = cp(inputs["L1_attn_bq"]); g["l1_bk"] = cp(inputs["L1_attn_bk"])
    g["l1_bv"] = cp(inputs["L1_attn_bv"])
    g["l1_gamma"] = np.asarray(inputs["L1_attn_gamma"], np.float32).reshape(1, 1)
    g["f1_ln1_g"] = cp(inputs["L1_ffn_ln1_g"]); g["f1_ln1_b"] = cp(inputs["L1_ffn_ln1_b"])
    g["f1_ln2_g"] = cp(inputs["L1_ffn_ln2_g"]); g["f1_ln2_b"] = cp(inputs["L1_ffn_ln2_b"])
    g["f1_wq"] = cp(inputs["L1_ffn_wq"]); g["f1_wk"] = cp(inputs["L1_ffn_wk"])
    g["f1_wv"] = cp(inputs["L1_ffn_wv"]); g["f1_wo"] = cp(inputs["L1_ffn_wo"])
    g["f1_bq"] = cp(inputs["L1_ffn_bq"]); g["f1_bk"] = cp(inputs["L1_ffn_bk"])
    g["f1_bv"] = cp(inputs["L1_ffn_bv"]); g["f1_bo"] = cp(inputs["L1_ffn_bo"])
    g["f1_gwT"] = cp(inputs["L1_ffn_gate_w"])
    g["f1_gb"] = cp(inputs["L1_ffn_gate_b"])
    g["f1_w1P"] = packh(np.asarray(inputs["L1_ffn_w1"], np.float32))
    g["f1_b1"] = cp(inputs["L1_ffn_b1"])
    g["f1_w2P"] = packo(np.asarray(inputs["L1_ffn_w2"], np.float32))
    g["f1_b2"] = cp(inputs["L1_ffn_b2"])
    colmaj = lambda a: np.ascontiguousarray(np.asarray(a, np.float32).reshape(-1, 128).T)
    g["l0_bqc"] = colmaj(g["l0_bq"]); g["l0_bkc"] = colmaj(g["l0_bk"])
    g["f1_b1c"] = np.ascontiguousarray(
        g["f1_b1"].reshape(E, 32, 128).transpose(2, 0, 1).reshape(128, 256))
    g["f1_b2c"] = np.ascontiguousarray(
        g["f1_b2"].reshape(E, 8, 128).transpose(2, 0, 1).reshape(128, 64))
    g["f1_bqc"] = colmaj(g["f1_bq"]); g["f1_bkc"] = colmaj(g["f1_bk"])
    g["f1_bvc"] = colmaj(g["f1_bv"])
    g["l1_bvc"] = colmaj(g["l1_bv"])
    g["f1_ln2_gc"] = colmaj(g["f1_ln2_g"]); g["f1_ln2_bc"] = colmaj(g["f1_ln2_b"])
    return g


def build_partial(phases=None, dumps=()):
    global _BUILD_PHASES
    old = _BUILD_PHASES
    _BUILD_PHASES = phases
    try:
        nc, k = build_program()
        emit_all(k)
        blk, st = k.block, k.sems["st"]
        for name, getter, dt in dumps:
            view = getter(k)
            if not isinstance(view, AP):
                view = view.ap()
            dt_out = nc.dram_tensor("dbg_" + name, list(view.shape), dt, kind="ExternalOutput")

            @blk.gpsimd
            def _(gp, view=view, dt_out=dt_out):
                for snm in ["vc", "sc", "tn", "gp", "dt", "ld", "wq"]:
                    if k.sems[snm].n:
                        gp.wait_ge(k.sems[snm].h, k.sems[snm].n)
                gp.dma_start(out=dt_out.ap(), in_=view).then_inc(st.h, 16)
            st.inc(16)

        if phases is not None and "p8" not in phases:
            @blk.gpsimd
            def _(gp):
                gp.dma_start(out=k.out.ap()[0:128, :], in_=k.fscr[:, 0:1024]).then_inc(st.h, 16)
            st.inc(16)
        finish_block(k)
        return nc, k
    finally:
        _BUILD_PHASES = old


def kernel(**inputs):
    g = prepare_host_inputs(inputs)
    xf = g.pop("__x_full")
    nc, k = build_program()
    emit_all(k)
    finish_block(k)
    in_maps = []
    for b in range(_N_CORES):
        m = dict(g)
        m["x"] = np.ascontiguousarray(xf[b])
        in_maps.append(m)
    res = run_bass_kernel_spmd(nc, in_maps, core_ids=list(range(_N_CORES)))
    return np.stack([np.asarray(r["out"]) for r in res.results], axis=0)



def finish_block(k):
    blk, st = k.block, k.sems["st"]

    @blk.sync
    def _(sync):
        sync.wait_ge(st.h, st.n)
    k.stack.close()


def emit_all(k):
    ph = _BUILD_PHASES
    on = lambda p: (ph is None) or (p in ph)
    emit_load_consts(k)
    if on("p0"):
        emit_gate_prep(k, 0)
        xnT = k.V(OFF_BIGB, 16384)
        tok = k.V(OFF_BIGA, 16384)
        emit_ln(k, src_ap=k.di["x"].ap(), dst_tok=tok, dst_xnT=xnT, gate=True)
        k.xnT = xnT
    if on("p1"):
        emit_moe_l0(k)
    if on("p2"):
        emit_attn_l0(k)
    if on("p3"):
        emit_combine_l0(k)
    if on("p4"):
        emit_lin_attn_l1(k)
    if on("p5"):
        emit_ffn1_attn(k)
    if on("p6p7"):
        emit_moe_l1(k)
    if on("p8"):
        emit_final(k)


def emit_load_consts(k):
    blk, di = k.block, k.di
    ld, gp_s = k.sems["ld"], k.sems["gp"]

    @blk.gpsimd
    def _(gp):
        gp.dma_start(out=k.cM[:, :], in_=di["l0_M"].ap()).then_inc(ld.h, 16)
        gp.dma_start(out=k.cz[:, :], in_=di["l0_z"].ap()).then_inc(ld.h, 16)
        gp.dma_start(out=k.sm[0:1, 8:9], in_=di["l0_gate_alpha"].ap()).then_inc(ld.h, 16)
        gp.dma_start(out=k.sm[0:1, 9:10], in_=di["l1_gamma"].ap()).then_inc(ld.h, 16)
        gp.memset(k.eye, 0.0).then_inc(gp_s.h, 1)
        gp.wait_ge(gp_s.h, gp_s.n + 1)
        gp.affine_select(out=k.eye, in_=k.eye, compare_op=ALU.not_equal,
                         fill=1.0, base=0, pattern=[[-1, 128]], channel_multiplier=1)
        gp.memset(k.ones_all, 1.0)
        gp.memset(k.onesel, 0.0).then_inc(gp_s.h, 1)
        gp.wait_ge(gp_s.h, gp_s.n + 2)
        gp.affine_select(out=k.onesel.rearrange("p (e j) -> p e j", e=8),
                         in_=k.onesel.rearrange("p (e j) -> p e j", e=8),
                         compare_op=ALU.not_equal, fill=1.0, base=0,
                         pattern=[[-1, 8], [0, 128]], channel_multiplier=1)
        gp.memset(k.onesf_row, 1.0)
        gp.memset(k.onesf_col, 1.0)
        gp.memset(k.sm[0:1, 14:15], 1024.0)
        gp.memset(k.eyef, 0.0).then_inc(gp_s.h, 1)
        gp.wait_ge(gp_s.h, gp_s.n + 3)
        gp.affine_select(out=k.eyef, in_=k.eyef, compare_op=ALU.not_equal,
                         fill=1.0, base=0, pattern=[[-1, 128]], channel_multiplier=1)
        gp.memset(k.ones_col, 1.0).then_inc(gp_s.h, 1)
    ld.inc(64)
    gp_s.inc(3)
    k.v_consts_g = gp_s.inc(1)
    k.v_consts_ld = ld.n



def bcast_row(k, dram_ap, dst_view, width, row_off=0):
    blk = k.block
    ld, tn, vc = k.sems["ld"], k.sems["tn"], k.sems["vc"]
    row = k.smb[0:1, row_off:row_off + width]

    @blk.gpsimd
    def _(gp):
        gp.dma_start(out=row, in_=dram_ap).then_inc(ld.h, 16)
    ldv = ld.inc(16)

    @blk.tensor
    def _(te):
        te.wait_ge(ld.h, ldv)
        te.wait_ge(k.sems["gp"].h, k.v_consts_g)
        te.wait_ge(vc.h, vc.n)
        for j in range((width + 511) // 512):
            w = min(512, width - j * 512)
            mm = te.matmul(out=k.P[3][:, j * 512:j * 512 + w], lhsT=k.ones_row,
                           rhs=row[:, j * 512:j * 512 + w], start=True, stop=True)
        mm.then_inc(tn.h, 1)
    tv = tn.inc(1)

    @blk.vector
    def _(v):
        v.wait_ge(tn.h, tv)
        v.tensor_copy(out=dst_view, in_=k.P[3][:, 0:width]).then_inc(vc.h, 1)
    return vc.inc(1)


def emit_gate_prep(k, layer):
    blk = k.block
    ld, tn, vc, sc, gp_s = (k.sems[n] for n in ["ld", "tn", "vc", "sc", "gp"])
    gwd = k.di["l0_gwT"] if layer == 0 else k.di["f1_gwT"]
    gcol = k.sm[:, 656:664]
    bcol = k.sm[:, 664:672]
    gbrow = k.sm[0:1, 672:680]
    cs_row = k.sm[0:1, 744:752]
    K_row = k.sm[0:1, 752:760]

    @blk.gpsimd
    def _(gp):
        gp.dma_start(out=k.gwTf.rearrange("p (c e) -> p c e", c=NCH),
                     in_=gwd.ap().rearrange("(c p) e -> p c e", p=128)).then_inc(ld.h, 16)
        if layer == 1:
            gp.dma_start(out=gcol, in_=k.di["f1_ln2_gc"].ap()).then_inc(ld.h, 16)
            gp.dma_start(out=bcol, in_=k.di["f1_ln2_bc"].ap()).then_inc(ld.h, 16)
            gp.dma_start(out=gbrow, in_=k.di["f1_gb"].ap().unsqueeze(0)).then_inc(ld.h, 16)
    ldv = ld.inc(64 if layer == 1 else 16)

    @blk.tensor
    def _(te, ldv=ldv):
        te.wait_ge(ld.h, ldv)
        te.wait_ge(gp_s.h, k.v_consts_g)
        te.wait_ge(vc.h, vc.n)
        if layer == 1:
            for c in range(NCH):
                mm = te.matmul(out=k.P[2][0:1, 512:520], lhsT=bcol[:, c:c + 1],
                               rhs=k.gwTf[:, c * 8:(c + 1) * 8],
                               start=(c == 0), stop=(c == NCH - 1))
            mm.then_inc(tn.h, 1)
        else:
            te.nop().then_inc(tn.h, 1)
    t_K = tn.inc(1)

    if layer == 1:
        @blk.vector
        def _(v, t_K=t_K):
            v.wait_ge(tn.h, t_K)
            v.tensor_tensor(out=K_row, in0=k.P[2][0:1, 512:520], in1=gbrow,
                            op=ALU.add).then_inc(vc.h, 1)
            for c in range(NCH):
                mm = v.tensor_scalar(out=k.gwTf[:, c * 8:(c + 1) * 8],
                                in0=k.gwTf[:, c * 8:(c + 1) * 8],
                                scalar1=gcol[:, c:c + 1], scalar2=None, op0=ALU.mult)
            mm.then_inc(vc.h, 1)
        v_scaled = vc.inc(2)
    else:
        v_scaled = vc.n

    @blk.tensor
    def _(te, t_K=t_K, v_scaled=v_scaled, layer=layer):
        te.wait_ge(tn.h, t_K)
        if layer == 1:
            te.wait_ge(vc.h, v_scaled)
        for c in range(NCH):
            mm = te.matmul(out=k.P[2][0:1, 0:8], lhsT=k.onesf_col,
                           rhs=k.gwTf[:, c * 8:(c + 1) * 8],
                           start=(c == 0), stop=(c == NCH - 1))
        mm.then_inc(tn.h, 1)
    t_cs = tn.inc(1)

    @blk.vector
    def _(v, t_cs=t_cs):
        v.wait_ge(tn.h, t_cs)
        v.tensor_copy(out=cs_row, in_=k.P[2][0:1, 0:8]).then_inc(vc.h, 1)
    v_cs = vc.inc(1)

    @blk.tensor
    def _(te, v_cs=v_cs):
        te.wait_ge(vc.h, v_cs)
        te.matmul(out=k.P[2][:, 0:8], lhsT=k.onesf_row, rhs=cs_row,
                  start=True, stop=True).then_inc(tn.h, 1)
        if True:
            pass
    t_bc = tn.inc(1)

    @blk.vector
    def _(v, t_bc=t_bc, layer=layer):
        v.wait_ge(tn.h, t_bc)
        v.tensor_copy(out=k.csb, in_=k.P[2][:, 0:8]).then_inc(vc.h, 1)
    v_bc = vc.inc(1)

    @blk.scalar
    def _(s_, v_bc=v_bc):
        s_.wait_ge(vc.h, v_bc)
        s_.activation(out=k.gwh, in_=k.gwTf, func=AF.Identity).then_inc(sc.h, 1)
    s_gh = sc.inc(1)

    @blk.vector
    def _(v, s_gh=s_gh):
        v.wait_ge(sc.h, s_gh)
        v.tensor_sub(out=k.gwl, in0=k.gwTf, in1=k.gwh).then_inc(vc.h, 1)
    v_bc = vc.inc(1)

    if layer == 1:
        @blk.tensor
        def _(te, v_bc=v_bc):
            te.wait_ge(vc.h, v_bc)
            te.matmul(out=k.P[2][:, 0:8], lhsT=k.onesf_row, rhs=K_row,
                      start=True, stop=True).then_inc(tn.h, 1)
        t_kb = tn.inc(1)

        @blk.vector
        def _(v, t_kb=t_kb):
            v.wait_ge(tn.h, t_kb)
            v.tensor_copy(out=k.Kb, in_=k.P[2][:, 0:8]).then_inc(vc.h, 1)
        v_bc = vc.inc(1)
    k.gate_prep = (v_bc, layer)


def emit_ln(k, src_ap, dst_tok, dst_xnT=None, g_row=None, b_row=None, eps=1e-5,
            stats_base=16, dst_tok_dram=None, gate=False, pre_wait=None):
    blk = k.block
    ld, tn, vc, sc, gp_s, dt_s = (k.sems[n] for n in ["ld", "tn", "vc", "sc", "gp", "dt"])
    st0 = stats_base
    sum_c = lambda t: k.sm[:, st0 + t:st0 + t + 1]
    ssq_c = lambda t: k.sm[:, st0 + NT + t:st0 + NT + t + 1]
    nmr_c = lambda t: k.sm[:, st0 + 2 * NT + t:st0 + 2 * NT + t + 1]
    rst_c = lambda t: k.sm[:, st0 + 3 * NT + t:st0 + 3 * NT + t + 1]
    var_c = lambda t: k.sm[:, st0 + 4 * NT + t:st0 + 4 * NT + t + 1]
    stage = [k.fscr[:, 0:1024], k.fscr[:, 1024:2048], k.fscr[:, 2048:3072]]
    trash = k.V(OFF_PAIR, 1024, F32)
    done_s = {}
    done_v = {}
    trev = {}

    for t in range(NT):
        xt = stage[t % 3]

        @blk.gpsimd
        def _(gp, t=t, xt=xt, gate=gate, pre_wait=pre_wait):
            if t == 0 and pre_wait is not None:
                for snm, sval in pre_wait:
                    gp.wait_ge(k.sems[snm].h, sval)
            if t >= 3:
                gp.wait_ge(sc.h, done_s[t - 3])
                gp.wait_ge(vc.h, done_v[t - 3])
                if gate:
                    gp.wait_ge(vc.h, k._g_lo[t - 3])
            gp.dma_start(out=xt, in_=src_ap[t * 128:(t + 1) * 128, :]).then_inc(ld.h, 16)
        ldv = ld.inc(16)

        @blk.vector
        def _(v, t=t, xt=xt, ldv=ldv):
            v.wait_ge(ld.h, ldv)
            v.reduce_sum(out=sum_c(t), in_=xt, axis=AX.X)
            v.scalar_tensor_tensor(out=trash, in0=xt, scalar=1.0, in1=xt,
                                   op0=ALU.mult, op1=ALU.mult,
                                   accum_out=ssq_c(t)).then_inc(vc.h, 1)
        vsum = vc.inc(1)

        @blk.gpsimd
        def _(gp, t=t, vsum=vsum):
            gp.wait_ge(vc.h, vsum)
            gp.tensor_scalar(out=nmr_c(t), in0=sum_c(t), scalar1=-1.0 / D,
                             scalar2=None, op0=ALU.mult)
            gp.tensor_tensor(out=var_c(t), in0=nmr_c(t), in1=nmr_c(t), op=ALU.mult)
            gp.tensor_scalar(out=ssq_c(t), in0=ssq_c(t), scalar1=1.0 / D,
                             scalar2=None, op0=ALU.mult)
            gp.tensor_tensor(out=var_c(t), in0=ssq_c(t), in1=var_c(t), op=ALU.subtract)
            gp.tensor_scalar(out=var_c(t), in0=var_c(t), scalar1=float(eps),
                             scalar2=None, op0=ALU.add).then_inc(gp_s.h, 1)
        gpv = gp_s.inc(1)

        @blk.scalar
        def _(s, t=t, gpv=gpv):
            s.wait_ge(gp_s.h, gpv)
            s.activation(out=rst_c(t), in_=var_c(t), func=AF.Sqrt).then_inc(sc.h, 1)
        scv2 = sc.inc(1)

        @blk.vector
        def _(v, t=t, scv2=scv2):
            v.wait_ge(sc.h, scv2)
            v.reciprocal(out=rst_c(t), in_=rst_c(t)).then_inc(vc.h, 1)
        vcv0 = vc.inc(1)

        @blk.gpsimd
        def _(gp, t=t, vcv0=vcv0):
            gp.wait_ge(vc.h, vcv0)
            gp.tensor_tensor(out=nmr_c(t), in0=nmr_c(t), in1=rst_c(t),
                             op=ALU.mult).then_inc(gp_s.h, 1)
        gpv2 = gp_s.inc(1)

        if gate:
            L_t = k.sm[:, 96 + t * 8:96 + t * 8 + 8]
            a_t = k.sm[:, 256 + t * 8:256 + t * 8 + 8]
            tr8 = k.sm[:, 224:232]

            xhi_t = k.xhi[t % 2]
            xlo_t = k.xlo[t % 2]
            xfT_t = k.xfT[t % 2]
            gpb = k.Pb[1] if t % 2 == 0 else k.Pb[3]

            @blk.vector
            def _(v, t=t, xt=xt, xhi_t=xhi_t, xlo_t=xlo_t, ldv=ldv):
                v.wait_ge(ld.h, ldv)
                if t >= 2:
                    v.wait_ge(tn.h, k._gx_tr[t - 2])
                v.tensor_copy(out=xhi_t, in_=xt)
                v.tensor_sub(out=xlo_t, in0=xt, in1=xhi_t).then_inc(vc.h, 1)
            v_lo = vc.inc(1)
            k._g_lo = getattr(k, "_g_lo", {})
            k._g_lo[t] = v_lo

            @blk.tensor
            def _(te, t=t, v_lo=v_lo, xhi_t=xhi_t, xlo_t=xlo_t, gpb=gpb):
                te.wait_ge(gp_s.h, k.v_consts_g)
                te.wait_ge(vc.h, v_lo)
                if t == 0:
                    te.wait_ge(vc.h, k.gate_prep[0])
                if t >= 2:
                    te.wait_ge(vc.h, k._gx_ev[t - 2])
                for c in range(NCH):
                    te.transpose(out=gpb[:, c * 128:(c + 1) * 128],
                                 in_=xhi_t[:, c * 128:(c + 1) * 128], identity=k.eye)
                for c in range(NCH):
                    mm = te.transpose(out=gpb[:, 1024 + c * 128:1024 + (c + 1) * 128],
                                      in_=xlo_t[:, c * 128:(c + 1) * 128], identity=k.eye)
                mm.then_inc(tn.h, 1)
            k._gx_tr = getattr(k, "_gx_tr", {})
            k._gx_tr[t] = tn.inc(1)

            @blk.vector
            def _(v, t=t, xfT_t=xfT_t, gpb=gpb):
                v.wait_ge(tn.h, k._gx_tr[t])
                if t >= 2:
                    v.wait_ge(tn.h, k._gL_mm[t - 2])
                v.tensor_copy(out=xfT_t, in_=gpb[:, 0:2048]).then_inc(vc.h, 1)
            k._gx_ev = getattr(k, "_gx_ev", {})
            k._gx_ev[t] = vc.inc(1)

            @blk.tensor
            def _(te, t=t, xfT_t=xfT_t):
                te.wait_ge(vc.h, k._gx_ev[t])
                if t >= 2:
                    te.wait_ge(vc.h, k._gL_ev[t - 2])
                nmm = 0
                for c in range(NCH):
                    hiT = xfT_t[:, c * 128:(c + 1) * 128]
                    loT = xfT_t[:, 1024 + c * 128:1024 + (c + 1) * 128]
                    gh = k.gwh[:, c * 8:(c + 1) * 8]
                    gl = k.gwl[:, c * 8:(c + 1) * 8]
                    for lhs_, rhs_ in ((hiT, gh), (hiT, gl), (loT, gh)):
                        mm = te.matmul(out=k.P[2][:, (t % 2) * 512:(t % 2) * 512 + 8],
                                       lhsT=lhs_, rhs=rhs_,
                                       start=(nmm == 0), stop=(nmm == 3 * NCH - 1))
                        nmm += 1
                mm.then_inc(tn.h, 1)
            k._gL_mm = getattr(k, "_gL_mm", {})
            k._gL_mm[t] = tn.inc(1)

            @blk.vector
            def _(v, t=t, gpv2=gpv2, a_t=a_t, L_t=L_t, tr8=tr8):
                v.wait_ge(tn.h, k._gL_mm[t])
                v.wait_ge(gp_s.h, gpv2)
                v.tensor_scalar(out=tr8, in0=k.P[2][:, (t % 2) * 512:(t % 2) * 512 + 8],
                                scalar1=rst_c(t), scalar2=None, op0=ALU.mult)
                v.scalar_tensor_tensor(out=(a_t if k.gate_prep[1] == 1 else L_t),
                                       in0=k.csb, scalar=nmr_c(t), in1=tr8,
                                       op0=ALU.mult, op1=ALU.add).then_inc(vc.h, 1)
            k._gL_ev = getattr(k, "_gL_ev", {})
            k._gL_ev[t] = vc.inc(1)

            if k.gate_prep[1] == 1:
                @blk.gpsimd
                def _(gp, t=t, a_t=a_t, L_t=L_t):
                    gp.wait_ge(vc.h, k._gL_ev[t])
                    gp.tensor_tensor(out=L_t, in0=a_t, in1=k.Kb,
                                     op=ALU.add).then_inc(gp_s.h, 1)
                k._gL_fin = getattr(k, "_gL_fin", {})
                k._gL_fin[t] = gp_s.inc(1)

        tokdst = dst_tok[:, t * D:(t + 1) * D]
        if g_row is None:
            @blk.vector
            def _(v, t=t, xt=xt, gpv2=gpv2, tokdst=tokdst):
                v.wait_ge(gp_s.h, gpv2)
                v.tensor_scalar(out=tokdst, in0=xt, scalar1=rst_c(t),
                                scalar2=nmr_c(t), op0=ALU.mult,
                                op1=ALU.add).then_inc(vc.h, 1)
            done_v[t] = vc.inc(1)
            done_s[t] = sc.n
            dep_tr = ("vc", done_v[t])
        else:
            nrm = k.smb[:, 0:1024]

            @blk.vector
            def _(v, t=t, xt=xt, gpv2=gpv2, tokdst=tokdst, nrm=nrm):
                v.wait_ge(gp_s.h, gpv2)
                v.tensor_scalar(out=nrm, in0=xt, scalar1=rst_c(t),
                                scalar2=nmr_c(t), op0=ALU.mult, op1=ALU.add)
                v.tensor_tensor(out=nrm, in0=nrm, in1=g_row, op=ALU.mult)
                v.tensor_tensor(out=tokdst, in0=nrm,
                                in1=b_row, op=ALU.add).then_inc(vc.h, 1)
            done_v[t] = vc.inc(1)
            done_s[t] = sc.n
            dep_tr = ("vc", done_v[t])

        if dst_tok_dram is not None:
            eng, depv = dep_tr

            @blk.gpsimd
            def _(gp, t=t, eng=eng, depv=depv, tokdst=tokdst):
                gp.wait_ge(k.sems[eng].h, depv)
                gp.dma_start(out=dst_tok_dram[t * 128:(t + 1) * 128, :],
                             in_=tokdst).then_inc(dt_s.h, 16)
            dt_s.inc(16)

        if dst_xnT is not None:
            eng, depv = dep_tr
            xpb = k.Pb[0] if (gate or t % 2 == 0) else k.Pb[3]
            xdep = 1 if gate else 2

            @blk.tensor
            def _(te, t=t, eng=eng, depv=depv, tokdst=tokdst, xpb=xpb, xdep=xdep):
                te.wait_ge(k.sems[eng].h, depv)
                te.wait_ge(gp_s.h, k.v_consts_g)
                if t >= xdep:
                    te.wait_ge(vc.h, trev[t - xdep])
                for c in range(NCH):
                    mm = te.transpose(out=xpb[:, c * 128:(c + 1) * 128],
                                      in_=tokdst[:, c * 128:(c + 1) * 128],
                                      identity=k.eye)
                mm.then_inc(tn.h, 1)
            tnv = tn.inc(1)

            @blk.vector
            def _(v, t=t, tnv=tnv, dst_xnT=dst_xnT, xpb=xpb):
                v.wait_ge(tn.h, tnv)
                dst = dst_xnT.rearrange("p (c s) -> p c s", c=NCH)[:, :, t * 128:(t + 1) * 128]
                v.tensor_copy(out=dst,
                              in_=xpb[:, 0:1024].rearrange("p (c j) -> p c j", c=NCH)
                              ).then_inc(vc.h, 1)
            trev[t] = vc.inc(1)
    k.ln_done = dict(v=vc.n, s=sc.n, t=tn.n, g=gp_s.n, dt=dt_s.n)
    if gate:
        k.gate_ready = dict(v=vc.n, g=gp_s.n)
        for nm in ["_gx_tr", "_gx_ev", "_gL_mm", "_gL_ev", "_gL_fin"]:
            if hasattr(k, nm):
                delattr(k, nm)


def emit_moe_l0(k):
    _moe(k, layer=0)


def emit_moe_l1(k):
    g2v = bcast_row(k, k.di["f1_ln2_g"].ap().unsqueeze(0), k.bca[2], 1024, row_off=0)
    b2v = bcast_row(k, k.di["f1_ln2_b"].ap().unsqueeze(0), k.bca[3], 1024, row_off=1024)
    emit_gate_prep(k, 1)
    hn2T = k.V(OFF_BIGB, NCH * S)
    tok = k.V(OFF_BIGA, 16384)
    dep = getattr(k, "ffn1a_done", None)
    pw = [("vc", max(g2v, b2v))]
    if dep is not None:
        pw.append(("dt", dep["dt"]))
    emit_ln(k, src_ap=k.sc_xpr.ap(), dst_tok=tok, dst_xnT=hn2T,
            g_row=k.bca[2], b_row=k.bca[3], gate=True, pre_wait=pw)
    _moe(k, layer=1)


def _moe(k, layer):
    nc, blk = k.nc, k.block
    ld, wq, tn, vc, sc, gp_s, dt_s = (k.sems[n] for n in
                                      ["ld", "wq", "tn", "vc", "sc", "gp", "dt"])
    xT = k.V(OFF_BIGB, NCH * S)
    slab = k.V(OFF_BIGA, 16 * S)
    s1 = k.V(OFF_PAIR, 2048)
    tsb = k.V(OFF_PAIR + 4096, 2048)
    if layer == 0:
        W1DEP = 3
        w1off = [OFF_WS, OFF_WS + 4096, OFF_FSCR + 6400]
        w1x = [k.V(o, 1024, F32) for o in w1off]
        w1b = [k.V(o, 1024) for o in w1off]
    else:
        W1DEP = 4
        w1x = [k.V(OFF_WS + i * 4096, 1024, F32) for i in range(4)]
        w1b = [k.V(o, 1024) for o in
               (OFF_FSCR + 6400, OFF_FSCR + 8448, OFF_PAIR, OFF_PAIR + 2048)]
    w3x = [k.V(OFF_WS + 8192 + i * 4096, 1024, F32) for i in range(2)]
    w3b = [k.V(OFF_WS + 8192 + i * 4096, 1024) for i in range(2)]
    W2DEP = 5
    w2b = [k.V(off, 2048) for off in
           (OFF_REL, OFF_REL + 4096, OFF_BCA, OFF_BCA + 4096, OFF_PAIR + 8192)]
    accb = [k.V(OFF_ATTN + i * 4096, 2048) for i in range(8)]
    wtebuf = [k.V(OFF_WTEB + i * 4096, 2048) for i in range(2)]
    gwT_sb = k.smb[:, 1024:1088]
    gb_b = k.smb[:, 1088:1096]
    L = lambda t: k.sm[:, 96 + t * 8:96 + (t + 1) * 8]
    acc_dram = k.sc_acc0 if layer == 0 else k.sc_moe1
    w1d = k.di["l0_w1P"] if layer == 0 else k.di["f1_w1P"]
    w3d = k.di["l0_w3P"] if layer == 0 else None
    w2d = k.di["l0_w2P"] if layer == 0 else k.di["f1_w2P"]
    ax2, ax3 = k.sems["ax2"], k.sems["ax3"]
    k._moe_pre_tn = k.sems["tn"].n
    k._moe_pre_vc = k.sems["vc"].n
    gwd = k.di["l0_gwT"] if layer == 0 else k.di["f1_gwT"]
    b1c = k.sm[:, 256:512]
    b2c = k.sm[:, 512:576]

    gate_ev = {}
    gr = k.gate_ready
    if layer == 1:
        @blk.gpsimd
        def _(gp):
            gp.dma_start(out=b1c, in_=k.di["f1_b1c"].ap()).then_inc(ld.h, 16)
            gp.dma_start(out=b2c, in_=k.di["f1_b2c"].ap()).then_inc(ld.h, 16)
        ld.inc(32)

    FS = k.fscr
    absl = lambda t: FS[:, 128 + t * 8:128 + (t + 1) * 8]
    nm1 = lambda t: FS[:, 256 + t:257 + t]
    nm2 = lambda t: FS[:, 272 + t:273 + t]
    m1 = lambda t: FS[:, 288 + t:289 + t]
    m2 = lambda t: FS[:, 304 + t:305 + t]
    Z1 = FS[:, 320:336]
    Z2 = FS[:, 336:352]
    mu1 = FS[:, 352:368]
    mu2 = FS[:, 368:384]
    f1_ = lambda t: FS[:, 384 + t * 8:384 + (t + 1) * 8]
    d1_ = lambda t: FS[:, 512 + t * 8:512 + (t + 1) * 8]
    u1_ = lambda t: FS[:, 640 + t * 8:640 + (t + 1) * 8]
    e1_ = lambda t: FS[:, 768 + t * 8:768 + (t + 1) * 8]
    i1_ = lambda t: FS[:, 896 + t * 8:896 + (t + 1) * 8]
    lm_ = lambda t: FS[:, 1024 + t * 8:1024 + (t + 1) * 8]
    nt_ = lambda t: FS[:, 1152 + t * 8:1152 + (t + 1) * 8]
    i2_ = lambda t: FS[:, 1280 + t * 8:1280 + (t + 1) * 8]
    wt_ = lambda t: FS[:, 1408 + t * 8:1408 + (t + 1) * 8]
    tr8 = FS[:, 1536:1544]

    vg = gr["v"]
    if layer == 0:
        @blk.scalar
        def _(s_, vg=vg):
            s_.wait_ge(vc.h, vg)
            s_.wait_ge(gp_s.h, gr["g"])
            s_.activation(out=FS[:, 128:256], in_=k.sm[:, 96:224], func=AF.Abs
                          ).then_inc(sc.h, 1)
        s_abs = sc.inc(1)

        @blk.vector
        def _(v, vg=vg, s_abs=s_abs):
            v.wait_ge(vc.h, vg)
            v.wait_ge(sc.h, s_abs)
            for t in range(NT):
                v.reduce_max(out=nm1(t), in_=L(t), axis=AX.X, negate=True)
            for t in range(NT):
                v.tensor_scalar(out=m1(t), in0=nm1(t), scalar1=-1.0, scalar2=None,
                                op0=ALU.mult)
            for t in range(NT):
                v.tensor_scalar(out=i1_(t), in0=L(t), scalar1=m1(t), scalar2=None,
                                op0=ALU.is_ge)
            for t in range(NT):
                v.tensor_scalar(out=f1_(t), in0=absl(t), scalar1=m1(t), scalar2=0.02,
                                op0=ALU.max, op1=ALU.mult)
            for t in range(NT):
                v.tensor_scalar(out=d1_(t), in0=L(t), scalar1=m1(t), scalar2=-1.0,
                                op0=ALU.subtract, op1=ALU.mult)
            for t in range(NT):
                mm = v.tensor_tensor(out=u1_(t), in0=d1_(t), in1=f1_(t), op=ALU.is_le)
            mm.then_inc(vc.h, 1)
        v_r1 = vc.inc(1)

        @blk.scalar
        def _(s_, v_r1=v_r1):
            s_.wait_ge(vc.h, v_r1)
            for t in range(NT):
                s_.activation(out=e1_(t), in_=L(t), func=AF.Exp, bias=nm1(t))

            s_.activation(out=tr8, in_=L(0), func=AF.Identity).then_inc(sc.h, 1)
        s_r1 = sc.inc(1)

        @blk.vector
        def _(v, s_r1=s_r1):
            v.wait_ge(sc.h, s_r1)
            for t in range(NT):
                v.scalar_tensor_tensor(out=d1_(t), in0=e1_(t), scalar=1.0, in1=u1_(t),
                                       op0=ALU.mult, op1=ALU.mult,
                                       accum_out=Z1[:, t:t + 1])
            for t in range(NT):
                v.scalar_tensor_tensor(out=lm_(t), in0=i1_(t), scalar=-1e30, in1=L(t),
                                       op0=ALU.mult, op1=ALU.add)
            for t in range(NT):
                v.reduce_max(out=nm2(t), in_=lm_(t), axis=AX.X, negate=True)
            for t in range(NT):
                v.tensor_scalar(out=m2(t), in0=nm2(t), scalar1=-1.0, scalar2=None,
                                op0=ALU.mult)
            for t in range(NT):
                v.tensor_scalar(out=nt_(t), in0=i1_(t), scalar1=-1.0, scalar2=1.0,
                                op0=ALU.mult, op1=ALU.add)
            for t in range(NT):
                v.tensor_scalar(out=f1_(t), in0=absl(t), scalar1=m2(t), scalar2=0.02,
                                op0=ALU.max, op1=ALU.mult)
            for t in range(NT):
                v.tensor_scalar(out=d1_(t), in0=L(t), scalar1=m2(t), scalar2=-1.0,
                                op0=ALU.subtract, op1=ALU.mult)
            for t in range(NT):
                v.tensor_tensor(out=u1_(t), in0=d1_(t), in1=f1_(t), op=ALU.is_le)
            for t in range(NT):
                v.tensor_tensor(out=u1_(t), in0=u1_(t), in1=nt_(t), op=ALU.mult)
            for t in range(NT):
                mm = v.tensor_scalar(out=i2_(t), in0=L(t), scalar1=m2(t), scalar2=None,
                                op0=ALU.is_ge)
            mm.then_inc(vc.h, 1)
        v_r2 = vc.inc(1)

        @blk.scalar
        def _(s_, v_r2=v_r2):
            s_.wait_ge(vc.h, v_r2)
            for t in range(NT):
                s_.activation(out=e1_(t), in_=L(t), func=AF.Exp, bias=nm2(t))
            s_.activation(out=tr8, in_=L(0), func=AF.Identity).then_inc(sc.h, 1)
        s_r2 = sc.inc(1)

        @blk.vector
        def _(v, s_r2=s_r2):
            v.wait_ge(sc.h, s_r2)
            for t in range(NT):
                v.scalar_tensor_tensor(out=d1_(t), in0=e1_(t), scalar=1.0, in1=u1_(t),
                                       op0=ALU.mult, op1=ALU.mult,
                                       accum_out=Z2[:, t:t + 1])
            v.reciprocal(out=mu1, in_=Z1)
            v.reciprocal(out=mu2, in_=Z2)
            for t in range(NT):
                v.tensor_tensor(out=i2_(t), in0=i2_(t), in1=nt_(t), op=ALU.mult)
            for t in range(NT):
                v.tensor_scalar(out=i2_(t), in0=i2_(t), scalar1=mu2[:, t:t + 1],
                                scalar2=None, op0=ALU.mult)
            for t in range(NT):
                mm = v.scalar_tensor_tensor(out=wt_(t), in0=i1_(t), scalar=mu1[:, t:t + 1],
                                       in1=i2_(t), op0=ALU.mult, op1=ALU.add)
            mm.then_inc(vc.h, 1)
        v_route = vc.inc(1)
    else:
        @blk.vector
        def _(v, vg=vg):
            v.wait_ge(vc.h, vg)
            v.wait_ge(gp_s.h, gr["g"])
            for t in range(NT):
                v.reduce_max(out=m1(t), in_=L(t), axis=AX.X)
            for t in range(NT):
                mm = v.tensor_scalar(out=wt_(t), in0=L(t), scalar1=m1(t), scalar2=None,
                                op0=ALU.is_ge)
            mm.then_inc(vc.h, 1)
        v_route = vc.inc(1)

    wrow = k.smb[0:1, 0:2048 * 0 + 2048]
    tr_ev = {}
    for t in range(NT):
        @blk.tensor
        def _(te, t=t, v_route=v_route):
            te.wait_ge(vc.h, v_route)
            te.wait_ge(gp_s.h, k.v_consts_g)
            if t >= 1:
                te.wait_ge(vc.h, tr_ev[t - 1])
            mm = te.transpose(out=k.P[3][0:8, (t % 4) * 128:(t % 4) * 128 + 128],
                              in_=wt_(t), identity=k.eyef)
            mm.then_inc(tn.h, 1)
        tnv = tn.inc(1)

        @blk.vector
        def _(v, t=t, tnv=tnv):
            v.wait_ge(tn.h, tnv)
            v.tensor_copy(out=k.smb[0:8, t * 128:(t + 1) * 128],
                          in_=k.P[3][0:8, (t % 4) * 128:(t % 4) * 128 + 128]
                          ).then_inc(vc.h, 1)
        tr_ev[t] = vc.inc(1)
    v_wteT = vc.n

    NFH = 16
    h_ev = {}
    wdma = {}
    first_acc = True
    for e in range(E):
        wb = wtebuf[e % 2]

        @blk.tensor
        def _(te, e=e, v_wteT=v_wteT):
            te.wait_ge(vc.h, v_wteT)
            if e >= 1:
                te.wait_ge(vc.h, k._moe_vwb[e - 1])
                te.wait_ge(vc.h, k._moe_lastout[e - 1])
            if e >= 2:
                te.wait_ge(vc.h, k._moe_lastev[e - 2])
            lhs = k.onesel[:, e * 128:(e + 1) * 128]
            for j in range(2):
                mm = te.matmul(out=k.P[3][:, j * 512:(j + 1) * 512],
                               lhsT=lhs, rhs=k.smb[0:8, j * 512:(j + 1) * 512],
                               start=True, stop=True)
            mm.then_inc(tn.h, 1)
        tnv = tn.inc(1)

        @blk.vector
        def _(v, e=e, tnv=tnv, wb=wb):
            v.wait_ge(tn.h, tnv)
            v.tensor_copy(out=wb[:, 0:1024], in_=k.P[3][:, 0:1024]).then_inc(vc.h, 1)
        v_wb0 = vc.inc(1)

        @blk.tensor
        def _(te, e=e, v_wb0=v_wb0):
            te.wait_ge(vc.h, v_wb0)
            lhs = k.onesel[:, e * 128:(e + 1) * 128]
            for j in range(2, 4):
                mm = te.matmul(out=k.P[3][:, (j - 2) * 512:(j - 1) * 512],
                               lhsT=lhs, rhs=k.smb[0:8, j * 512:(j + 1) * 512],
                               start=True, stop=True)
            mm.then_inc(tn.h, 1)
        tnv2 = tn.inc(1)

        @blk.vector
        def _(v, e=e, tnv2=tnv2, wb=wb):
            v.wait_ge(tn.h, tnv2)
            v.tensor_copy(out=wb[:, 1024:2048], in_=k.P[3][:, 0:1024]).then_inc(vc.h, 1)
        v_wb = vc.inc(1)
        if not hasattr(k, "_moe_vwb"):
            k._moe_vwb = {}
        k._moe_vwb[e] = v_wb

        for Hf in range(2):
            fb = Hf * 2048
            for ft in range(NFH):
                par = ft % 2
                gi = (e * 2 + Hf) * NFH + ft

                widx = gi
                wpar = gi % W1DEP
                w1_on_sync = (layer == 0) or (gi % 2 == 0)
                w1_eng = blk.sync if w1_on_sync else blk.scalar
                w1_sem = ax2 if w1_on_sync else ax3

                @w1_eng
                def _(sy, wpar=wpar, gi=gi, widx=widx, w1_sem=w1_sem):
                    if gi >= W1DEP:
                        sy.wait_ge(tn.h, wdma[gi - W1DEP])
                    else:
                        sy.wait_ge(tn.h, k._moe_pre_tn)
                    sy.dma_start(out=w1x[wpar], in_=w1d.ap()[widx]
                                 ).then_inc(w1_sem.h, 16)
                axv2 = w1_sem.inc(16)
                axv3 = ax3.n
                wqv3 = wq.n
                w3_swdge = (w3d is not None) and (gi % 2 == 1)
                if w3d is not None and not w3_swdge:
                    @blk.scalar
                    def _(s_, par=par, gi=gi, widx=widx):
                        if gi >= 2:
                            s_.wait_ge(tn.h, wdma[gi - 2])
                        else:
                            s_.wait_ge(tn.h, k._moe_pre_tn)
                        s_.dma_start(out=w3x[par], in_=w3d.ap()[widx]
                                     ).then_inc(ax3.h, 16)
                    axv3 = ax3.inc(16)
                elif w3_swdge:
                    @blk.gpsimd
                    def _(gp, par=par, gi=gi, widx=widx):
                        if gi >= 2:
                            gp.wait_ge(tn.h, wdma[gi - 2])
                        else:
                            gp.wait_ge(tn.h, k._moe_pre_tn)
                        gp.dma_start(out=w3b[par], in_=w3d.ap()[widx]
                                     ).then_inc(wq.h, 16)
                    wqv3 = wq.inc(16)

                if layer == 0:
                    @blk.scalar
                    def _(s_, par=par, wpar=wpar, axv2=axv2, axv3=axv3,
                          w1_sem=w1_sem, w3_swdge=w3_swdge):
                        s_.wait_ge(w1_sem.h, axv2)
                        mm = s_.activation(out=w1b[wpar], in_=w1x[wpar],
                                           func=AF.Copy, scale=1.0)
                        if w3d is not None and not w3_swdge:
                            s_.wait_ge(ax3.h, axv3)
                            mm = s_.activation(out=w3b[par], in_=w3x[par],
                                               func=AF.Copy, scale=1.0)
                        mm.then_inc(sc.h, 1)
                    cast_ev = sc.inc(1)
                    cast_sem = sc
                else:
                    @blk.vector
                    def _(v, wpar=wpar, axv2=axv2, gi=gi, w1_sem=w1_sem):
                        v.wait_ge(w1_sem.h, axv2)
                        if gi >= W1DEP:
                            v.wait_ge(tn.h, wdma[gi - W1DEP])
                        v.tensor_copy(out=w1b[wpar],
                                      in_=w1x[wpar]).then_inc(vc.h, 1)
                    cast_ev = vc.inc(1)
                    cast_sem = vc

                @blk.tensor
                def _(te, e=e, ft=ft, par=par, wpar=wpar, gi=gi, cast_ev=cast_ev,
                      cast_sem=cast_sem, v_wb=v_wb, Hf=Hf, w3_swdge=w3_swdge,
                      wqv3=wqv3):
                    te.wait_ge(cast_sem.h, cast_ev)
                    if Hf == 0 and ft == 0:
                        te.wait_ge(vc.h, v_wb)
                    if gi >= 1:
                        te.wait_ge(sc.h, h_ev[(gi - 1, "s")])
                    for c in range(NCH):
                        lhs = w1b[wpar][:, c * 128:(c + 1) * 128]
                        for n_ in range(4):
                            pp = k.P[0] if n_ < 2 else k.P[1]
                            mm = te.matmul(out=pp[:, (n_ % 2) * 512:(n_ % 2 + 1) * 512],
                                           lhsT=lhs,
                                           rhs=xT[:, c * S + n_ * 512:c * S + (n_ + 1) * 512],
                                           start=(c == 0), stop=(c == NCH - 1))
                    mm.then_inc(tn.h, 1)
                    if w3d is not None:
                        if w3_swdge:
                            te.wait_ge(wq.h, wqv3)
                        if gi >= 1:
                            te.wait_ge(vc.h, h_ev[(gi - 1, "v")])
                        for c in range(NCH):
                            lhs = w3b[par][:, c * 128:(c + 1) * 128]
                            for n_ in range(4):
                                pp = k.P[2] if n_ < 2 else k.P[3]
                                mm = te.matmul(out=pp[:, (n_ % 2) * 512:(n_ % 2 + 1) * 512],
                                               lhsT=lhs,
                                               rhs=xT[:, c * S + n_ * 512:c * S + (n_ + 1) * 512],
                                               start=(c == 0), stop=(c == NCH - 1))
                        mm.then_inc(tn.h, 1)
                wdma[gi] = tn.inc(2 if w3d is not None else 1)
                tn_h1 = wdma[gi] - (1 if w3d is not None else 0)

                if layer == 0:
                    @blk.scalar
                    def _(s_, ft=ft, gi=gi, tn_h1=tn_h1):
                        s_.wait_ge(tn.h, tn_h1)
                        if gi >= 1:
                            s_.wait_ge(vc.h, h_ev[(gi - 1, "v")])
                        s_.activation(out=s1[:, 0:1024], in_=k.P[0][:, :], func=AF.Silu)
                        s_.activation(out=s1[:, 1024:2048], in_=k.P[1][:, :],
                                      func=AF.Silu).then_inc(sc.h, 1)
                    h_ev[(gi, "s")] = sc.inc(1)

                    @blk.vector
                    def _(v, ft=ft, gi=gi):
                        v.wait_ge(sc.h, h_ev[(gi, "s")])
                        v.wait_ge(tn.h, wdma[gi])
                        v.tensor_tensor(out=slab[:, ft * S:ft * S + 1024],
                                        in0=s1[:, 0:1024], in1=k.P[2][:, :], op=ALU.mult)
                        v.tensor_tensor(out=slab[:, ft * S + 1024:ft * S + 2048],
                                        in0=s1[:, 1024:2048], in1=k.P[3][:, :],
                                        op=ALU.mult).then_inc(vc.h, 1)
                    h_ev[(gi, "v")] = vc.inc(1)
                else:
                    @blk.scalar
                    def _(s_, e=e, fb=fb, ft=ft, gi=gi, tn_h1=tn_h1):
                        s_.wait_ge(tn.h, tn_h1)
                        if gi >= 1:
                            s_.wait_ge(sc.h, h_ev[(gi - 1, "s")])
                        bia = b1c[:, e * 32 + (fb // 128) + ft:e * 32 + (fb // 128) + ft + 1]
                        s_.activation(out=slab[:, ft * S:ft * S + 1024], in_=k.P[0][:, :],
                                      func=AF.Relu, bias=bia)
                        s_.activation(out=slab[:, ft * S + 1024:ft * S + 2048],
                                      in_=k.P[1][:, :], func=AF.Relu,
                                      bias=bia).then_inc(sc.h, 1)
                    h_ev[(gi, "s")] = sc.inc(1)
                    h_ev[(gi, "v")] = vc.n

            h_last_s = h_ev[((e * 2 + Hf) * NFH + NFH - 1, "s")]
            h_last_v = h_ev[((e * 2 + Hf) * NFH + NFH - 1, "v")]

            for dmt in range(NCH):
                par = dmt % 2
                pj = (k.P[0], k.P[1]) if par == 0 else (k.P[2], k.P[3])
                gj = (e * 2 + Hf) * NCH + dmt

                @blk.gpsimd
                def _(gp, e=e, fb=fb, dmt=dmt, gj=gj):
                    if gj >= W2DEP:
                        gp.wait_ge(tn.h, k._moe_out_mm[gj - W2DEP])
                    else:
                        gp.wait_ge(tn.h, k._moe_pre_tn)
                        gp.wait_ge(vc.h, k._moe_pre_vc)
                    gp.dma_start(out=w2b[gj % W2DEP],
                                 in_=w2d.ap()[(e * 2 + fb // 2048) * NCH + dmt]
                                 ).then_inc(wq.h, 16)
                wq.inc(16)
                wqv = wq.n
                if not hasattr(k, "_moe_out_mm"):
                    k._moe_out_mm = {}

                @blk.tensor
                def _(te, dmt=dmt, par=par, gj=gj, pj=pj, wqv=wqv,
                      h_last_s=h_last_s, h_last_v=h_last_v):
                    te.wait_ge(wq.h, wqv)
                    te.wait_ge(sc.h, h_last_s)
                    te.wait_ge(vc.h, h_last_v)
                    if gj >= 2:
                        te.wait_ge(vc.h, k._moe_out_ev[gj - 2])
                    for ftt in range(16):
                        lhs = w2b[gj % W2DEP][:, ftt * 128:(ftt + 1) * 128]
                        for n_ in range(4):
                            pp = pj[0] if n_ < 2 else pj[1]
                            mm = te.matmul(out=pp[:, (n_ % 2) * 512:(n_ % 2 + 1) * 512],
                                           lhsT=lhs,
                                           rhs=slab[:, ftt * S + n_ * 512:ftt * S + (n_ + 1) * 512],
                                           start=(ftt == 0), stop=(ftt == 15))
                    mm.then_inc(tn.h, 1)
                k._moe_out_mm[gj] = tn.inc(1)
                if not hasattr(k, "_moe_out_ev"):
                    k._moe_out_ev = {}

                @blk.vector
                def _(v, e=e, dmt=dmt, gj=gj, pj=pj, v_wb=v_wb, wb=wb,
                      first=(e == 0 and Hf == 0)):
                    v.wait_ge(tn.h, k._moe_out_mm[gj])
                    v.wait_ge(vc.h, v_wb)
                    dst = accb[dmt] if first else tsb
                    if layer == 0:
                        v.tensor_tensor(out=dst[:, 0:1024], in0=pj[0][:, :],
                                        in1=wb[:, 0:1024], op=ALU.mult)
                        mm = v.tensor_tensor(out=dst[:, 1024:2048], in0=pj[1][:, :],
                                             in1=wb[:, 1024:2048], op=ALU.mult)
                    else:
                        bia = b2c[:, e * 8 + dmt:e * 8 + dmt + 1]
                        v.scalar_tensor_tensor(out=dst[:, 0:1024], in0=pj[0][:, :],
                                               scalar=bia, in1=wb[:, 0:1024],
                                               op0=ALU.add, op1=ALU.mult)
                        mm = v.scalar_tensor_tensor(out=dst[:, 1024:2048],
                                                    in0=pj[1][:, :],
                                                    scalar=bia, in1=wb[:, 1024:2048],
                                                    op0=ALU.add, op1=ALU.mult)
                    if not first:
                        v.tensor_tensor(out=accb[dmt][:, 0:1024],
                                        in0=accb[dmt][:, 0:1024],
                                        in1=tsb[:, 0:1024], op=ALU.add)
                        mm = v.tensor_tensor(out=accb[dmt][:, 1024:2048],
                                             in0=accb[dmt][:, 1024:2048],
                                             in1=tsb[:, 1024:2048], op=ALU.add)
                    mm.then_inc(vc.h, 1)
                k._moe_out_ev[gj] = vc.inc(1)
        if not hasattr(k, "_moe_lastev"):
            k._moe_lastev = {}
            k._moe_lastout = {}
        k._moe_lastev[e] = vc.n
        k._moe_lastout[e] = vc.n

    vlast = vc.n

    @blk.gpsimd
    def _(gp, vlast=vlast):
        gp.wait_ge(vc.h, vlast)
        for dmt in range(8):
            gp.dma_start(out=acc_dram.ap()[dmt, :, :], in_=accb[dmt]
                         ).then_inc(dt_s.h, 16)
    dt_s.inc(128)
    k.moe_done = dict(dt=dt_s.h and dt_s.n, v=vc.n, s=sc.n, t=tn.n)
    for nm in ["_moe_out_mm", "_moe_out_ev", "_moe_acc_t", "_moe_lastev", "_moe_lastout", "_moe_vwb"]:
        if hasattr(k, nm):
            delattr(k, nm)


def emit_attn_l0(k):
    nc, blk = k.nc, k.block
    ld, wq, tn, vc, sc, gp_s, dt_s = (k.sems[n] for n in
                                      ["ld", "wq", "tn", "vc", "sc", "gp", "dt"])
    ax_s = k.sems["ax"]
    xT = k.V(OFF_BIGB, NCH * S)
    comb = k.V(OFF_BIGA, 16384)
    qT = k.V(OFF_PAIR, 2048)
    kT = k.V(OFF_PAIR + 4096, 2048)
    vaug = k.V(OFF_PAIR + 8192, 2080)
    sgm1 = k.V(OFF_WTEB, 2048)
    sgm2 = k.V(OFF_WTEB + 4096, 2048)
    attnA = k.V(OFF_ATTN, 2048)
    attnB = k.V(OFF_ATTN + 4096, 2048)
    attTA = k.V(OFF_ATTNT, 2048)
    attTB = k.V(OFF_ATTNT + 4096, 2048)
    relbA = k.V(OFF_RELB, 2048)
    relbB = k.V(OFF_RELB + 4096, 2048)
    slabA = k.V(OFF_SLAB, 2176)
    slabB = k.V(OFF_SLAB + 4352, 2176)
    slabA1 = k.V(OFF_FSCR + 2048, 2176)
    slabB1 = k.V(OFF_FSCR + 6400, 2176)
    wqb = [k.V(OFF_WS + i * 2048, 1024) for i in range(2)]
    wkb = [k.V(OFF_WS + 4096 + i * 2048, 1024) for i in range(2)]
    wvb = [k.V(OFF_WS + 8192 + i * 2048, 1024) for i in range(2)]
    rbA = k.smb[:, 1152:1280]
    rbB = k.smb[:, 1280:1408]
    czb = k.smb[:, 1408:1416]
    rdA = k.smb[0:1, 0:512]
    rdB = k.smb[0:1, 512:1024]
    rr = k.fscr[:, 0:512]
    nmc = lambda i: k.sm[:, 704 + i:705 + i]
    bqc = k.sm[:, 680:688]
    bkc = k.sm[:, 688:696]
    gate_c = k.sm[:, 712:713]
    omg_c = k.sm[:, 714:715]

    dep = getattr(k, "moe_done", None)

    @blk.gpsimd
    def _(gp):
        gp.dma_start(out=bqc, in_=k.di["l0_bqc"].ap()).then_inc(ld.h, 16)
        gp.dma_start(out=bkc, in_=k.di["l0_bkc"].ap()).then_inc(ld.h, 16)
        gp.dma_start(out=k.relextT[0:64, :],
                     in_=k.di["l0_relextT"].ap()).then_inc(ld.h, 16)
        gp.dma_start(out=k.relextT[64:128, :],
                     in_=k.di["l0_relextT"].ap()).then_inc(ld.h, 16)
        gp.memset(vaug, 1.0).then_inc(gp_s.h, 1)
    ld.inc(64)
    k.relext_ld = ld.n
    v_vaug_init = gp_s.inc(1)
    bvbv = bcast_row(k, k.di["l0_bv"].ap().unsqueeze(0), k.bca[0], 1024, row_off=0)
    bobv = bcast_row(k, k.di["l0_bo"].ap().unsqueeze(0), k.bca[1], 1024, row_off=1024)

    @blk.scalar
    def _(s_):
        s_.wait_ge(ld.h, k.v_consts_ld)
        s_.activation(out=k.sm[0:1, 10:11], in_=k.sm[0:1, 8:9], func=AF.Sigmoid
                      ).then_inc(sc.h, 1)
    s_gate = sc.inc(1)

    @blk.vector
    def _(v, s_gate=s_gate, bobv=bobv):
        v.wait_ge(sc.h, s_gate)
        v.wait_ge(vc.h, bobv)
        v.tensor_copy(out=k.smb[0:1, 1416:1417], in_=k.sm[0:1, 10:11])
        v.tensor_copy(out=czb, in_=k.cz).then_inc(vc.h, 1)
    v_gtrow = vc.inc(1)

    @blk.tensor
    def _(te, v_gtrow=v_gtrow):
        te.wait_ge(vc.h, v_gtrow)
        te.matmul(out=k.P[3][:, 0:1], lhsT=k.ones_row, rhs=k.smb[0:1, 1416:1417],
                  start=True, stop=True).then_inc(tn.h, 1)
    t_gb = tn.inc(1)

    @blk.vector
    def _(v, t_gb=t_gb):
        v.wait_ge(tn.h, t_gb)
        v.tensor_copy(out=gate_c, in_=k.P[3][:, 0:1]).then_inc(vc.h, 1)
    v_g1 = vc.inc(1)

    @blk.gpsimd
    def _(gp, v_g1=v_g1):
        gp.wait_ge(vc.h, v_g1)
        gp.tensor_scalar(out=omg_c, in0=gate_c, scalar1=-1.0, scalar2=1.0,
                         op0=ALU.mult, op1=ALU.add).then_inc(gp_s.h, 1)
    v_gates = gp_s.inc(1)

    proj_ev = {}
    slab_w = {}
    ax2, ax3, ax4 = k.sems["ax2"], k.sems["ax3"], k.sems["ax4"]
    relb_rd = {(0, 0): ax_s.n, (0, 1): ax4.n, (1, 0): ax_s.n, (1, 1): ax4.n}
    pair_done = {}
    projmm = {}

    for pr in range(8):
        par = pr % 2

        @blk.gpsimd
        def _(gp, pr=pr, par=par):
            if pr >= 2:
                gp.wait_ge(tn.h, projmm[pr - 2])
            for wsb, d_ in ((wqb[par], "l0_wqT"), (wkb[par], "l0_wkT"),
                            (wvb[par], "l0_wvT")):
                gp.dma_start(out=wsb.rearrange("p (c j) -> p c j", c=NCH),
                             in_=AP(k.di[d_], pr * 128,
                                    [[D, 128], [128 * D, NCH], [1, 128]])
                             ).then_inc(wq.h, 16)
        wq.inc(48)
        wqv = wq.n

        @blk.tensor
        def _(te, pr=pr, par=par, wqv=wqv):
            te.wait_ge(wq.h, wqv)
            if pr == 0:
                if dep is not None:
                    te.wait_ge(dt_s.h, dep["dt"])
                te.wait_ge(gp_s.h, v_gates)
                te.wait_ge(vc.h, max(bvbv, bobv))
            else:
                te.wait_ge(vc.h, pair_done[pr - 1])
            for c in range(NCH):
                for n_ in range(4):
                    pp = k.P[0] if n_ < 2 else k.P[1]
                    te.matmul(out=pp[:, (n_ % 2) * 512:(n_ % 2 + 1) * 512],
                              lhsT=wqb[par][:, c * 128:(c + 1) * 128],
                              rhs=xT[:, c * S + n_ * 512:c * S + (n_ + 1) * 512],
                              start=(c == 0), stop=(c == NCH - 1))
            for c in range(NCH):
                for n_ in range(4):
                    pp = k.P[2] if n_ < 2 else k.P[3]
                    mm = te.matmul(out=pp[:, (n_ % 2) * 512:(n_ % 2 + 1) * 512],
                                   lhsT=wkb[par][:, c * 128:(c + 1) * 128],
                                   rhs=xT[:, c * S + n_ * 512:c * S + (n_ + 1) * 512],
                                   start=(c == 0), stop=(c == NCH - 1))
            mm.then_inc(tn.h, 1)
        t_qk = tn.inc(1)

        @blk.scalar
        def _(s_, pr=pr, t_qk=t_qk):
            s_.wait_ge(tn.h, t_qk)
            s_.activation(out=qT[:, 0:1024], in_=k.P[0][:, :], func=AF.Identity,
                          bias=bqc[:, pr:pr + 1])
            s_.activation(out=qT[:, 1024:2048], in_=k.P[1][:, :], func=AF.Identity,
                          bias=bqc[:, pr:pr + 1])
            s_.activation(out=kT[:, 0:1024], in_=k.P[2][:, :], func=AF.Identity,
                          bias=bkc[:, pr:pr + 1])
            s_.activation(out=kT[:, 1024:2048], in_=k.P[3][:, :], func=AF.Identity,
                          bias=bkc[:, pr:pr + 1]).then_inc(sc.h, 1)
        proj_ev[pr] = sc.inc(1)

        v_base = vc.n

        @blk.tensor
        def _(te, pr=pr, par=par, v_base=v_base):
            te.wait_ge(sc.h, proj_ev[pr])
            if pr == 0:
                te.wait_ge(gp_s.h, v_vaug_init)
            for kt in range(NT):
                if kt >= 2:
                    te.wait_ge(vc.h, v_base + kt - 1)
                for c in range(NCH):
                    mm = te.matmul(out=k.P[0][:, (kt % 2) * 512:(kt % 2) * 512 + 128],
                                   lhsT=xT[:, c * S + kt * 128:c * S + (kt + 1) * 128],
                                   rhs=wvb[par][:, c * 128:(c + 1) * 128],
                                   start=(c == 0), stop=(c == NCH - 1))
                mm.then_inc(tn.h, 1)
        tn.inc(NT)
        t_vbase = tn.n - NT

        @blk.vector
        def _(v, pr=pr, t_vbase=t_vbase):
            for kt in range(NT):
                v.wait_ge(tn.h, t_vbase + kt + 1)
                dst = vaug[:, kt * 130:(kt + 1) * 130].rearrange(
                    "p (b j) -> p b j", b=2)[:, :, 0:64]
                src = k.P[0][:, (kt % 2) * 512:(kt % 2) * 512 + 128].rearrange(
                    "p (b j) -> p b j", b=2)
                bvs = k.bca[0][:, pr * 128:(pr + 1) * 128].rearrange(
                    "p (b j) -> p b j", b=2)
                v.tensor_tensor(out=dst, in0=src, in1=bvs, op=ALU.add).then_inc(vc.h, 1)
                vc.inc(1)
        v_vaug = vc.n

        bcolsA = k.fscr[:, 2688:2704]
        bcolsB = k.fscr[:, 2704:2720]

        @blk.vector
        def _(v, v_vaug=v_vaug):
            v.wait_ge(vc.h, v_vaug)
            v.tensor_tensor(out=attnA, in0=qT, in1=qT, op=ALU.mult)
            v.tensor_tensor(out=attnB, in0=kT, in1=kT, op=ALU.mult).then_inc(vc.h, 1)
        v_sq = vc.inc(1)

        @blk.tensor
        def _(te, v_sq=v_sq):
            te.wait_ge(vc.h, v_sq)
            onesA = k.ones_all[0:64, 0:1]
            onesB = k.ones_all[64:128, 0:1]
            for j in range(4):
                pq = (k.P[0], k.P[0], k.P[1], k.P[1])[j]
                cs_ = slice((j % 2) * 512, (j % 2) * 512 + 512)
                ts_ = slice(j * 512, (j + 1) * 512)
                te.matmul(out=pq[0:1, cs_], lhsT=onesA, rhs=attnA[0:64, ts_],
                          start=True, stop=True)
                pq2 = (k.P[2], k.P[2], k.P[3], k.P[3])[j]
                te.matmul(out=pq2[0:1, cs_], lhsT=onesB, rhs=attnA[64:128, ts_],
                          start=True, stop=True, tile_position=(64, 0))
                te.matmul(out=pq[64:65, cs_], lhsT=onesA, rhs=attnB[0:64, ts_],
                          start=True, stop=True, tile_position=(0, 64))
                mm = te.matmul(out=pq2[64:65, cs_], lhsT=onesB, rhs=attnB[64:128, ts_],
                               start=True, stop=True, tile_position=(64, 64))
            mm.then_inc(tn.h, 1)
        t_nrm = tn.inc(1)

        @blk.scalar
        def _(s_, t_nrm=t_nrm):
            s_.wait_ge(tn.h, t_nrm)
            s_.activation(out=attTA[0:1, 0:1024], in_=k.P[0][0:1, 0:1024], func=AF.Sqrt)
            s_.activation(out=attTA[0:1, 1024:2048], in_=k.P[1][0:1, 0:1024], func=AF.Sqrt)
            s_.activation(out=attTB[0:1, 0:1024], in_=k.P[2][0:1, 0:1024], func=AF.Sqrt)
            s_.activation(out=attTB[0:1, 1024:2048], in_=k.P[3][0:1, 0:1024],
                          func=AF.Sqrt).then_inc(sc.h, 1)
        s_nrm = sc.inc(1)

        @blk.vector
        def _(v, s_nrm=s_nrm, selfw=vc.n + 1):
            v.wait_ge(sc.h, s_nrm)
            v.reduce_max(out=k.sm[64:65, 704:705], in_=k.P[0][64:65, 0:1024], axis=AX.X)
            v.reduce_max(out=k.sm[64:65, 705:706], in_=k.P[1][64:65, 0:1024],
                         axis=AX.X).then_inc(vc.h, 1)
            v.wait_ge(vc.h, selfw)
            v.tensor_scalar(out=k.sm[64:65, 704:705], in0=k.sm[64:65, 704:705],
                            scalar1=k.sm[64:65, 705:706], scalar2=None, op0=ALU.max)
            v.reduce_max(out=k.sm[64:65, 706:707], in_=k.P[2][64:65, 0:1024], axis=AX.X)
            v.reduce_max(out=k.sm[64:65, 707:708], in_=k.P[3][64:65, 0:1024],
                         axis=AX.X).then_inc(vc.h, 1)
            v.wait_ge(vc.h, selfw + 1)
            v.tensor_scalar(out=k.sm[64:65, 706:707], in0=k.sm[64:65, 706:707],
                            scalar1=k.sm[64:65, 707:708], scalar2=None, op0=ALU.max
                            ).then_inc(vc.h, 1)
            v.tensor_copy(out=k.smb[64:65, 1676:1677], in_=k.sm[64:65, 704:705])
            v.tensor_copy(out=k.smb[64:65, 1677:1678],
                          in_=k.sm[64:65, 706:707]).then_inc(vc.h, 1)
        v_kmax = vc.inc(4)

        @blk.tensor
        def _(te, v_kmax=v_kmax):
            te.wait_ge(vc.h, v_kmax)
            te.matmul(out=k.P[0][:, 0:1], lhsT=k.ones_all[64:65, 0:128],
                      rhs=k.smb[64:65, 1676:1677], start=True, stop=True,
                      tile_position=(64, 0))
            te.matmul(out=k.P[2][:, 0:1], lhsT=k.ones_all[64:65, 0:128],
                      rhs=k.smb[64:65, 1677:1678], start=True, stop=True,
                      tile_position=(64, 0))
            for Q in range(NT):
                te.matmul(out=k.P[1][:, Q:Q + 1],
                          lhsT=attTA[0:1, Q * 128:(Q + 1) * 128],
                          rhs=k.ones_all[0:1, 0:1], start=True, stop=True)
                mm = te.matmul(out=k.P[3][:, Q:Q + 1],
                               lhsT=attTB[0:1, Q * 128:(Q + 1) * 128],
                               rhs=k.ones_all[0:1, 0:1], start=True, stop=True)
            mm.then_inc(tn.h, 1)
        t_bc = tn.inc(1)

        @blk.scalar
        def _(s_, t_bc=t_bc):
            s_.wait_ge(tn.h, t_bc)
            s_.activation(out=k.sm[:, 708:709], in_=k.P[0][:, 0:1], func=AF.Sqrt)
            s_.activation(out=k.sm[:, 709:710], in_=k.P[2][:, 0:1],
                          func=AF.Sqrt).then_inc(sc.h, 1)
        s_mk = sc.inc(1)

        @blk.vector
        def _(v, s_mk=s_mk, selfw=vc.n + 1):
            v.wait_ge(sc.h, s_mk)
            v.tensor_scalar(out=bcolsA, in0=k.P[1][:, 0:16],
                            scalar1=k.sm[:, 708:709], scalar2=-ALPHA,
                            op0=ALU.mult, op1=ALU.mult)
            v.tensor_scalar(out=bcolsB, in0=k.P[3][:, 0:16],
                            scalar1=k.sm[:, 709:710], scalar2=-ALPHA,
                            op0=ALU.mult, op1=ALU.mult).then_inc(vc.h, 1)
            v.wait_ge(vc.h, selfw)
            v.tensor_scalar(out=bcolsA, in0=bcolsA, scalar1=30.0, scalar2=None,
                            op0=ALU.add)
            v.tensor_scalar(out=bcolsB, in0=bcolsB, scalar1=30.0, scalar2=None,
                            op0=ALU.add).then_inc(vc.h, 1)
        b_done = vc.inc(2)

        ext_ev = {}
        sx1_ev = {}
        slbs = ((slabA, slabA1), (slabB, slabB1))
        CH = ((0, k.P[0], k.P[1], (0, 0)), (1, k.P[2], k.P[3], (64, 0)))
        for Q in range(NT):
            w0 = 1920 - 128 * Q
            t_ext1x = {}
            t_ext2x = {}
            for h01, pp0, pp1, tp in CH:
                @blk.tensor
                def _(te, pr=pr, Q=Q, w0=w0, h01=h01, pp0=pp0, pp1=pp1, tp=tp,
                      b_done=b_done):
                    if Q == 0:
                        te.wait_ge(vc.h, b_done)
                        if pr == 0 and h01 == 0:
                            te.wait_ge(ld.h, k.relext_ld)
                        if pr >= 2:
                            te.wait_ge(ax_s.h if h01 == 0 else ax4.h,
                                       relb_rd[(pr - 2, h01)])
                    else:
                        te.wait_ge(sc.h, ext_ev[(h01, Q - 1)])
                    lhs = qT[h01 * 64:h01 * 64 + 64, Q * 128:(Q + 1) * 128]
                    rel = k.relextT[h01 * 64:h01 * 64 + 64, :]
                    for j in range(4):
                        out = (pp0 if j < 2 else pp1)[:, (j % 2) * 512:(j % 2) * 512 + 512]
                        mm = te.matmul(out=out, lhsT=lhs,
                                       rhs=rel[:, w0 + j * 512:w0 + (j + 1) * 512],
                                       start=True, stop=True, tile_position=tp)
                    mm.then_inc(tn.h, 1)
                t_ext1x[h01] = tn.inc(1)

            for h01, pp0, pp1, tp in CH:
                @blk.scalar
                def _(s_, pr=pr, Q=Q, h01=h01, pp0=pp0, pp1=pp1,
                      slb=slbs[h01][Q % 2], t_ext1=t_ext1x[h01]):
                    s_.wait_ge(tn.h, t_ext1)
                    if Q >= 2:
                        s_.wait_ge(ax2.h if h01 == 0 else ax3.h,
                                   slab_w[(pr, h01, Q - 2)])
                    s_.activation(out=slb[:, 0:1024], in_=pp0[:, :], func=AF.Exp,
                                  scale=ALPHA)
                    s_.activation(out=slb[:, 1024:2048], in_=pp1[:, :],
                                  func=AF.Exp, scale=ALPHA).then_inc(sc.h, 1)
                sx1_ev[(h01, Q)] = sc.inc(1)

            for h01, pp0, pp1, tp in CH:
                @blk.tensor
                def _(te, Q=Q, w0=w0, h01=h01, pp0=pp0, tp=tp,
                      sx=sx1_ev[(h01, Q)]):
                    te.wait_ge(sc.h, sx)
                    lhs = qT[h01 * 64:h01 * 64 + 64, Q * 128:(Q + 1) * 128]
                    rel = k.relextT[h01 * 64:h01 * 64 + 64, :]
                    te.matmul(out=pp0[:, 0:127], lhsT=lhs,
                              rhs=rel[:, w0 + 2048:w0 + 2175],
                              start=True, stop=True,
                              tile_position=tp).then_inc(tn.h, 1)
                t_ext2x[h01] = tn.inc(1)

            for h01, pp0, pp1, tp in CH:
                @blk.scalar
                def _(s_, pr=pr, Q=Q, h01=h01, pp0=pp0, slb=slbs[h01][Q % 2],
                      t_ext2=t_ext2x[h01]):
                    s_.wait_ge(tn.h, t_ext2)
                    s_.activation(out=slb[:, 2048:2175], in_=pp0[:, 0:127],
                                  func=AF.Exp, scale=ALPHA).then_inc(sc.h, 1)
                ext_ev[(h01, Q)] = sc.inc(1)
                if h01 == 1:
                    @blk.scalar
                    def _(s_, pr=pr, Q=Q, slb=slbs[1][Q % 2], ev=ext_ev[(1, Q)]):
                        s_.wait_ge(sc.h, ev)
                        s_.dma_start(out=k.sc_slab.ap()[pr % 2, 1, Q, :, 0:2175],
                                     in_=slb[:, 0:2175]).then_inc(ax3.h, 16)
                    slab_w[(pr, 1, Q)] = ax3.inc(16)
                else:
                    @blk.sync
                    def _(sy, pr=pr, Q=Q, slb=slbs[0][Q % 2], ev=ext_ev[(0, Q)]):
                        sy.wait_ge(sc.h, ev)
                        sy.dma_start(out=k.sc_slab.ap()[pr % 2, 0, Q, :, 0:2175],
                                     in_=slb[:, 0:2175]).then_inc(ax2.h, 16)
                    slab_w[(pr, 0, Q)] = ax2.inc(16)

        relused = {}
        ctx_ev = {}
        rd_ev = {}
        for Q in range(NT):
            @blk.gpsimd
            def _(gp, pr=pr, Q=Q):
                gp.wait_ge(ax2.h, slab_w[(pr, 0, Q)])
                if Q >= 1:
                    gp.wait_ge(vc.h, relused[(0, Q - 1)])
                base = ((pr % 2) * 2 + 0) * NT * 128 * 2176 + Q * 128 * 2176
                gp.dma_start(out=relbA,
                             in_=AP(k.sc_slab, base + 127,
                                    [[2175, 128], [1, 2048]])).then_inc(ax_s.h, 16)
            rd_ev[(0, Q)] = ax_s.inc(16)

            eng_b = blk.sync if Q % 2 == 0 else blk.scalar

            @eng_b
            def _(sy, pr=pr, Q=Q):
                sy.wait_ge(ax3.h, slab_w[(pr, 1, Q)])
                if Q >= 1:
                    sy.wait_ge(vc.h, relused[(1, Q - 1)])
                base = ((pr % 2) * 2 + 1) * NT * 128 * 2176 + Q * 128 * 2176
                sy.dma_start(out=relbB,
                             in_=AP(k.sc_slab, base + 127,
                                    [[2175, 128], [1, 2048]])).then_inc(ax4.h, 16)
            rd_ev[(1, Q)] = ax4.inc(16)

            t_scx = {}
            g_nmx = {}
            s_expx = {}
            t_trx = {}
            v_attTx = {}
            t_ctxx = {}
            v_rrx = {}
            t_rbx = {}
            attnXs = (attnA, attnB)
            relbXs = (relbA, relbB)
            attTXs = (attTA, attTB)
            for h01, pp0, pp1, tp in CH:
                @blk.tensor
                def _(te, pr=pr, Q=Q, h01=h01, pp0=pp0, pp1=pp1, tp=tp):
                    if Q == 0:
                        te.wait_ge(sc.h, ext_ev[(h01, NT - 1)])
                    else:
                        te.wait_ge(vc.h, ctx_ev[(h01, Q - 1)])
                    lhs = qT[h01 * 64:h01 * 64 + 64, Q * 128:(Q + 1) * 128]
                    for n_ in range(4):
                        pp = pp0 if n_ < 2 else pp1
                        mm = te.matmul(out=pp[:, (n_ % 2) * 512:(n_ % 2 + 1) * 512],
                                       lhsT=lhs,
                                       rhs=kT[h01 * 64:h01 * 64 + 64,
                                              n_ * 512:(n_ + 1) * 512],
                                       start=True, stop=True, tile_position=tp)
                    mm.then_inc(tn.h, 1)
                t_scx[h01] = tn.inc(1)

            for h01, pp0, pp1, tp in CH:
                @blk.scalar
                def _(s_, Q=Q, h01=h01, pp0=pp0, pp1=pp1, attnX=attnXs[h01],
                      bcol=(bcolsA if h01 == 0 else bcolsB)[:, Q:Q + 1],
                      t_sc=t_scx[h01]):
                    s_.wait_ge(tn.h, t_sc)
                    s_.activation(out=attnX[:, 0:1024], in_=pp0[:, :], func=AF.Exp,
                                  scale=ALPHA, bias=bcol)
                    s_.activation(out=attnX[:, 1024:2048], in_=pp1[:, :],
                                  func=AF.Exp, scale=ALPHA,
                                  bias=bcol).then_inc(sc.h, 1)
                s_expx[h01] = sc.inc(1)

            for h01, pp0, pp1, tp in CH:
                @blk.vector
                def _(v, Q=Q, h01=h01, attnX=attnXs[h01], relbX=relbXs[h01],
                      s_exp=s_expx[h01]):
                    v.wait_ge(sc.h, s_exp)
                    v.wait_ge(ax_s.h if h01 == 0 else ax4.h, rd_ev[(h01, Q)])
                    v.tensor_mul(out=attnX[:, 0:1024], in0=attnX[:, 0:1024],
                                 in1=relbX[:, 0:1024])
                    v.tensor_mul(out=attnX[:, 1024:2048], in0=attnX[:, 1024:2048],
                                 in1=relbX[:, 1024:2048]).then_inc(vc.h, 1)
                relused[(h01, Q)] = vc.inc(1)

            for h01, pp0, pp1, tp in CH:
                @blk.tensor
                def _(te, h01=h01, attnX=attnXs[h01], v_rel=relused[(h01, Q)]):
                    te.wait_ge(vc.h, v_rel)
                    te.wait_ge(gp_s.h, k.v_consts_g)
                    pb = k.Pb[0] if h01 == 0 else k.Pb[2]
                    for kt in range(NT):
                        mm = te.transpose(out=pb[:, kt * 128:(kt + 1) * 128],
                                          in_=attnX[:, kt * 128:(kt + 1) * 128],
                                          identity=k.eye)
                    mm.then_inc(tn.h, 1)
                t_trx[h01] = tn.inc(1)

            for h01, pp0, pp1, tp in CH:
                @blk.vector
                def _(v, h01=h01, attTX=attTXs[h01], t_tr=t_trx[h01]):
                    v.wait_ge(tn.h, t_tr)
                    pb = k.Pb[0] if h01 == 0 else k.Pb[2]
                    v.tensor_copy(out=attTX, in_=pb[:, 0:2048]).then_inc(vc.h, 1)
                v_attTx[h01] = vc.inc(1)

            for h01, pp0, pp1, tp in CH:
                @blk.tensor
                def _(te, h01=h01, attTX=attTXs[h01], v_attT=v_attTx[h01]):
                    te.wait_ge(vc.h, v_attT)
                    if h01 == 0:
                        for kt in range(NT):
                            mm = te.matmul(out=k.P[1][0:65, 0:128],
                                           lhsT=vaug[:, kt * 130:kt * 130 + 65],
                                           rhs=attTX[:, kt * 128:(kt + 1) * 128],
                                           start=(kt == 0), stop=(kt == NT - 1))
                    else:
                        for kt in range(NT):
                            te.matmul(out=k.P[3][64:128, 0:128],
                                      lhsT=vaug[:, kt * 130 + 65:kt * 130 + 129],
                                      rhs=attTX[:, kt * 128:(kt + 1) * 128],
                                      start=(kt == 0), stop=(kt == NT - 1),
                                      tile_position=(0, 64))
                        for kt in range(NT):
                            mm = te.matmul(out=k.P[2][0:1, 0:128],
                                           lhsT=vaug[:, kt * 130 + 129:kt * 130 + 130],
                                           rhs=attTX[:, kt * 128:(kt + 1) * 128],
                                           start=(kt == 0), stop=(kt == NT - 1))
                    mm.then_inc(tn.h, 1)
                t_ctxx[h01] = tn.inc(1)

            for h01, pp0, pp1, tp in CH:
                @blk.vector
                def _(v, h01=h01, t_ctx=t_ctxx[h01], selfw=vc.n + 1):
                    v.wait_ge(tn.h, t_ctx)
                    if h01 == 0:
                        v.reciprocal(out=rr[64:65, 0:128],
                                     in_=k.P[1][64:65, 0:128]).then_inc(vc.h, 1)
                        v.wait_ge(vc.h, selfw)
                        v.tensor_copy(out=k.smb[64:65, 1416:1544],
                                      in_=rr[64:65, 0:128]).then_inc(vc.h, 1)
                    else:
                        v.reciprocal(out=rr[0:1, 128:256],
                                     in_=k.P[2][0:1, 0:128]).then_inc(vc.h, 1)
                        v.wait_ge(vc.h, selfw)
                        v.tensor_copy(out=k.smb[0:1, 1544:1672],
                                      in_=rr[0:1, 128:256]).then_inc(vc.h, 1)
                v_rrx[h01] = vc.inc(2)

            for h01, pp0, pp1, tp in CH:
                @blk.tensor
                def _(te, h01=h01, v_rr=v_rrx[h01]):
                    te.wait_ge(vc.h, v_rr)
                    if h01 == 0:
                        te.matmul(out=k.P[0][0:64, 0:128],
                                  lhsT=k.ones_all[64:65, 0:64],
                                  rhs=k.smb[64:65, 1416:1544], start=True,
                                  stop=True).then_inc(tn.h, 1)
                    else:
                        te.matmul(out=k.P[2][64:128, 0:128],
                                  lhsT=k.ones_all[0:1, 0:64],
                                  rhs=k.smb[0:1, 1544:1672], start=True, stop=True,
                                  tile_position=(0, 64)).then_inc(tn.h, 1)
                t_rbx[h01] = tn.inc(1)

            for h01, pp0, pp1, tp in CH:
                @blk.vector
                def _(v, pr=pr, Q=Q, h01=h01, t_rb=t_rbx[h01]):
                    v.wait_ge(tn.h, t_rb)
                    cs = slice(pr * 2048 + Q * 128, pr * 2048 + (Q + 1) * 128)
                    if h01 == 0:
                        v.tensor_copy(out=rbA[0:64, :], in_=k.P[0][0:64, 0:128])
                        v.tensor_tensor(out=comb[0:64, cs],
                                        in0=k.P[1][0:64, 0:128], in1=rbA[0:64, :],
                                        op=ALU.mult).then_inc(vc.h, 1)
                    else:
                        v.tensor_copy(out=rbB[64:128, :], in_=k.P[2][64:128, 0:128])
                        v.tensor_tensor(out=comb[64:128, cs],
                                        in0=k.P[3][64:128, 0:128],
                                        in1=rbB[64:128, :],
                                        op=ALU.mult).then_inc(vc.h, 1)
                ctx_ev[(h01, Q)] = vc.inc(1)

        relb_rd[(pr, 0)] = rd_ev[(0, NT - 1)]
        relb_rd[(pr, 1)] = rd_ev[(1, NT - 1)]

        @blk.vector
        def _(v, pr=pr, last_ctx=ctx_ev[(1, NT - 1)]):
            v.wait_ge(vc.h, last_ctx)
            v.tensor_scalar_min(out=sgm1, in0=qT, scalar1=0.0).then_inc(vc.h, 1)
        v_sg1 = vc.inc(1)

        @blk.scalar
        def _(s_, v_sg1=v_sg1):
            s_.wait_ge(vc.h, v_sg1)
            s_.activation(out=sgm2, in_=sgm1, func=AF.Exp).then_inc(sc.h, 1)
        s_sg = sc.inc(1)

        @blk.vector
        def _(v, s_sg=s_sg):
            v.wait_ge(sc.h, s_sg)
            v.scalar_tensor_tensor(out=sgm1, in0=qT, scalar=0.0, in1=sgm2,
                                   op0=ALU.max, op1=ALU.add).then_inc(vc.h, 1)
        v_sig = vc.inc(1)

        gchunk_ev = {}
        for n_ in range(4):
            @blk.tensor
            def _(te, pr=pr, n_=n_, v_sig=v_sig):
                if n_ == 0:
                    te.wait_ge(vc.h, v_sig)
                else:
                    te.wait_ge(vc.h, gchunk_ev[n_ - 1])
                te.matmul(out=k.P[0][0:64, 0:512],
                          lhsT=k.cM[0:64, pr * 64:(pr + 1) * 64],
                          rhs=sgm1[0:64, n_ * 512:(n_ + 1) * 512], start=True,
                          stop=True, tile_position=(0, 0))
                te.matmul(out=k.P[2][64:128, 0:512],
                          lhsT=k.cM[64:128, pr * 64:(pr + 1) * 64],
                          rhs=sgm1[64:128, n_ * 512:(n_ + 1) * 512], start=True,
                          stop=True, tile_position=(64, 64))
                te.matmul(out=k.P[1][0:1, 0:512], lhsT=czb[0:64, pr:pr + 1],
                          rhs=sgm1[0:64, n_ * 512:(n_ + 1) * 512], start=True,
                          stop=True, tile_position=(0, 0))
                mm = te.matmul(out=k.P[3][0:1, 0:512], lhsT=czb[64:128, pr:pr + 1],
                               rhs=sgm1[64:128, n_ * 512:(n_ + 1) * 512], start=True,
                               stop=True, tile_position=(64, 0))
                mm.then_inc(tn.h, 1)
            t_g = tn.inc(1)

            @blk.vector
            def _(v, pr=pr, n_=n_, t_g=t_g):
                v.wait_ge(tn.h, t_g)
                v.tensor_scalar(out=rr[0:1, 0:512], in0=k.P[1][0:1, 0:512],
                                scalar1=1e-6, scalar2=None, op0=ALU.add)
                v.tensor_scalar(out=rr[64:65, 0:512], in0=k.P[3][0:1, 0:512],
                                scalar1=1e-6, scalar2=None, op0=ALU.add)
                v.reciprocal(out=rr[0:1, 0:512], in_=rr[0:1, 0:512])
                v.reciprocal(out=rr[64:65, 0:512], in_=rr[64:65, 0:512])
                v.tensor_copy(out=rdA, in_=rr[0:1, 0:512])
                mm = v.tensor_copy(out=k.smb[64:65, 512:1024], in_=rr[64:65, 0:512])
                mm.then_inc(vc.h, 1)
            v_rd = vc.inc(1)

            @blk.tensor
            def _(te, v_rd=v_rd):
                te.wait_ge(vc.h, v_rd)
                te.matmul(out=k.P[1][0:64, 0:512], lhsT=k.ones_all[0:1, 0:64],
                          rhs=rdA, start=True, stop=True)
                te.matmul(out=k.P[3][64:128, 0:512], lhsT=k.ones_all[64:65, 0:64],
                          rhs=k.smb[64:65, 512:1024], start=True, stop=True,
                          tile_position=(64, 64)).then_inc(tn.h, 1)
            t_rdb = tn.inc(1)

            @blk.vector
            def _(v, pr=pr, n_=n_, t_rdb=t_rdb):
                v.wait_ge(tn.h, t_rdb)
                v.tensor_copy(out=attTA[0:64, 0:512], in_=k.P[1][0:64, 0:512])
                v.tensor_copy(out=attTA[64:128, 0:512], in_=k.P[3][64:128, 0:512])
                v.scalar_tensor_tensor(out=attTB[0:64, 0:512], in0=k.P[0][0:64, 0:512],
                                       scalar=omg_c[0:64, :], in1=attTA[0:64, 0:512],
                                       op0=ALU.mult, op1=ALU.mult)
                v.scalar_tensor_tensor(out=attTB[64:128, 0:512],
                                       in0=k.P[2][64:128, 0:512],
                                       scalar=omg_c[64:128, :], in1=attTA[64:128, 0:512],
                                       op0=ALU.mult, op1=ALU.mult)
                cs = comb[:, pr * 2048 + n_ * 512:pr * 2048 + (n_ + 1) * 512]
                mm = v.scalar_tensor_tensor(out=cs, in0=cs, scalar=gate_c,
                                            in1=attTB[:, 0:512],
                                            op0=ALU.mult, op1=ALU.add)
                mm.then_inc(vc.h, 1)
            gchunk_ev[n_] = vc.inc(1)
        pair_done[pr] = gchunk_ev[3]
        projmm[pr] = tn.n

    @blk.gpsimd
    def _(gp):
        gp.wait_ge(tn.h, projmm[7])
        gp.dma_start(out=k.V(OFF_WS, 8192).rearrange("p (c j) -> p c j", c=NCH),
                     in_=k.di["l0_woT"].ap().rearrange("(c p) j -> p c j", p=128)
                     ).then_inc(wq.h, 16)
    wo_ld = wq.inc(16)
    woT_sb = k.V(OFF_WS, 8192)
    st_ev = {}
    vo_ev = {}
    for st in range(NT):
        pj = (k.P[0], k.P[1]) if st % 2 == 0 else (k.P[2], k.P[3])
        ob = k.smb[:, 0:1024] if st % 2 == 0 else k.smb[:, 1024:2048]

        @blk.tensor
        def _(te, st=st, pj=pj, wo_ld=wo_ld):
            if st == 0:
                te.wait_ge(wq.h, wo_ld)
                te.wait_ge(vc.h, pair_done[7])
            if st >= 2:
                te.wait_ge(vc.h, vo_ev[st - 2])
            for c in range(NCH):
                for n_ in range(2):
                    mm = te.matmul(
                        out=pj[n_][:, 0:512],
                        lhsT=comb[:, c * 2048 + st * 128:c * 2048 + (st + 1) * 128],
                        rhs=woT_sb[:, c * 1024 + n_ * 512:c * 1024 + (n_ + 1) * 512],
                        start=(c == 0), stop=(c == NCH - 1))
            mm.then_inc(tn.h, 1)
        t_o = tn.inc(1)

        @blk.vector
        def _(v, st=st, pj=pj, ob=ob, t_o=t_o):
            v.wait_ge(tn.h, t_o)
            if st >= 2:
                v.wait_ge(dt_s.h, st_ev[st - 2])
            v.tensor_tensor(out=ob[:, 0:512], in0=pj[0][:, 0:512],
                            in1=k.bca[1][:, 0:512], op=ALU.add)
            mm = v.tensor_tensor(out=ob[:, 512:1024], in0=pj[1][:, 0:512],
                                 in1=k.bca[1][:, 512:1024], op=ALU.add)
            mm.then_inc(vc.h, 1)
        vo_ev[st] = vc.inc(1)

        @blk.gpsimd
        def _(gp, st=st, ob=ob):
            gp.wait_ge(vc.h, vo_ev[st])
            gp.dma_start(out=k.sc_satt.ap()[st * 128:(st + 1) * 128, :],
                         in_=ob).then_inc(dt_s.h, 16)
        st_ev[st] = dt_s.inc(16)
    k.attn0_done = dict(dt=dt_s.n, v=vc.n, s=sc.n, t=tn.n,
                        ax2=k.sems["ax2"].n, ax3=k.sems["ax3"].n)


def emit_combine_l0(k):
    nc, blk = k.nc, k.block
    ld, tn, vc, sc, gp_s, dt_s = (k.sems[n] for n in
                                  ["ld", "tn", "vc", "sc", "gp", "dt"])
    stage = k.fscr[:, 0:1024]
    stage2 = k.fscr[:, 1024:2048]
    satt_b = [k.V(OFF_ATTN, 1024), k.V(OFF_ATTN + 2048, 1024)]
    accp_b = [k.V(OFF_ATTNT, 1024), k.V(OFF_ATTNT + 2048, 1024)]
    g_b = [k.V(OFF_RELB, 1024), k.V(OFF_RELB + 2048, 1024)]
    adep = getattr(k, "attn0_done", None)
    ev_v = {}
    ev_s = {}
    x1_w = {}

    for t in range(NT):
        par = t % 2
        xt = stage if par == 0 else stage2

        @blk.gpsimd
        def _(gp, t=t, par=par, xt=xt):
            if t == 0 and adep is not None:
                gp.wait_ge(dt_s.h, adep["dt"])
                gp.wait_ge(k.sems["ax2"].h, adep["ax2"])
                gp.wait_ge(k.sems["ax3"].h, adep["ax3"])
            if t >= 2:
                gp.wait_ge(vc.h, ev_v[t - 2])
                gp.wait_ge(dt_s.h, x1_w[t - 2])
            gp.dma_start(out=xt, in_=k.di["x"].ap()[t * 128:(t + 1) * 128, :]
                         ).then_inc(ld.h, 16)
            gp.dma_start(out=satt_b[par], in_=k.sc_satt.ap()[t * 128:(t + 1) * 128, :]
                         ).then_inc(ld.h, 16)
            gp.dma_start(out=accp_b[par].rearrange("p (c j) -> p c j", c=NCH),
                         in_=k.sc_acc0.ap()[:, :, t * 128:(t + 1) * 128]
                         .transpose([1, 0, 2])).then_inc(ld.h, 16)
        ldv = ld.inc(48)

        @blk.tensor
        def _(te, t=t, par=par, ldv=ldv):
            te.wait_ge(ld.h, ldv)
            te.wait_ge(gp_s.h, k.v_consts_g)
            if t >= 2:
                te.wait_ge(sc.h, ev_s[t - 2])
            for c in range(NCH):
                mm = te.transpose(out=k.Pb[2 * par][:, c * 128:(c + 1) * 128],
                                  in_=accp_b[par][:, c * 128:(c + 1) * 128],
                                  identity=k.eye)
            mm.then_inc(tn.h, 1)
        t_tr = tn.inc(1)

        @blk.scalar
        def _(s_, t=t, par=par, t_tr=t_tr):
            s_.wait_ge(tn.h, t_tr)
            s_.activation(out=g_b[par], in_=k.Pb[2 * par][:, 0:1024],
                          func=AF.Silu).then_inc(sc.h, 1)
        ev_s[t] = sc.inc(1)

        @blk.vector
        def _(v, t=t, par=par, xt=xt):
            v.wait_ge(sc.h, ev_s[t])
            v.tensor_mul(out=g_b[par], in0=satt_b[par], in1=g_b[par])
            mm = v.tensor_add(out=xt, in0=xt, in1=g_b[par])
            mm.then_inc(vc.h, 1)
        ev_v[t] = vc.inc(1)

        @blk.gpsimd
        def _(gp, t=t, xt=xt):
            gp.wait_ge(vc.h, ev_v[t])
            gp.dma_start(out=k.sc_x1.ap()[t * 128:(t + 1) * 128, :], in_=xt
                         ).then_inc(dt_s.h, 16)
        x1_w[t] = dt_s.inc(16)

    xn1T = k.V(OFF_BIGB, 16384)
    tok = k.V(OFF_BIGA, 16384)
    emit_ln(k, src_ap=k.sc_x1.ap(), dst_tok=tok, dst_xnT=xn1T,
            dst_tok_dram=k.sc_xn1.ap(), pre_wait=[("dt", dt_s.n)])
    k.comb0_done = dict(v=vc.n, s=sc.n, t=tn.n, dt=k.sems["dt"].n)



def emit_lin_attn_l1(k):
    nc, blk = k.nc, k.block
    ld, wq, tn, vc, sc, gp_s, dt_s = (k.sems[n] for n in
                                      ["ld", "wq", "tn", "vc", "sc", "gp", "dt"])
    xT = k.V(OFF_BIGB, NCH * S)
    QN = k.V(OFF_BIGA, 16384)
    KN = k.V(OFF_BIGA + 32768, 16384)
    wsb = k.V(OFF_WS, 8192)
    AT_v = ([k.V(OFF_ATTN + i * 2048, 1024) for i in range(4)]
            + [k.V(OFF_ATTNT + i * 2048, 1024) for i in range(4)])
    q2b = [k.V(OFF_ATTN, 1024), k.V(OFF_ATTN + 2048, 1024)]
    invb = [k.V(OFF_ATTNT, 1024), k.V(OFF_ATTNT + 2048, 1024)]
    tailb = k.V(OFF_WTEB, 1024)
    disc = k.V(OFF_RELB, 1024)
    xn1b = [k.V(OFF_RELB + 2048, 1024), k.V(OFF_RELB + 4096, 1024)]
    sl1b = [k.V(OFF_SLAB, 1024), k.V(OFF_SLAB + 2048, 1024)]
    bvc = k.sm[:, 680:688]
    rsk_c = lambda t: k.sm[:, 720 + t:721 + t]
    vs_c = lambda t: k.sm[:, 736 + t:737 + t]
    gm_c = k.sm[:, 13:14]
    rskb = k.smb[:, 1024:1040]
    dep = getattr(k, "comb0_done", None)

    bqv = bcast_row(k, k.di["l1_bq"].ap().unsqueeze(0), k.bca[0], 1024, row_off=0)
    bkv = bcast_row(k, k.di["l1_bk"].ap().unsqueeze(0), k.bca[1], 1024, row_off=1024)
    bvv = bcast_row(k, k.di["l1_bv"].ap().unsqueeze(0), k.bca[2], 1024, row_off=0)

    @blk.gpsimd
    def _(gp):
        gp.dma_start(out=bvc, in_=k.di["l1_bvc"].ap()).then_inc(ld.h, 16)
    ld.inc(16)

    @blk.vector
    def _(v, bvv=bvv):
        v.wait_ge(vc.h, bvv)
        v.wait_ge(ld.h, k.v_consts_ld)
        v.tensor_copy(out=k.smb[0:1, 1040:1041], in_=k.sm[0:1, 9:10]).then_inc(vc.h, 1)
    v_gr = vc.inc(1)

    @blk.tensor
    def _(te, v_gr=v_gr):
        te.wait_ge(vc.h, v_gr)
        te.matmul(out=k.P[3][:, 0:1], lhsT=k.ones_row, rhs=k.smb[0:1, 1040:1041],
                  start=True, stop=True).then_inc(tn.h, 1)
    t_gm = tn.inc(1)

    @blk.vector
    def _(v, t_gm=t_gm):
        v.wait_ge(tn.h, t_gm)
        v.tensor_copy(out=gm_c, in_=k.P[3][:, 0:1]).then_inc(vc.h, 1)
    v_gm = vc.inc(1)

    def load_w(dname, after_tn):
        @blk.gpsimd
        def _(gp, dname=dname, after_tn=after_tn):
            gp.wait_ge(tn.h, after_tn)
            gp.dma_start(out=wsb.rearrange("p (c j) -> p c j", c=NCH),
                         in_=k.di[dname].ap().rearrange("(c p) j -> p c j", p=128)
                         ).then_inc(wq.h, 16)
        return wq.inc(16)

    wq_ld = load_w("l1_wqT", 0)
    ev = {}

    def proj_tok(dst, bcab, wld, first_wait=None, accum_vs=False):
        entry_vc = vc.n
        for st in range(NT):
            pj = k.P[0] if st % 2 == 0 else k.P[1]

            @blk.tensor
            def _(te, st=st, pj=pj, wld=wld, first_wait=first_wait, entry_vc=entry_vc):
                te.wait_ge(wq.h, wld)
                if st == 0 and first_wait:
                    for snm, sval in first_wait:
                        te.wait_ge(k.sems[snm].h, sval)
                if st < 2:
                    te.wait_ge(vc.h, entry_vc)
                else:
                    te.wait_ge(vc.h, ev[st - 2])
                for c in range(NCH):
                    for n_ in range(2):
                        mm = te.matmul(
                            out=pj[:, n_ * 512:(n_ + 1) * 512],
                            lhsT=xT[:, c * S + st * 128:c * S + (st + 1) * 128],
                            rhs=wsb[:, c * 1024 + n_ * 512:c * 1024 + (n_ + 1) * 512],
                            start=(c == 0), stop=(c == NCH - 1))
                mm.then_inc(tn.h, 1)
            t_p = tn.inc(1)

            @blk.vector
            def _(v, st=st, pj=pj, t_p=t_p, dst=dst, bcab=bcab, accum_vs=accum_vs):
                v.wait_ge(tn.h, t_p)
                if accum_vs:
                    mm = v.scalar_tensor_tensor(out=disc, in0=pj[:, :], scalar=0.0,
                                                in1=bcab, op0=ALU.add, op1=ALU.add,
                                                accum_out=vs_c(st))
                else:
                    mm = v.tensor_add(out=dst[:, st * 1024:(st + 1) * 1024],
                                      in0=pj[:, :], in1=bcab)
                mm.then_inc(vc.h, 1)
            ev[st] = vc.inc(1)
        return tn.n, vc.n

    fw = [("vc", max(bqv, v_gm))]
    if dep is not None:
        fw.append(("dt", dep["dt"]))
    tq, vq = proj_tok(QN, k.bca[0], wq_ld, first_wait=fw)
    wk_ld = load_w("l1_wkT", tq)
    tk, vk = proj_tok(KN, k.bca[1], wk_ld, first_wait=[("vc", bkv)])
    wv_ld = load_w("l1_wvT", tk)
    tv, vv_ = proj_tok(None, k.bca[2], wv_ld, first_wait=[("vc", bvv)], accum_vs=True)

    def colnorm(src, invbuf, sqpar, t_prev):
        t0v = tn.n
        for st in range(NT):
            @blk.scalar
            def _(s_, st=st, src=src, sqpar=sqpar, t_prev=t_prev):
                if st == 0:
                    s_.wait_ge(vc.h, vc.n)
                if st >= 2:
                    s_.wait_ge(tn.h, k._l1_sqmm[st - 2])
                s_.activation(out=q2b[st % 2], in_=src[:, st * 1024:(st + 1) * 1024],
                              func=AF.Square).then_inc(sc.h, 1)
            s_q2 = sc.inc(1)
            k._l1_sqmm = getattr(k, "_l1_sqmm", {})

            @blk.tensor
            def _(te, st=st, s_q2=s_q2):
                te.wait_ge(sc.h, s_q2)
                te.matmul(out=k.P[2][0:1, 0:512], lhsT=k.ones_col,
                          rhs=q2b[st % 2][:, 0:512], start=(st == 0),
                          stop=(st == NT - 1))
                te.matmul(out=k.P[2][0:1, 512:1024], lhsT=k.ones_col,
                          rhs=q2b[st % 2][:, 512:1024], start=(st == 0),
                          stop=(st == NT - 1)).then_inc(tn.h, 1)
            k._l1_sqmm[st] = tn.inc(1)
        t_sq = tn.n

        @blk.scalar
        def _(s_, t_sq=t_sq):
            s_.wait_ge(tn.h, t_sq)
            s_.activation(out=k.fscr[0:1, 0:1024], in_=k.P[2][0:1, 0:1024],
                          func=AF.Sqrt).then_inc(sc.h, 1)
        s_sq = sc.inc(1)

        @blk.vector
        def _(v, s_sq=s_sq):
            v.wait_ge(sc.h, s_sq)
            v.reciprocal(out=k.fscr[0:1, 1024:2048], in_=k.fscr[0:1, 0:1024])
            mm = v.tensor_copy(out=k.smb[0:1, 0:1024], in_=k.fscr[0:1, 1024:2048])
            mm.then_inc(vc.h, 1)
        v_iv = vc.inc(1)

        @blk.tensor
        def _(te, v_iv=v_iv):
            te.wait_ge(vc.h, v_iv)
            te.matmul(out=k.P[2][:, 0:512], lhsT=k.ones_row, rhs=k.smb[0:1, 0:512],
                      start=True, stop=True)
            te.matmul(out=k.P[2][:, 512:1024], lhsT=k.ones_row,
                      rhs=k.smb[0:1, 512:1024], start=True, stop=True).then_inc(tn.h, 1)
        t_iv = tn.inc(1)

        @blk.vector
        def _(v, t_iv=t_iv, invbuf=invbuf):
            v.wait_ge(tn.h, t_iv)
            v.tensor_copy(out=invbuf, in_=k.P[2][:, 0:1024]).then_inc(vc.h, 1)
        v_ib = vc.inc(1)

        for st in range(NT):
            @blk.vector
            def _(v, st=st, src=src, invbuf=invbuf, v_ib=v_ib):
                if st == 0:
                    v.wait_ge(vc.h, v_ib)
                mm = v.tensor_mul(out=src[:, st * 1024:(st + 1) * 1024],
                                  in0=src[:, st * 1024:(st + 1) * 1024], in1=invbuf)
                mm.then_inc(vc.h, 1)
            vc.inc(1)
        return vc.n

    vqn = colnorm(QN, invb[0], 0, tq)
    vkn = colnorm(KN, invb[1], 1, tk)

    for st in range(NT):
        @blk.vector
        def _(v, st=st, vkn=vkn):
            if st == 0:
                v.wait_ge(vc.h, vkn)
            mm = v.tensor_scalar(out=disc, in0=KN[:, st * 1024:(st + 1) * 1024],
                                 scalar1=1.0, scalar2=1e-6, op0=ALU.mult,
                                 op1=ALU.add, accum_out=rsk_c(st))
            mm.then_inc(vc.h, 1)
        vc.inc(1)

    @blk.vector
    def _(v):
        v.wait_ge(vc.h, vc.n)
        v.tensor_copy(out=rskb, in_=k.sm[:, 720:736]).then_inc(vc.h, 1)
    v_rskb = vc.inc(1)

    @blk.tensor
    def _(te, v_rskb=v_rskb):
        te.wait_ge(vc.h, v_rskb)
        for st in range(NT):
            for n_ in range(2):
                mm = te.matmul(out=k.P[2][0:1, n_ * 512:(n_ + 1) * 512],
                               lhsT=rskb[:, st:st + 1],
                               rhs=QN[:, st * 1024 + n_ * 512:st * 1024 + (n_ + 1) * 512],
                               start=(st == 0), stop=(st == NT - 1))
        mm.then_inc(tn.h, 1)
    t_tp = tn.inc(1)

    @blk.scalar
    def _(s_, t_tp=t_tp):
        s_.wait_ge(tn.h, t_tp)
        s_.activation(out=k.fscr[0:1, 0:1024], in_=k.P[2][0:1, 0:1024],
                      func=AF.Identity, bias=k.sm[0:1, 14:15]).then_inc(sc.h, 1)
    s_tl = sc.inc(1)

    @blk.vector
    def _(v, s_tl=s_tl):
        v.wait_ge(sc.h, s_tl)
        v.reciprocal(out=k.fscr[0:1, 1024:2048], in_=k.fscr[0:1, 0:1024])
        mm = v.tensor_copy(out=k.smb[0:1, 0:1024], in_=k.fscr[0:1, 1024:2048])
        mm.then_inc(vc.h, 1)
    v_tl = vc.inc(1)

    @blk.tensor
    def _(te, v_tl=v_tl):
        te.wait_ge(vc.h, v_tl)
        te.matmul(out=k.P[2][:, 0:512], lhsT=k.ones_row, rhs=k.smb[0:1, 0:512],
                  start=True, stop=True)
        te.matmul(out=k.P[2][:, 512:1024], lhsT=k.ones_row,
                  rhs=k.smb[0:1, 512:1024], start=True, stop=True).then_inc(tn.h, 1)
    t_tlb = tn.inc(1)

    @blk.vector
    def _(v, t_tlb=t_tlb):
        v.wait_ge(tn.h, t_tlb)
        v.tensor_copy(out=tailb, in_=k.P[2][:, 0:1024]).then_inc(vc.h, 1)
    v_tlb = vc.inc(1)

    at_ev = {}
    for dt_ in range(NCH):
        pj = k.P[0] if dt_ % 2 == 0 else k.P[1]

        @blk.tensor
        def _(te, dt_=dt_, pj=pj, v_tlb=v_tlb):
            if dt_ == 0:
                te.wait_ge(vc.h, v_tlb)
            if dt_ >= 2:
                te.wait_ge(sc.h, at_ev[dt_ - 2])
            for st in range(NT):
                for n_ in range(2):
                    mm = te.matmul(
                        out=pj[:, n_ * 512:(n_ + 1) * 512],
                        lhsT=KN[:, st * 1024 + dt_ * 128:st * 1024 + dt_ * 128 + 128],
                        rhs=QN[:, st * 1024 + n_ * 512:st * 1024 + (n_ + 1) * 512],
                        start=(st == 0), stop=(st == NT - 1))
            mm.then_inc(tn.h, 1)
        t_at = tn.inc(1)

        @blk.scalar
        def _(s_, dt_=dt_, pj=pj, t_at=t_at):
            s_.wait_ge(tn.h, t_at)
            s_.activation(out=AT_v[dt_], in_=pj[:, :],
                          func=AF.Identity).then_inc(sc.h, 1)
        at_ev[dt_] = sc.inc(1)

    VT = QN
    vt_ev = {}
    for dt_ in range(NCH):
        pj2 = (k.P[0], k.P[1]) if dt_ % 2 == 0 else (k.P[2], k.P[3])

        @blk.tensor
        def _(te, dt_=dt_, pj2=pj2):
            te.wait_ge(sc.h, at_ev[NCH - 1])
            if dt_ >= 2:
                te.wait_ge(sc.h, vt_ev[dt_ - 2])
            for c in range(NCH):
                for n_ in range(4):
                    pp = pj2[0] if n_ < 2 else pj2[1]
                    mm = te.matmul(
                        out=pp[:, (n_ % 2) * 512:(n_ % 2 + 1) * 512],
                        lhsT=wsb[:, c * 1024 + dt_ * 128:c * 1024 + dt_ * 128 + 128],
                        rhs=xT[:, c * S + n_ * 512:c * S + (n_ + 1) * 512],
                        start=(c == 0), stop=(c == NCH - 1))
            mm.then_inc(tn.h, 1)
        t_vt = tn.inc(1)

        @blk.scalar
        def _(s_, dt_=dt_, pj2=pj2, t_vt=t_vt):
            s_.wait_ge(tn.h, t_vt)
            s_.activation(out=VT[:, dt_ * 2048:dt_ * 2048 + 1024], in_=pj2[0][:, :],
                          func=AF.Identity, bias=bvc[:, dt_:dt_ + 1])
            s_.activation(out=VT[:, dt_ * 2048 + 1024:dt_ * 2048 + 2048],
                          in_=pj2[1][:, :], func=AF.Identity,
                          bias=bvc[:, dt_:dt_ + 1]).then_inc(sc.h, 1)
        vt_ev[dt_] = sc.inc(1)

    for ct in range(NT):
        pj = k.P[0] if ct % 2 == 0 else k.P[1]
        par = ct % 2

        @blk.gpsimd
        def _(gp, ct=ct, par=par):
            if ct >= 2:
                gp.wait_ge(vc.h, k._l1_slev[ct - 2])
            gp.dma_start(out=xn1b[par], in_=k.sc_xn1.ap()[ct * 128:(ct + 1) * 128, :]
                         ).then_inc(ld.h, 16)
        ld_x = ld.inc(16)
        k._l1_slev = getattr(k, "_l1_slev", {})

        @blk.tensor
        def _(te, ct=ct, pj=pj, ld_x=ld_x):
            if ct == 0:
                te.wait_ge(sc.h, vt_ev[NCH - 1])
                te.wait_ge(sc.h, vt_ev[NCH - 2])
            if ct >= 2:
                te.wait_ge(vc.h, k._l1_slev[ct - 2])
            for dc in range(NCH):
                for n_ in range(2):
                    mm = te.matmul(
                        out=pj[:, n_ * 512:(n_ + 1) * 512],
                        lhsT=VT[:, dc * 2048 + ct * 128:dc * 2048 + ct * 128 + 128],
                        rhs=AT_v[dc][:, n_ * 512:(n_ + 1) * 512],
                        start=(dc == 0), stop=(dc == NCH - 1))
            mm.then_inc(tn.h, 1)
        t_tm = tn.inc(1)

        @blk.vector
        def _(v, ct=ct, pj=pj, par=par, t_tm=t_tm, ld_x=ld_x):
            v.wait_ge(tn.h, t_tm)
            v.wait_ge(ld.h, ld_x)
            if ct >= 2:
                v.wait_ge(dt_s.h, k._l1_stw[ct - 2])
            v.scalar_tensor_tensor(out=disc, in0=pj[:, :], scalar=vs_c(ct),
                                   in1=tailb, op0=ALU.add, op1=ALU.mult)
            mm = v.scalar_tensor_tensor(out=sl1b[par], in0=disc, scalar=gm_c,
                                        in1=xn1b[par], op0=ALU.mult, op1=ALU.add)
            mm.then_inc(vc.h, 1)
        v_sl = vc.inc(1)
        k._l1_slev[ct] = v_sl

        @blk.gpsimd
        def _(gp, ct=ct, par=par, v_sl=v_sl):
            gp.wait_ge(vc.h, v_sl)
            gp.dma_start(out=k.sc_sl1.ap()[ct * 128:(ct + 1) * 128, :],
                         in_=sl1b[par]).then_inc(dt_s.h, 16)
        k._l1_stw = getattr(k, "_l1_stw", {})
        k._l1_stw[ct] = dt_s.inc(16)
    for nm in ["_l1_sqmm", "_l1_slev", "_l1_stw"]:
        if hasattr(k, nm):
            delattr(k, nm)
    k.lin1_done = dict(v=vc.n, s=sc.n, t=tn.n, dt=dt_s.n)



def emit_ffn1_attn(k):
    nc, blk = k.nc, k.block
    ld, wq, tn, vc, sc, gp_s, dt_s = (k.sems[n] for n in
                                      ["ld", "wq", "tn", "vc", "sc", "gp", "dt"])
    ax_s = k.sems["ax"]

    g1v = bcast_row(k, k.di["f1_ln1_g"].ap().unsqueeze(0), k.bca[2], 1024, row_off=0)
    b1v = bcast_row(k, k.di["f1_ln1_b"].ap().unsqueeze(0), k.bca[3], 1024, row_off=1024)
    hnT = k.V(OFF_BIGB, NCH * S)
    tok = k.V(OFF_BIGA, 16384)
    dep = getattr(k, "lin1_done", None)
    pw = [("vc", max(g1v, b1v))]
    if dep is not None:
        pw.append(("dt", dep["dt"]))
    emit_ln(k, src_ap=k.sc_xn1.ap(), dst_tok=tok, dst_xnT=hnT,
            g_row=k.bca[2], b_row=k.bca[3], pre_wait=pw)

    xT = hnT
    comb = k.V(OFF_BIGA, 16384)
    qT = k.V(OFF_PAIR, 2048)
    kT = k.V(OFF_PAIR + 4096, 2048)
    vaug = k.V(OFF_PAIR + 8192, 2080)
    attnA = k.V(OFF_ATTN, 2048)
    attnB = k.V(OFF_ATTN + 4096, 2048)
    attTA = k.V(OFF_ATTNT, 2048)
    attTB = k.V(OFF_ATTNT + 4096, 2048)
    wqb = [k.V(OFF_WS + i * 2048, 1024) for i in range(2)]
    wkb = [k.V(OFF_WS + 4096 + i * 2048, 1024) for i in range(2)]
    wvb = [k.V(OFF_WS + 8192 + i * 2048, 1024) for i in range(2)]
    rbA = k.smb[:, 1152:1280]
    rbB = k.smb[:, 1280:1408]
    rr = k.fscr[:, 0:512]
    nmc = lambda i: k.sm[:, 704 + i:705 + i]
    bqc = k.sm[:, 680:688]
    bkc = k.sm[:, 688:696]

    @blk.gpsimd
    def _(gp):
        gp.dma_start(out=bqc, in_=k.di["f1_bqc"].ap()).then_inc(ld.h, 16)
        gp.dma_start(out=bkc, in_=k.di["f1_bkc"].ap()).then_inc(ld.h, 16)
        gp.memset(vaug, 1.0).then_inc(gp_s.h, 1)
    ld.inc(32)
    v_vaug_init = gp_s.inc(1)
    bvv = bcast_row(k, k.di["f1_bv"].ap().unsqueeze(0), k.bca[0], 1024, row_off=0)
    bov = bcast_row(k, k.di["f1_bo"].ap().unsqueeze(0), k.bca[1], 1024, row_off=1024)

    proj_ev = {}
    pair_done = {}
    projmm = {}

    for pr in range(8):
        par = pr % 2

        @blk.gpsimd
        def _(gp, pr=pr, par=par):
            if pr >= 2:
                gp.wait_ge(tn.h, projmm[pr - 2])
            for wsb, d_ in ((wqb[par], "f1_wq"), (wkb[par], "f1_wk"),
                            (wvb[par], "f1_wv")):
                gp.dma_start(out=wsb.rearrange("p (c j) -> p c j", c=NCH),
                             in_=AP(k.di[d_], pr * 128,
                                    [[D, 128], [128 * D, NCH], [1, 128]])
                             ).then_inc(wq.h, 16)
        wq.inc(48)
        wqv = wq.n

        @blk.tensor
        def _(te, pr=pr, par=par, wqv=wqv):
            te.wait_ge(wq.h, wqv)
            if pr == 0:
                te.wait_ge(vc.h, max(bvv, bov))
                te.wait_ge(gp_s.h, v_vaug_init)
                te.wait_ge(k.sems["vc"].h, k.ln_done["v"])
                te.wait_ge(k.sems["sc"].h, k.ln_done["s"])
            else:
                te.wait_ge(vc.h, pair_done[pr - 1])
            for c in range(NCH):
                for n_ in range(4):
                    pp = k.P[0] if n_ < 2 else k.P[1]
                    te.matmul(out=pp[:, (n_ % 2) * 512:(n_ % 2 + 1) * 512],
                              lhsT=wqb[par][:, c * 128:(c + 1) * 128],
                              rhs=xT[:, c * S + n_ * 512:c * S + (n_ + 1) * 512],
                              start=(c == 0), stop=(c == NCH - 1))
            for c in range(NCH):
                for n_ in range(4):
                    pp = k.P[2] if n_ < 2 else k.P[3]
                    mm = te.matmul(out=pp[:, (n_ % 2) * 512:(n_ % 2 + 1) * 512],
                                   lhsT=wkb[par][:, c * 128:(c + 1) * 128],
                                   rhs=xT[:, c * S + n_ * 512:c * S + (n_ + 1) * 512],
                                   start=(c == 0), stop=(c == NCH - 1))
            mm.then_inc(tn.h, 1)
        t_qk = tn.inc(1)

        @blk.scalar
        def _(s_, pr=pr, t_qk=t_qk):
            s_.wait_ge(tn.h, t_qk)
            s_.activation(out=qT[:, 0:1024], in_=k.P[0][:, :], func=AF.Identity,
                          bias=bqc[:, pr:pr + 1])
            s_.activation(out=qT[:, 1024:2048], in_=k.P[1][:, :], func=AF.Identity,
                          bias=bqc[:, pr:pr + 1])
            s_.activation(out=kT[:, 0:1024], in_=k.P[2][:, :], func=AF.Identity,
                          bias=bkc[:, pr:pr + 1])
            s_.activation(out=kT[:, 1024:2048], in_=k.P[3][:, :], func=AF.Identity,
                          bias=bkc[:, pr:pr + 1]).then_inc(sc.h, 1)
        proj_ev[pr] = sc.inc(1)

        v_base = vc.n

        @blk.tensor
        def _(te, pr=pr, par=par, v_base=v_base):
            te.wait_ge(sc.h, proj_ev[pr])
            for kt in range(NT):
                if kt >= 2:
                    te.wait_ge(vc.h, v_base + kt - 1)
                for c in range(NCH):
                    mm = te.matmul(out=k.P[0][:, (kt % 2) * 512:(kt % 2) * 512 + 128],
                                   lhsT=xT[:, c * S + kt * 128:c * S + (kt + 1) * 128],
                                   rhs=wvb[par][:, c * 128:(c + 1) * 128],
                                   start=(c == 0), stop=(c == NCH - 1))
                mm.then_inc(tn.h, 1)
        tn.inc(NT)
        t_vbase = tn.n - NT

        @blk.vector
        def _(v, pr=pr, t_vbase=t_vbase):
            for kt in range(NT):
                v.wait_ge(tn.h, t_vbase + kt + 1)
                dst = vaug[:, kt * 130:(kt + 1) * 130].rearrange(
                    "p (b j) -> p b j", b=2)[:, :, 0:64]
                src = k.P[0][:, (kt % 2) * 512:(kt % 2) * 512 + 128].rearrange(
                    "p (b j) -> p b j", b=2)
                bvs = k.bca[0][:, pr * 128:(pr + 1) * 128].rearrange(
                    "p (b j) -> p b j", b=2)
                v.tensor_tensor(out=dst, in0=src, in1=bvs, op=ALU.add).then_inc(vc.h, 1)
                vc.inc(1)
        v_vaug = vc.n

        bcolsA = k.fscr[:, 2688:2704]
        bcolsB = k.fscr[:, 2704:2720]

        @blk.vector
        def _(v, v_vaug=v_vaug):
            v.wait_ge(vc.h, v_vaug)
            v.tensor_tensor(out=attnA, in0=qT, in1=qT, op=ALU.mult)
            v.tensor_tensor(out=attnB, in0=kT, in1=kT, op=ALU.mult).then_inc(vc.h, 1)
        v_sq = vc.inc(1)

        @blk.tensor
        def _(te, v_sq=v_sq):
            te.wait_ge(vc.h, v_sq)
            onesA = k.ones_all[0:64, 0:1]
            onesB = k.ones_all[64:128, 0:1]
            for j in range(4):
                pq = (k.P[0], k.P[0], k.P[1], k.P[1])[j]
                cs_ = slice((j % 2) * 512, (j % 2) * 512 + 512)
                ts_ = slice(j * 512, (j + 1) * 512)
                te.matmul(out=pq[0:1, cs_], lhsT=onesA, rhs=attnA[0:64, ts_],
                          start=True, stop=True)
                pq2 = (k.P[2], k.P[2], k.P[3], k.P[3])[j]
                te.matmul(out=pq2[0:1, cs_], lhsT=onesB, rhs=attnA[64:128, ts_],
                          start=True, stop=True, tile_position=(64, 0))
                te.matmul(out=pq[64:65, cs_], lhsT=onesA, rhs=attnB[0:64, ts_],
                          start=True, stop=True, tile_position=(0, 64))
                mm = te.matmul(out=pq2[64:65, cs_], lhsT=onesB, rhs=attnB[64:128, ts_],
                               start=True, stop=True, tile_position=(64, 64))
            mm.then_inc(tn.h, 1)
        t_nrm = tn.inc(1)

        @blk.scalar
        def _(s_, t_nrm=t_nrm):
            s_.wait_ge(tn.h, t_nrm)
            s_.activation(out=attTA[0:1, 0:1024], in_=k.P[0][0:1, 0:1024], func=AF.Sqrt)
            s_.activation(out=attTA[0:1, 1024:2048], in_=k.P[1][0:1, 0:1024], func=AF.Sqrt)
            s_.activation(out=attTB[0:1, 0:1024], in_=k.P[2][0:1, 0:1024], func=AF.Sqrt)
            s_.activation(out=attTB[0:1, 1024:2048], in_=k.P[3][0:1, 0:1024],
                          func=AF.Sqrt).then_inc(sc.h, 1)
        s_nrm = sc.inc(1)

        @blk.vector
        def _(v, s_nrm=s_nrm, selfw=vc.n + 1):
            v.wait_ge(sc.h, s_nrm)
            v.reduce_max(out=k.sm[64:65, 704:705], in_=k.P[0][64:65, 0:1024], axis=AX.X)
            v.reduce_max(out=k.sm[64:65, 705:706], in_=k.P[1][64:65, 0:1024],
                         axis=AX.X).then_inc(vc.h, 1)
            v.wait_ge(vc.h, selfw)
            v.tensor_scalar(out=k.sm[64:65, 704:705], in0=k.sm[64:65, 704:705],
                            scalar1=k.sm[64:65, 705:706], scalar2=None, op0=ALU.max)
            v.reduce_max(out=k.sm[64:65, 706:707], in_=k.P[2][64:65, 0:1024], axis=AX.X)
            v.reduce_max(out=k.sm[64:65, 707:708], in_=k.P[3][64:65, 0:1024],
                         axis=AX.X).then_inc(vc.h, 1)
            v.wait_ge(vc.h, selfw + 1)
            v.tensor_scalar(out=k.sm[64:65, 706:707], in0=k.sm[64:65, 706:707],
                            scalar1=k.sm[64:65, 707:708], scalar2=None, op0=ALU.max
                            ).then_inc(vc.h, 1)
            v.tensor_copy(out=k.smb[64:65, 1676:1677], in_=k.sm[64:65, 704:705])
            v.tensor_copy(out=k.smb[64:65, 1677:1678],
                          in_=k.sm[64:65, 706:707]).then_inc(vc.h, 1)
        v_kmax = vc.inc(4)

        @blk.tensor
        def _(te, v_kmax=v_kmax):
            te.wait_ge(vc.h, v_kmax)
            te.matmul(out=k.P[0][:, 0:1], lhsT=k.ones_all[64:65, 0:128],
                      rhs=k.smb[64:65, 1676:1677], start=True, stop=True,
                      tile_position=(64, 0))
            te.matmul(out=k.P[2][:, 0:1], lhsT=k.ones_all[64:65, 0:128],
                      rhs=k.smb[64:65, 1677:1678], start=True, stop=True,
                      tile_position=(64, 0))
            for Q in range(NT):
                te.matmul(out=k.P[1][:, Q:Q + 1],
                          lhsT=attTA[0:1, Q * 128:(Q + 1) * 128],
                          rhs=k.ones_all[0:1, 0:1], start=True, stop=True)
                mm = te.matmul(out=k.P[3][:, Q:Q + 1],
                               lhsT=attTB[0:1, Q * 128:(Q + 1) * 128],
                               rhs=k.ones_all[0:1, 0:1], start=True, stop=True)
            mm.then_inc(tn.h, 1)
        t_bc = tn.inc(1)

        @blk.scalar
        def _(s_, t_bc=t_bc):
            s_.wait_ge(tn.h, t_bc)
            s_.activation(out=k.sm[:, 708:709], in_=k.P[0][:, 0:1], func=AF.Sqrt)
            s_.activation(out=k.sm[:, 709:710], in_=k.P[2][:, 0:1],
                          func=AF.Sqrt).then_inc(sc.h, 1)
        s_mk = sc.inc(1)

        @blk.vector
        def _(v, s_mk=s_mk, selfw=vc.n + 1):
            v.wait_ge(sc.h, s_mk)
            v.tensor_scalar(out=bcolsA, in0=k.P[1][:, 0:16],
                            scalar1=k.sm[:, 708:709], scalar2=-ALPHA,
                            op0=ALU.mult, op1=ALU.mult)
            v.tensor_scalar(out=bcolsB, in0=k.P[3][:, 0:16],
                            scalar1=k.sm[:, 709:710], scalar2=-ALPHA,
                            op0=ALU.mult, op1=ALU.mult).then_inc(vc.h, 1)
            v.wait_ge(vc.h, selfw)
            v.tensor_scalar(out=bcolsA, in0=bcolsA, scalar1=30.0, scalar2=None,
                            op0=ALU.add)
            v.tensor_scalar(out=bcolsB, in0=bcolsB, scalar1=30.0, scalar2=None,
                            op0=ALU.add).then_inc(vc.h, 1)
        b_done = vc.inc(2)

        ctx_ev = {}
        CH5 = ((0, k.P[0], k.P[1], (0, 0)), (1, k.P[2], k.P[3], (64, 0)))
        attnXs = (attnA, attnB)
        attTXs = (attTA, attTB)
        bcolsXs = (bcolsA, bcolsB)
        for Q in range(NT):
            t_scx = {}
            g_nmx = {}
            s_expx = {}
            t_trx = {}
            v_attTx = {}
            t_ctxx = {}
            v_rrx = {}
            t_rbx = {}
            for h01, pp0, pp1, tp in CH5:
                @blk.tensor
                def _(te, pr=pr, Q=Q, h01=h01, pp0=pp0, pp1=pp1, tp=tp,
                      b_done=b_done):
                    if Q == 0:
                        te.wait_ge(vc.h, b_done)
                    else:
                        te.wait_ge(vc.h, ctx_ev[(h01, Q - 1)])
                    lhs = qT[h01 * 64:h01 * 64 + 64, Q * 128:(Q + 1) * 128]
                    for n_ in range(4):
                        pp = pp0 if n_ < 2 else pp1
                        mm = te.matmul(out=pp[:, (n_ % 2) * 512:(n_ % 2 + 1) * 512],
                                       lhsT=lhs,
                                       rhs=kT[h01 * 64:h01 * 64 + 64,
                                              n_ * 512:(n_ + 1) * 512],
                                       start=True, stop=True, tile_position=tp)
                    mm.then_inc(tn.h, 1)
                t_scx[h01] = tn.inc(1)

            for h01, pp0, pp1, tp in CH5:
                @blk.scalar
                def _(s_, Q=Q, h01=h01, pp0=pp0, pp1=pp1, attnX=attnXs[h01],
                      bcol=bcolsXs[h01][:, Q:Q + 1], t_sc=t_scx[h01]):
                    s_.wait_ge(tn.h, t_sc)
                    s_.activation(out=attnX[:, 0:1024], in_=pp0[:, :], func=AF.Exp,
                                  scale=ALPHA, bias=bcol)
                    s_.activation(out=attnX[:, 1024:2048], in_=pp1[:, :],
                                  func=AF.Exp, scale=ALPHA,
                                  bias=bcol).then_inc(sc.h, 1)
                s_expx[h01] = sc.inc(1)

            for h01, pp0, pp1, tp in CH5:
                @blk.tensor
                def _(te, h01=h01, attnX=attnXs[h01], s_exp=s_expx[h01]):
                    te.wait_ge(sc.h, s_exp)
                    te.wait_ge(gp_s.h, k.v_consts_g)
                    pb = k.Pb[0] if h01 == 0 else k.Pb[2]
                    for kt in range(NT):
                        mm = te.transpose(out=pb[:, kt * 128:(kt + 1) * 128],
                                          in_=attnX[:, kt * 128:(kt + 1) * 128],
                                          identity=k.eye)
                    mm.then_inc(tn.h, 1)
                t_trx[h01] = tn.inc(1)

            for h01, pp0, pp1, tp in CH5:
                @blk.vector
                def _(v, h01=h01, attTX=attTXs[h01], t_tr=t_trx[h01]):
                    v.wait_ge(tn.h, t_tr)
                    pb = k.Pb[0] if h01 == 0 else k.Pb[2]
                    v.tensor_copy(out=attTX, in_=pb[:, 0:2048]).then_inc(vc.h, 1)
                v_attTx[h01] = vc.inc(1)

            for h01, pp0, pp1, tp in CH5:
                @blk.tensor
                def _(te, h01=h01, attTX=attTXs[h01], v_attT=v_attTx[h01]):
                    te.wait_ge(vc.h, v_attT)
                    if h01 == 0:
                        for kt in range(NT):
                            mm = te.matmul(out=k.P[1][0:65, 0:128],
                                           lhsT=vaug[:, kt * 130:kt * 130 + 65],
                                           rhs=attTX[:, kt * 128:(kt + 1) * 128],
                                           start=(kt == 0), stop=(kt == NT - 1))
                    else:
                        for kt in range(NT):
                            te.matmul(out=k.P[3][64:128, 0:128],
                                      lhsT=vaug[:, kt * 130 + 65:kt * 130 + 129],
                                      rhs=attTX[:, kt * 128:(kt + 1) * 128],
                                      start=(kt == 0), stop=(kt == NT - 1),
                                      tile_position=(0, 64))
                        for kt in range(NT):
                            mm = te.matmul(out=k.P[2][0:1, 0:128],
                                           lhsT=vaug[:, kt * 130 + 129:kt * 130 + 130],
                                           rhs=attTX[:, kt * 128:(kt + 1) * 128],
                                           start=(kt == 0), stop=(kt == NT - 1))
                    mm.then_inc(tn.h, 1)
                t_ctxx[h01] = tn.inc(1)

            for h01, pp0, pp1, tp in CH5:
                @blk.vector
                def _(v, h01=h01, t_ctx=t_ctxx[h01], selfw=vc.n + 1):
                    v.wait_ge(tn.h, t_ctx)
                    if h01 == 0:
                        v.reciprocal(out=rr[64:65, 0:128],
                                     in_=k.P[1][64:65, 0:128]).then_inc(vc.h, 1)
                        v.wait_ge(vc.h, selfw)
                        v.tensor_copy(out=k.smb[64:65, 1416:1544],
                                      in_=rr[64:65, 0:128]).then_inc(vc.h, 1)
                    else:
                        v.reciprocal(out=rr[0:1, 128:256],
                                     in_=k.P[2][0:1, 0:128]).then_inc(vc.h, 1)
                        v.wait_ge(vc.h, selfw)
                        v.tensor_copy(out=k.smb[0:1, 1544:1672],
                                      in_=rr[0:1, 128:256]).then_inc(vc.h, 1)
                v_rrx[h01] = vc.inc(2)

            for h01, pp0, pp1, tp in CH5:
                @blk.tensor
                def _(te, h01=h01, v_rr=v_rrx[h01]):
                    te.wait_ge(vc.h, v_rr)
                    if h01 == 0:
                        te.matmul(out=k.P[0][0:64, 0:128],
                                  lhsT=k.ones_all[64:65, 0:64],
                                  rhs=k.smb[64:65, 1416:1544], start=True,
                                  stop=True).then_inc(tn.h, 1)
                    else:
                        te.matmul(out=k.P[2][64:128, 0:128],
                                  lhsT=k.ones_all[0:1, 0:64],
                                  rhs=k.smb[0:1, 1544:1672], start=True, stop=True,
                                  tile_position=(0, 64)).then_inc(tn.h, 1)
                t_rbx[h01] = tn.inc(1)

            for h01, pp0, pp1, tp in CH5:
                @blk.vector
                def _(v, pr=pr, Q=Q, h01=h01, t_rb=t_rbx[h01]):
                    v.wait_ge(tn.h, t_rb)
                    cs = slice(pr * 2048 + Q * 128, pr * 2048 + (Q + 1) * 128)
                    if h01 == 0:
                        v.tensor_copy(out=rbA[0:64, :], in_=k.P[0][0:64, 0:128])
                        v.tensor_tensor(out=comb[0:64, cs],
                                        in0=k.P[1][0:64, 0:128], in1=rbA[0:64, :],
                                        op=ALU.mult).then_inc(vc.h, 1)
                    else:
                        v.tensor_copy(out=rbB[64:128, :], in_=k.P[2][64:128, 0:128])
                        v.tensor_tensor(out=comb[64:128, cs],
                                        in0=k.P[3][64:128, 0:128],
                                        in1=rbB[64:128, :],
                                        op=ALU.mult).then_inc(vc.h, 1)
                ctx_ev[(h01, Q)] = vc.inc(1)
        pair_done[pr] = ctx_ev[(1, NT - 1)]
        projmm[pr] = tn.n

    @blk.gpsimd
    def _(gp):
        gp.wait_ge(tn.h, projmm[7])
        gp.dma_start(out=k.V(OFF_WS, 8192).rearrange("p (c j) -> p c j", c=NCH),
                     in_=k.di["f1_wo"].ap().rearrange("(c p) j -> p c j", p=128)
                     ).then_inc(wq.h, 16)
    wo_ld = wq.inc(16)
    woT_sb = k.V(OFF_WS, 8192)
    xn1b = [k.V(OFF_RELB, 1024), k.V(OFF_RELB + 2048, 1024)]
    xprb = [k.fscr[:, 0:1024], k.fscr[:, 1024:2048]]
    st_ev = {}
    vo_ev = {}
    for st in range(NT):
        pj = (k.P[0], k.P[1]) if st % 2 == 0 else (k.P[2], k.P[3])
        par = st % 2

        @blk.gpsimd
        def _(gp, st=st, par=par):
            if st >= 2:
                gp.wait_ge(vc.h, vo_ev[st - 2])
            gp.dma_start(out=xn1b[par], in_=k.sc_xn1.ap()[st * 128:(st + 1) * 128, :]
                         ).then_inc(ld.h, 16)
        ld_x = ld.inc(16)

        @blk.tensor
        def _(te, st=st, pj=pj, wo_ld=wo_ld):
            if st == 0:
                te.wait_ge(wq.h, wo_ld)
                te.wait_ge(vc.h, pair_done[7])
            if st >= 2:
                te.wait_ge(vc.h, vo_ev[st - 2])
            for c in range(NCH):
                for n_ in range(2):
                    mm = te.matmul(
                        out=pj[n_][:, 0:512],
                        lhsT=comb[:, c * 2048 + st * 128:c * 2048 + (st + 1) * 128],
                        rhs=woT_sb[:, c * 1024 + n_ * 512:c * 1024 + (n_ + 1) * 512],
                        start=(c == 0), stop=(c == NCH - 1))
            mm.then_inc(tn.h, 1)
        t_o = tn.inc(1)

        @blk.vector
        def _(v, st=st, pj=pj, par=par, t_o=t_o, ld_x=ld_x):
            v.wait_ge(tn.h, t_o)
            v.wait_ge(ld.h, ld_x)
            if st >= 2:
                v.wait_ge(dt_s.h, st_ev[st - 2])
            v.tensor_add(out=k.smb[:, 0:512], in0=pj[0][:, 0:512],
                         in1=k.bca[1][:, 0:512])
            v.tensor_add(out=k.smb[:, 512:1024], in0=pj[1][:, 0:512],
                         in1=k.bca[1][:, 512:1024])
            mm = v.tensor_add(out=xprb[par], in0=k.smb[:, 0:1024], in1=xn1b[par])
            mm.then_inc(vc.h, 1)
        vo_ev[st] = vc.inc(1)

        @blk.gpsimd
        def _(gp, st=st, par=par):
            gp.wait_ge(vc.h, vo_ev[st])
            gp.dma_start(out=k.sc_xpr.ap()[st * 128:(st + 1) * 128, :],
                         in_=xprb[par]).then_inc(dt_s.h, 16)
        st_ev[st] = dt_s.inc(16)
    k.ffn1a_done = dict(v=vc.n, s=sc.h and sc.n, t=tn.n, dt=dt_s.n)



def emit_final(k):
    nc, blk = k.nc, k.block
    ld, tn, vc, sc, gp_s, dt_s, st_s = (k.sems[n] for n in
                                        ["ld", "tn", "vc", "sc", "gp", "dt", "st"])
    x1b = [k.fscr[:, 0:1024], k.fscr[:, 1024:2048]]
    tmpb = [k.V(OFF_WTEB + 4096, 1024), k.V(OFF_WTEB + 6144, 1024)]
    xpr0 = k.V(OFF_RELB, 1024, F32)
    xpr1 = k.V(OFF_SLAB, 1024, F32)
    xprs = [xpr0, xpr1]
    sl1b = [k.V(OFF_ATTN, 1024), k.V(OFF_ATTN + 2048, 1024)]
    mpb = [k.V(OFF_ATTNT, 1024), k.V(OFF_ATTNT + 2048, 1024)]
    gb = [k.V(OFF_WTEB, 1024), k.V(OFF_WTEB + 2048, 1024)]
    dep = getattr(k, "moe_done", None)
    ev_v = {}
    ev_s = {}
    ev_v0 = {}
    st_w = {}

    for t in range(NT):
        par = t % 2

        @blk.gpsimd
        def _(gp, t=t, par=par):
            if t == 0 and dep is not None:
                gp.wait_ge(dt_s.h, dep["dt"])
            if t >= 2:
                gp.wait_ge(vc.h, ev_v[t - 2])
                gp.wait_ge(st_s.h, st_w[t - 2])
            gp.dma_start(out=x1b[par], in_=k.sc_x1.ap()[t * 128:(t + 1) * 128, :]
                         ).then_inc(ld.h, 16)
            gp.dma_start(out=xprs[par], in_=k.sc_xpr.ap()[t * 128:(t + 1) * 128, :]
                         ).then_inc(ld.h, 16)
            gp.dma_start(out=sl1b[par], in_=k.sc_sl1.ap()[t * 128:(t + 1) * 128, :]
                         ).then_inc(ld.h, 16)
            gp.dma_start(out=mpb[par].rearrange("p (c j) -> p c j", c=NCH),
                         in_=k.sc_moe1.ap()[:, :, t * 128:(t + 1) * 128]
                         .transpose([1, 0, 2])).then_inc(ld.h, 16)
        ldv = ld.inc(64)

        @blk.tensor
        def _(te, t=t, par=par, ldv=ldv):
            te.wait_ge(ld.h, ldv)
            te.wait_ge(gp_s.h, k.v_consts_g)
            if t == 0 and dep is not None:
                te.wait_ge(vc.h, dep["v"])
                te.wait_ge(sc.h, dep["s"])
            if t >= 2:
                te.wait_ge(sc.h, ev_s[t - 2])
            if t >= 1:
                te.wait_ge(vc.h, ev_v0[t - 1])
            for c in range(NCH):
                mm = te.transpose(out=k.Pb[2 * par][:, c * 128:(c + 1) * 128],
                                  in_=mpb[par][:, c * 128:(c + 1) * 128],
                                  identity=k.eye)
            mm.then_inc(tn.h, 1)
        t_tr = tn.inc(1)

        @blk.vector
        def _(v, t=t, par=par, t_tr=t_tr):
            v.wait_ge(tn.h, t_tr)
            mm = v.tensor_add(out=tmpb[par], in0=k.Pb[2 * par][:, 0:1024],
                              in1=xprs[par])
            mm.then_inc(vc.h, 1)
        v_m = vc.inc(1)
        ev_v0[t] = v_m

        @blk.scalar
        def _(s_, t=t, par=par, v_m=v_m):
            s_.wait_ge(vc.h, v_m)
            s_.activation(out=gb[par], in_=tmpb[par], func=AF.Silu).then_inc(sc.h, 1)
        ev_s[t] = sc.inc(1)

        @blk.vector
        def _(v, t=t, par=par):
            v.wait_ge(sc.h, ev_s[t])
            if t >= 2:
                v.wait_ge(st_s.h, st_w[t - 2])
            v.tensor_mul(out=gb[par], in0=sl1b[par], in1=gb[par])
            mm = v.tensor_add(out=x1b[par], in0=x1b[par], in1=gb[par])
            mm.then_inc(vc.h, 1)
        ev_v[t] = vc.inc(1)

        @blk.gpsimd
        def _(gp, t=t, par=par):
            gp.wait_ge(vc.h, ev_v[t])
            gp.dma_start(out=k.out.ap()[t * 128:(t + 1) * 128, :], in_=x1b[par]
                         ).then_inc(st_s.h, 16)
        st_w[t] = st_s.inc(16)



# version 14
# speedup vs baseline: 1.1651x; 1.1651x over previous
import contextlib
import numpy as np
import concourse.bass as bass
import concourse.mybir as mybir
from concourse.bass_utils import run_bass_kernel_spmd
from concourse.ap import AP

F32 = mybir.dt.float32
BF16 = mybir.dt.bfloat16
U8 = mybir.dt.uint8
AF = mybir.ActivationFunctionType
ALU = mybir.AluOpType
AX = mybir.AxisListType

S = 2048
D = 1024
NH = 16
HD = 64
E = 8
FF = 4096
NT = S // 128
NCH = D // 128
ALPHA = 0.125

_BUILD_PHASES = None
_N_CORES = 8

CAP = 212736
OFF_FSCR = 0
OFF_SM = 12288
OFF_SMB = 16384
OFF_REL = 20480
OFF_EYE = 28672
OFF_ONESR = 28928
OFF_ONESC = 29184
OFF_CM = 29248
OFF_CZ = 30272
OFF_WS = 30720
OFF_ATTN = 47104
OFF_ATTNT = 55296
OFF_RELB = 63488
OFF_SLAB = 71680
OFF_WTEB = 80384
OFF_BCA = 88576
OFF_PAIR = 98816
OFF_BIGA = 111360
OFF_BIGB = 176896


class Sem:
    def __init__(self, handle):
        self.h = handle
        self.n = 0

    def inc(self, kk=1):
        self.n += kk
        return self.n


class K:
    pass


def build_program():
    nc = bass.Bass("TRN2", target_bir_lowering=False)
    k = K()
    k.nc = nc
    k.stack = contextlib.ExitStack()

    di = {}

    def inp(name, shape):
        di[name] = nc.dram_tensor(name, list(shape), F32, kind="ExternalInput")

    inp("x", (S, D))
    inp("l0_wqT", (D, D)); inp("l0_wkT", (D, D)); inp("l0_wvT", (D, D)); inp("l0_woT", (D, D))
    inp("l0_bq", (D,)); inp("l0_bk", (D,)); inp("l0_bv", (D,)); inp("l0_bo", (D,))
    inp("l0_relextT", (HD, 4095))
    inp("l0_M", (128, NH // 2 * HD))
    inp("l0_z", (128, NH // 2))
    inp("l0_gate_alpha", (1, 1))
    inp("l0_gwT", (D, E))
    inp("l0_w1P", (E * 32, 128, 1024)); inp("l0_w3P", (E * 32, 128, 1024))
    inp("l0_w2P", (E * 16, 128, 2048))
    inp("l1_wqT", (D, D)); inp("l1_wkT", (D, D)); inp("l1_wvT", (D, D))
    inp("l1_bq", (D,)); inp("l1_bk", (D,)); inp("l1_bv", (D,))
    inp("l1_gamma", (1, 1))
    inp("f1_ln1_g", (D,)); inp("f1_ln1_b", (D,)); inp("f1_ln2_g", (D,)); inp("f1_ln2_b", (D,))
    inp("f1_wq", (D, D)); inp("f1_wk", (D, D)); inp("f1_wv", (D, D)); inp("f1_wo", (D, D))
    inp("f1_bq", (D,)); inp("f1_bk", (D,)); inp("f1_bv", (D,)); inp("f1_bo", (D,))
    inp("f1_gwT", (D, E)); inp("f1_gb", (E,))
    inp("f1_w1P", (E * 32, 128, 1024)); inp("f1_b1", (E, FF))
    inp("f1_w2P", (E * 16, 128, 2048)); inp("f1_b2", (E, D))
    inp("l0_bqc", (128, 8)); inp("l0_bkc", (128, 8))
    inp("f1_b1c", (128, 256)); inp("f1_b2c", (128, 64))
    inp("f1_bqc", (128, 8)); inp("f1_bkc", (128, 8)); inp("f1_bvc", (128, 8))
    inp("l1_bvc", (128, 8))
    inp("f1_ln2_gc", (128, 8)); inp("f1_ln2_bc", (128, 8))
    k.di = di

    k.out = nc.dram_tensor("out", [S, D], F32, kind="ExternalOutput")
    k.sc_x1 = nc.dram_tensor("sc_x1", [S, D], F32)
    k.sc_xn1 = nc.dram_tensor("sc_xn1", [S, D], BF16)
    k.sc_xpr = nc.dram_tensor("sc_xpr", [S, D], F32)
    k.sc_acc0 = nc.dram_tensor("sc_acc0", [NCH, 128, S], BF16)
    k.sc_satt = nc.dram_tensor("sc_satt", [S, D], BF16)
    k.sc_sl1 = nc.dram_tensor("sc_sl1", [S, D], BF16)
    k.sc_moe1 = nc.dram_tensor("sc_moe1", [NCH, 128, S], BF16)
    k.sc_slab = nc.dram_tensor("sc_slab", [2, 2, NT, 128, 2176], BF16)

    k.sb = nc.alloc_sbuf_tensor("arena", [128, CAP], U8)
    ar = k.sb

    def V(off, n_elem, dt=BF16, p0=0, p1=128):
        esz = 2 if dt in (BF16,) else 4
        return ar[p0:p1, off:off + n_elem * esz].bitcast(dt)

    k.V = V
    k.fscr = V(OFF_FSCR, 3072, F32)
    k.sm = V(OFF_SM, 1024, F32)
    k.smb = V(OFF_SMB, 2048)
    k.relextT = V(OFF_REL, 4095)
    k.eye = V(OFF_EYE, 128)
    k.ones_row = V(OFF_ONESR, 128, BF16, 0, 1)
    k.ones_all = V(OFF_ONESR, 128, BF16, 0, 128)
    k.ones_col = V(OFF_ONESC, 1)
    k.cM = V(OFF_CM, 512)
    k.cz = V(OFF_CZ, 8, F32)
    k.bca = [V(OFF_BCA + i * 2048, 1024) for i in range(4)]
    k.onesel = V(OFF_BCA + 4 * 2048, 1024, BF16, 0, 8)
    k.eyef = V(OFF_SM + 896 * 4, 128, F32)
    k.onesf_row = V(OFF_SM + 768 * 4, 128, F32, 0, 1)
    k.onesf_col = V(OFF_SM + 767 * 4, 1, F32)
    k.gwTf = V(OFF_SM + 576 * 4, 64, F32)
    k.csb = V(OFF_SM + 640 * 4, 8, F32)
    k.Kb = V(OFF_SM + 648 * 4, 8, F32)
    k.xfT = [V(OFF_WTEB, 2048), V(OFF_WTEB + 4096, 2048)]
    k.gwh = V(OFF_SMB + 2048, 64)
    k.gwl = V(OFF_SMB + 2176, 64)
    k.xhi = [V(OFF_RELB, 1024), V(OFF_RELB + 4096, 1024)]
    k.xlo = [V(OFF_RELB + 2048, 1024), V(OFF_RELB + 6144, 1024)]

    k.P = [nc.alloc_psum_tensor(f"P{i}", [128, 1024], F32) for i in range(4)]
    k.Pb = [p.ap().bitcast(BF16) for p in k.P]

    k.sems = {}
    for n in ["ld", "wq", "tn", "vc", "sc", "gp", "st", "dt", "ax", "ax2", "ax3", "ax4"]:
        k.sems[n] = Sem(k.stack.enter_context(nc.semaphore(n)))
    k.block = k.stack.enter_context(nc.Block())
    return nc, k



def prepare_host_inputs(inputs):
    g = {}
    x = np.ascontiguousarray(np.asarray(inputs["L0_attn_hidden_states"], dtype=np.float32))
    g["__x_full"] = x
    t = lambda a: np.ascontiguousarray(np.asarray(a, dtype=np.float32).T)
    cp = lambda a: np.ascontiguousarray(np.asarray(a, dtype=np.float32))
    g["l0_wqT"] = t(inputs["L0_attn_Wq"]); g["l0_wkT"] = t(inputs["L0_attn_Wk"])
    g["l0_wvT"] = t(inputs["L0_attn_Wv"]); g["l0_woT"] = t(inputs["L0_attn_Wo"])
    g["l0_bq"] = cp(inputs["L0_attn_bq"]); g["l0_bk"] = cp(inputs["L0_attn_bk"])
    g["l0_bv"] = cp(inputs["L0_attn_bv"]); g["l0_bo"] = cp(inputs["L0_attn_bo"])
    rel = np.asarray(inputs["L0_attn_rel_emb"], dtype=np.float32)
    relf = rel[::-1]
    idx = np.clip(np.arange(4095) - 1023, 0, 2048)
    g["l0_relextT"] = np.ascontiguousarray(relf[idx].T)
    M = np.asarray(inputs["L0_attn_M"], dtype=np.float32)
    z = np.asarray(inputs["L0_attn_z"], dtype=np.float32)
    Mp = np.zeros((128, NH // 2 * HD), np.float32)
    zp = np.zeros((128, NH // 2), np.float32)
    for p in range(NH // 2):
        Mp[0:64, p * 64:(p + 1) * 64] = M[2 * p]
        Mp[64:128, p * 64:(p + 1) * 64] = M[2 * p + 1]
        zp[0:64, p] = z[2 * p]
        zp[64:128, p] = z[2 * p + 1]
    g["l0_M"] = Mp; g["l0_z"] = zp
    g["l0_gate_alpha"] = np.asarray(inputs["L0_attn_gate_alpha"], dtype=np.float32).reshape(1, 1)
    g["l0_gwT"] = t(inputs["L0_ffn_gate_w"])
    packh = lambda wT: np.ascontiguousarray(
        wT.reshape(E, NCH, 128, 2, 16, 128).transpose(0, 3, 4, 2, 1, 5)
        .reshape(E * 32, 128, NCH * 128))
    packo = lambda w2T: np.ascontiguousarray(
        w2T.reshape(E, 2, 16, 128, 8, 128).transpose(0, 1, 4, 3, 2, 5)
        .reshape(E * 16, 128, 16 * 128))
    g["l0_w1P"] = packh(np.asarray(inputs["L0_ffn_w1"], np.float32).transpose(0, 2, 1))
    g["l0_w3P"] = packh(np.asarray(inputs["L0_ffn_w3"], np.float32).transpose(0, 2, 1))
    g["l0_w2P"] = packo(np.asarray(inputs["L0_ffn_w2"], np.float32).transpose(0, 2, 1))
    g["l1_wqT"] = t(inputs["L1_attn_Wq"]); g["l1_wkT"] = t(inputs["L1_attn_Wk"])
    g["l1_wvT"] = t(inputs["L1_attn_Wv"])
    g["l1_bq"] = cp(inputs["L1_attn_bq"]); g["l1_bk"] = cp(inputs["L1_attn_bk"])
    g["l1_bv"] = cp(inputs["L1_attn_bv"])
    g["l1_gamma"] = np.asarray(inputs["L1_attn_gamma"], np.float32).reshape(1, 1)
    g["f1_ln1_g"] = cp(inputs["L1_ffn_ln1_g"]); g["f1_ln1_b"] = cp(inputs["L1_ffn_ln1_b"])
    g["f1_ln2_g"] = cp(inputs["L1_ffn_ln2_g"]); g["f1_ln2_b"] = cp(inputs["L1_ffn_ln2_b"])
    g["f1_wq"] = cp(inputs["L1_ffn_wq"]); g["f1_wk"] = cp(inputs["L1_ffn_wk"])
    g["f1_wv"] = cp(inputs["L1_ffn_wv"]); g["f1_wo"] = cp(inputs["L1_ffn_wo"])
    g["f1_bq"] = cp(inputs["L1_ffn_bq"]); g["f1_bk"] = cp(inputs["L1_ffn_bk"])
    g["f1_bv"] = cp(inputs["L1_ffn_bv"]); g["f1_bo"] = cp(inputs["L1_ffn_bo"])
    g["f1_gwT"] = cp(inputs["L1_ffn_gate_w"])
    g["f1_gb"] = cp(inputs["L1_ffn_gate_b"])
    g["f1_w1P"] = packh(np.asarray(inputs["L1_ffn_w1"], np.float32))
    g["f1_b1"] = cp(inputs["L1_ffn_b1"])
    g["f1_w2P"] = packo(np.asarray(inputs["L1_ffn_w2"], np.float32))
    g["f1_b2"] = cp(inputs["L1_ffn_b2"])
    colmaj = lambda a: np.ascontiguousarray(np.asarray(a, np.float32).reshape(-1, 128).T)
    g["l0_bqc"] = colmaj(g["l0_bq"]); g["l0_bkc"] = colmaj(g["l0_bk"])
    g["f1_b1c"] = np.ascontiguousarray(
        g["f1_b1"].reshape(E, 32, 128).transpose(2, 0, 1).reshape(128, 256))
    g["f1_b2c"] = np.ascontiguousarray(
        g["f1_b2"].reshape(E, 8, 128).transpose(2, 0, 1).reshape(128, 64))
    g["f1_bqc"] = colmaj(g["f1_bq"]); g["f1_bkc"] = colmaj(g["f1_bk"])
    g["f1_bvc"] = colmaj(g["f1_bv"])
    g["l1_bvc"] = colmaj(g["l1_bv"])
    g["f1_ln2_gc"] = colmaj(g["f1_ln2_g"]); g["f1_ln2_bc"] = colmaj(g["f1_ln2_b"])
    return g


def build_partial(phases=None, dumps=()):
    global _BUILD_PHASES
    old = _BUILD_PHASES
    _BUILD_PHASES = phases
    try:
        nc, k = build_program()
        emit_all(k)
        blk, st = k.block, k.sems["st"]
        for name, getter, dt in dumps:
            view = getter(k)
            if not isinstance(view, AP):
                view = view.ap()
            dt_out = nc.dram_tensor("dbg_" + name, list(view.shape), dt, kind="ExternalOutput")

            @blk.gpsimd
            def _(gp, view=view, dt_out=dt_out):
                for snm in ["vc", "sc", "tn", "gp", "dt", "ld", "wq"]:
                    if k.sems[snm].n:
                        gp.wait_ge(k.sems[snm].h, k.sems[snm].n)
                gp.dma_start(out=dt_out.ap(), in_=view).then_inc(st.h, 16)
            st.inc(16)

        if phases is not None and "p8" not in phases:
            @blk.gpsimd
            def _(gp):
                gp.dma_start(out=k.out.ap()[0:128, :], in_=k.fscr[:, 0:1024]).then_inc(st.h, 16)
            st.inc(16)
        finish_block(k)
        return nc, k
    finally:
        _BUILD_PHASES = old


def kernel(**inputs):
    g = prepare_host_inputs(inputs)
    xf = g.pop("__x_full")
    nc, k = build_program()
    emit_all(k)
    finish_block(k)
    in_maps = []
    for b in range(_N_CORES):
        m = dict(g)
        m["x"] = np.ascontiguousarray(xf[b])
        in_maps.append(m)
    res = run_bass_kernel_spmd(nc, in_maps, core_ids=list(range(_N_CORES)))
    return np.stack([np.asarray(r["out"]) for r in res.results], axis=0)



def finish_block(k):
    blk, st = k.block, k.sems["st"]

    @blk.sync
    def _(sync):
        sync.wait_ge(st.h, st.n)
    k.stack.close()


def emit_all(k):
    ph = _BUILD_PHASES
    on = lambda p: (ph is None) or (p in ph)
    emit_load_consts(k)
    if on("p0"):
        emit_gate_prep(k, 0)
        xnT = k.V(OFF_BIGB, 16384)
        tok = k.V(OFF_BIGA, 16384)
        emit_ln(k, src_ap=k.di["x"].ap(), dst_tok=tok, dst_xnT=xnT, gate=True)
        k.xnT = xnT
    if on("p1"):
        emit_moe_l0(k)
    if on("p2"):
        emit_attn_l0(k)
    if on("p3"):
        emit_combine_l0(k)
    if on("p4"):
        emit_lin_attn_l1(k)
    if on("p5"):
        emit_ffn1_attn(k)
    if on("p6p7"):
        emit_moe_l1(k)
    if on("p8"):
        emit_final(k)


"""
NOTE on the fp64 simulator's SemaphoreRace report: this program waits on
intermediate DMA-completion semaphore values (e.g. ld >= 80 when several DMAs
on one queue each inc by 16).  Hardware DMA queues complete descriptors in
order, so these intermediate waits are well-defined; the strict simulator
models the increments of a queue as unordered and therefore abstains.
"""


def emit_load_consts(k):
    blk, di = k.block, k.di
    ld, gp_s = k.sems["ld"], k.sems["gp"]

    @blk.gpsimd
    def _(gp):
        gp.dma_start(out=k.cM[:, :], in_=di["l0_M"].ap()).then_inc(ld.h, 16)
        gp.dma_start(out=k.cz[:, :], in_=di["l0_z"].ap()).then_inc(ld.h, 16)
        gp.dma_start(out=k.sm[0:1, 8:9], in_=di["l0_gate_alpha"].ap()).then_inc(ld.h, 16)
        gp.dma_start(out=k.sm[0:1, 9:10], in_=di["l1_gamma"].ap()).then_inc(ld.h, 16)
        gp.memset(k.eye, 0.0).then_inc(gp_s.h, 1)
        gp.wait_ge(gp_s.h, gp_s.n + 1)
        gp.affine_select(out=k.eye, in_=k.eye, compare_op=ALU.not_equal,
                         fill=1.0, base=0, pattern=[[-1, 128]], channel_multiplier=1)
        gp.memset(k.ones_all, 1.0)
        gp.memset(k.onesel, 0.0).then_inc(gp_s.h, 1)
        gp.wait_ge(gp_s.h, gp_s.n + 2)
        gp.affine_select(out=k.onesel.rearrange("p (e j) -> p e j", e=8),
                         in_=k.onesel.rearrange("p (e j) -> p e j", e=8),
                         compare_op=ALU.not_equal, fill=1.0, base=0,
                         pattern=[[-1, 8], [0, 128]], channel_multiplier=1)
        gp.memset(k.onesf_row, 1.0)
        gp.memset(k.onesf_col, 1.0)
        gp.memset(k.sm[0:1, 14:15], 1024.0)
        gp.memset(k.eyef, 0.0).then_inc(gp_s.h, 1)
        gp.wait_ge(gp_s.h, gp_s.n + 3)
        gp.affine_select(out=k.eyef, in_=k.eyef, compare_op=ALU.not_equal,
                         fill=1.0, base=0, pattern=[[-1, 128]], channel_multiplier=1)
        gp.memset(k.ones_col, 1.0).then_inc(gp_s.h, 1)
    ld.inc(64)
    gp_s.inc(3)
    k.v_consts_g = gp_s.inc(1)
    k.v_consts_ld = ld.n



def bcast_row(k, dram_ap, dst_view, width, row_off=0):
    blk = k.block
    ld, tn, vc = k.sems["ld"], k.sems["tn"], k.sems["vc"]
    row = k.smb[0:1, row_off:row_off + width]

    @blk.gpsimd
    def _(gp):
        gp.dma_start(out=row, in_=dram_ap).then_inc(ld.h, 16)
    ldv = ld.inc(16)

    @blk.tensor
    def _(te):
        te.wait_ge(ld.h, ldv)
        te.wait_ge(k.sems["gp"].h, k.v_consts_g)
        te.wait_ge(vc.h, vc.n)
        for j in range((width + 511) // 512):
            w = min(512, width - j * 512)
            mm = te.matmul(out=k.P[3][:, j * 512:j * 512 + w], lhsT=k.ones_row,
                           rhs=row[:, j * 512:j * 512 + w], start=True, stop=True)
        mm.then_inc(tn.h, 1)
    tv = tn.inc(1)

    @blk.vector
    def _(v):
        v.wait_ge(tn.h, tv)
        v.tensor_copy(out=dst_view, in_=k.P[3][:, 0:width]).then_inc(vc.h, 1)
    return vc.inc(1)


def emit_gate_prep(k, layer):
    blk = k.block
    ld, tn, vc, sc, gp_s = (k.sems[n] for n in ["ld", "tn", "vc", "sc", "gp"])
    gwd = k.di["l0_gwT"] if layer == 0 else k.di["f1_gwT"]
    gcol = k.sm[:, 656:664]
    bcol = k.sm[:, 664:672]
    gbrow = k.sm[0:1, 672:680]
    cs_row = k.sm[0:1, 744:752]
    K_row = k.sm[0:1, 752:760]

    @blk.gpsimd
    def _(gp):
        gp.dma_start(out=k.gwTf.rearrange("p (c e) -> p c e", c=NCH),
                     in_=gwd.ap().rearrange("(c p) e -> p c e", p=128)).then_inc(ld.h, 16)
        if layer == 1:
            gp.dma_start(out=gcol, in_=k.di["f1_ln2_gc"].ap()).then_inc(ld.h, 16)
            gp.dma_start(out=bcol, in_=k.di["f1_ln2_bc"].ap()).then_inc(ld.h, 16)
            gp.dma_start(out=gbrow, in_=k.di["f1_gb"].ap().unsqueeze(0)).then_inc(ld.h, 16)
    ldv = ld.inc(64 if layer == 1 else 16)

    @blk.tensor
    def _(te, ldv=ldv):
        te.wait_ge(ld.h, ldv)
        te.wait_ge(gp_s.h, k.v_consts_g)
        te.wait_ge(vc.h, vc.n)
        if layer == 1:
            for c in range(NCH):
                mm = te.matmul(out=k.P[2][0:1, 512:520], lhsT=bcol[:, c:c + 1],
                               rhs=k.gwTf[:, c * 8:(c + 1) * 8],
                               start=(c == 0), stop=(c == NCH - 1))
            mm.then_inc(tn.h, 1)
        else:
            te.nop().then_inc(tn.h, 1)
    t_K = tn.inc(1)

    if layer == 1:
        @blk.vector
        def _(v, t_K=t_K):
            v.wait_ge(tn.h, t_K)
            v.tensor_tensor(out=K_row, in0=k.P[2][0:1, 512:520], in1=gbrow,
                            op=ALU.add).then_inc(vc.h, 1)
            for c in range(NCH):
                mm = v.tensor_scalar(out=k.gwTf[:, c * 8:(c + 1) * 8],
                                in0=k.gwTf[:, c * 8:(c + 1) * 8],
                                scalar1=gcol[:, c:c + 1], scalar2=None, op0=ALU.mult)
            mm.then_inc(vc.h, 1)
        v_scaled = vc.inc(2)
    else:
        v_scaled = vc.n

    @blk.tensor
    def _(te, t_K=t_K, v_scaled=v_scaled, layer=layer):
        te.wait_ge(tn.h, t_K)
        if layer == 1:
            te.wait_ge(vc.h, v_scaled)
        for c in range(NCH):
            mm = te.matmul(out=k.P[2][0:1, 0:8], lhsT=k.onesf_col,
                           rhs=k.gwTf[:, c * 8:(c + 1) * 8],
                           start=(c == 0), stop=(c == NCH - 1))
        mm.then_inc(tn.h, 1)
    t_cs = tn.inc(1)

    @blk.vector
    def _(v, t_cs=t_cs):
        v.wait_ge(tn.h, t_cs)
        v.tensor_copy(out=cs_row, in_=k.P[2][0:1, 0:8]).then_inc(vc.h, 1)
    v_cs = vc.inc(1)

    @blk.tensor
    def _(te, v_cs=v_cs):
        te.wait_ge(vc.h, v_cs)
        te.matmul(out=k.P[2][:, 0:8], lhsT=k.onesf_row, rhs=cs_row,
                  start=True, stop=True).then_inc(tn.h, 1)
        if True:
            pass
    t_bc = tn.inc(1)

    @blk.vector
    def _(v, t_bc=t_bc, layer=layer):
        v.wait_ge(tn.h, t_bc)
        v.tensor_copy(out=k.csb, in_=k.P[2][:, 0:8]).then_inc(vc.h, 1)
    v_bc = vc.inc(1)

    @blk.scalar
    def _(s_, v_bc=v_bc):
        s_.wait_ge(vc.h, v_bc)
        s_.activation(out=k.gwh, in_=k.gwTf, func=AF.Identity).then_inc(sc.h, 1)
    s_gh = sc.inc(1)

    @blk.vector
    def _(v, s_gh=s_gh):
        v.wait_ge(sc.h, s_gh)
        v.tensor_sub(out=k.gwl, in0=k.gwTf, in1=k.gwh).then_inc(vc.h, 1)
    v_bc = vc.inc(1)

    if layer == 1:
        @blk.tensor
        def _(te, v_bc=v_bc):
            te.wait_ge(vc.h, v_bc)
            te.matmul(out=k.P[2][:, 0:8], lhsT=k.onesf_row, rhs=K_row,
                      start=True, stop=True).then_inc(tn.h, 1)
        t_kb = tn.inc(1)

        @blk.vector
        def _(v, t_kb=t_kb):
            v.wait_ge(tn.h, t_kb)
            v.tensor_copy(out=k.Kb, in_=k.P[2][:, 0:8]).then_inc(vc.h, 1)
        v_bc = vc.inc(1)
    k.gate_prep = (v_bc, layer)


def emit_ln(k, src_ap, dst_tok, dst_xnT=None, g_row=None, b_row=None, eps=1e-5,
            stats_base=16, dst_tok_dram=None, gate=False, pre_wait=None):
    blk = k.block
    ld, tn, vc, sc, gp_s, dt_s = (k.sems[n] for n in ["ld", "tn", "vc", "sc", "gp", "dt"])
    st0 = stats_base
    sum_c = lambda t: k.sm[:, st0 + t:st0 + t + 1]
    ssq_c = lambda t: k.sm[:, st0 + NT + t:st0 + NT + t + 1]
    nmr_c = lambda t: k.sm[:, st0 + 2 * NT + t:st0 + 2 * NT + t + 1]
    rst_c = lambda t: k.sm[:, st0 + 3 * NT + t:st0 + 3 * NT + t + 1]
    var_c = lambda t: k.sm[:, st0 + 4 * NT + t:st0 + 4 * NT + t + 1]
    stage = [k.fscr[:, 0:1024], k.fscr[:, 1024:2048], k.fscr[:, 2048:3072]]
    trash = k.V(OFF_PAIR, 1024, F32)
    done_s = {}
    done_v = {}
    trev = {}

    for t in range(NT):
        xt = stage[t % 3]

        @blk.gpsimd
        def _(gp, t=t, xt=xt, gate=gate, pre_wait=pre_wait):
            if t == 0 and pre_wait is not None:
                for snm, sval in pre_wait:
                    gp.wait_ge(k.sems[snm].h, sval)
            if t >= 3:
                gp.wait_ge(sc.h, done_s[t - 3])
                gp.wait_ge(vc.h, done_v[t - 3])
                if gate:
                    gp.wait_ge(vc.h, k._g_lo[t - 3])
            gp.dma_start(out=xt, in_=src_ap[t * 128:(t + 1) * 128, :]).then_inc(ld.h, 16)
        ldv = ld.inc(16)

        @blk.vector
        def _(v, t=t, xt=xt, ldv=ldv):
            v.wait_ge(ld.h, ldv)
            v.reduce_sum(out=sum_c(t), in_=xt, axis=AX.X)
            v.scalar_tensor_tensor(out=trash, in0=xt, scalar=1.0, in1=xt,
                                   op0=ALU.mult, op1=ALU.mult,
                                   accum_out=ssq_c(t)).then_inc(vc.h, 1)
        vsum = vc.inc(1)

        @blk.gpsimd
        def _(gp, t=t, vsum=vsum):
            gp.wait_ge(vc.h, vsum)
            gp.tensor_scalar(out=nmr_c(t), in0=sum_c(t), scalar1=-1.0 / D,
                             scalar2=None, op0=ALU.mult)
            gp.tensor_tensor(out=var_c(t), in0=nmr_c(t), in1=nmr_c(t), op=ALU.mult)
            gp.tensor_scalar(out=ssq_c(t), in0=ssq_c(t), scalar1=1.0 / D,
                             scalar2=None, op0=ALU.mult)
            gp.tensor_tensor(out=var_c(t), in0=ssq_c(t), in1=var_c(t), op=ALU.subtract)
            gp.tensor_scalar(out=var_c(t), in0=var_c(t), scalar1=float(eps),
                             scalar2=None, op0=ALU.add).then_inc(gp_s.h, 1)
        gpv = gp_s.inc(1)

        @blk.scalar
        def _(s, t=t, gpv=gpv):
            s.wait_ge(gp_s.h, gpv)
            s.activation(out=rst_c(t), in_=var_c(t), func=AF.Sqrt).then_inc(sc.h, 1)
        scv2 = sc.inc(1)

        @blk.vector
        def _(v, t=t, scv2=scv2):
            v.wait_ge(sc.h, scv2)
            v.reciprocal(out=rst_c(t), in_=rst_c(t)).then_inc(vc.h, 1)
        vcv0 = vc.inc(1)

        @blk.gpsimd
        def _(gp, t=t, vcv0=vcv0):
            gp.wait_ge(vc.h, vcv0)
            gp.tensor_tensor(out=nmr_c(t), in0=nmr_c(t), in1=rst_c(t),
                             op=ALU.mult).then_inc(gp_s.h, 1)
        gpv2 = gp_s.inc(1)

        if gate:
            L_t = k.sm[:, 96 + t * 8:96 + t * 8 + 8]
            a_t = k.sm[:, 256 + t * 8:256 + t * 8 + 8]
            tr8 = k.sm[:, 224:232]

            xhi_t = k.xhi[t % 2]
            xlo_t = k.xlo[t % 2]
            xfT_t = k.xfT[t % 2]
            gpb = k.Pb[1] if t % 2 == 0 else k.Pb[3]

            @blk.vector
            def _(v, t=t, xt=xt, xhi_t=xhi_t, xlo_t=xlo_t, ldv=ldv):
                v.wait_ge(ld.h, ldv)
                if t >= 2:
                    v.wait_ge(tn.h, k._gx_tr[t - 2])
                v.tensor_copy(out=xhi_t, in_=xt)
                v.tensor_sub(out=xlo_t, in0=xt, in1=xhi_t).then_inc(vc.h, 1)
            v_lo = vc.inc(1)
            k._g_lo = getattr(k, "_g_lo", {})
            k._g_lo[t] = v_lo

            @blk.tensor
            def _(te, t=t, v_lo=v_lo, xhi_t=xhi_t, xlo_t=xlo_t, gpb=gpb):
                te.wait_ge(gp_s.h, k.v_consts_g)
                te.wait_ge(vc.h, v_lo)
                if t == 0:
                    te.wait_ge(vc.h, k.gate_prep[0])
                if t >= 2:
                    te.wait_ge(vc.h, k._gx_ev[t - 2])
                for c in range(NCH):
                    te.transpose(out=gpb[:, c * 128:(c + 1) * 128],
                                 in_=xhi_t[:, c * 128:(c + 1) * 128], identity=k.eye)
                for c in range(NCH):
                    mm = te.transpose(out=gpb[:, 1024 + c * 128:1024 + (c + 1) * 128],
                                      in_=xlo_t[:, c * 128:(c + 1) * 128], identity=k.eye)
                mm.then_inc(tn.h, 1)
            k._gx_tr = getattr(k, "_gx_tr", {})
            k._gx_tr[t] = tn.inc(1)

            @blk.vector
            def _(v, t=t, xfT_t=xfT_t, gpb=gpb):
                v.wait_ge(tn.h, k._gx_tr[t])
                if t >= 2:
                    v.wait_ge(tn.h, k._gL_mm[t - 2])
                v.tensor_copy(out=xfT_t, in_=gpb[:, 0:2048]).then_inc(vc.h, 1)
            k._gx_ev = getattr(k, "_gx_ev", {})
            k._gx_ev[t] = vc.inc(1)

            @blk.tensor
            def _(te, t=t, xfT_t=xfT_t):
                te.wait_ge(vc.h, k._gx_ev[t])
                if t >= 2:
                    te.wait_ge(vc.h, k._gL_ev[t - 2])
                nmm = 0
                for c in range(NCH):
                    hiT = xfT_t[:, c * 128:(c + 1) * 128]
                    loT = xfT_t[:, 1024 + c * 128:1024 + (c + 1) * 128]
                    gh = k.gwh[:, c * 8:(c + 1) * 8]
                    gl = k.gwl[:, c * 8:(c + 1) * 8]
                    for lhs_, rhs_ in ((hiT, gh), (hiT, gl), (loT, gh)):
                        mm = te.matmul(out=k.P[2][:, (t % 2) * 512:(t % 2) * 512 + 8],
                                       lhsT=lhs_, rhs=rhs_,
                                       start=(nmm == 0), stop=(nmm == 3 * NCH - 1))
                        nmm += 1
                mm.then_inc(tn.h, 1)
            k._gL_mm = getattr(k, "_gL_mm", {})
            k._gL_mm[t] = tn.inc(1)

            @blk.vector
            def _(v, t=t, gpv2=gpv2, a_t=a_t, L_t=L_t, tr8=tr8):
                v.wait_ge(tn.h, k._gL_mm[t])
                v.wait_ge(gp_s.h, gpv2)
                v.tensor_scalar(out=tr8, in0=k.P[2][:, (t % 2) * 512:(t % 2) * 512 + 8],
                                scalar1=rst_c(t), scalar2=None, op0=ALU.mult)
                v.scalar_tensor_tensor(out=(a_t if k.gate_prep[1] == 1 else L_t),
                                       in0=k.csb, scalar=nmr_c(t), in1=tr8,
                                       op0=ALU.mult, op1=ALU.add).then_inc(vc.h, 1)
            k._gL_ev = getattr(k, "_gL_ev", {})
            k._gL_ev[t] = vc.inc(1)

            if k.gate_prep[1] == 1:
                @blk.gpsimd
                def _(gp, t=t, a_t=a_t, L_t=L_t):
                    gp.wait_ge(vc.h, k._gL_ev[t])
                    gp.tensor_tensor(out=L_t, in0=a_t, in1=k.Kb,
                                     op=ALU.add).then_inc(gp_s.h, 1)
                k._gL_fin = getattr(k, "_gL_fin", {})
                k._gL_fin[t] = gp_s.inc(1)

        tokdst = dst_tok[:, t * D:(t + 1) * D]
        if g_row is None:
            @blk.vector
            def _(v, t=t, xt=xt, gpv2=gpv2, tokdst=tokdst):
                v.wait_ge(gp_s.h, gpv2)
                v.tensor_scalar(out=tokdst, in0=xt, scalar1=rst_c(t),
                                scalar2=nmr_c(t), op0=ALU.mult,
                                op1=ALU.add).then_inc(vc.h, 1)
            done_v[t] = vc.inc(1)
            done_s[t] = sc.n
            dep_tr = ("vc", done_v[t])
        else:
            nrm = k.smb[:, 0:1024]

            @blk.vector
            def _(v, t=t, xt=xt, gpv2=gpv2, tokdst=tokdst, nrm=nrm):
                v.wait_ge(gp_s.h, gpv2)
                v.tensor_scalar(out=nrm, in0=xt, scalar1=rst_c(t),
                                scalar2=nmr_c(t), op0=ALU.mult, op1=ALU.add)
                v.tensor_tensor(out=nrm, in0=nrm, in1=g_row, op=ALU.mult)
                v.tensor_tensor(out=tokdst, in0=nrm,
                                in1=b_row, op=ALU.add).then_inc(vc.h, 1)
            done_v[t] = vc.inc(1)
            done_s[t] = sc.n
            dep_tr = ("vc", done_v[t])

        if dst_tok_dram is not None:
            eng, depv = dep_tr

            @blk.gpsimd
            def _(gp, t=t, eng=eng, depv=depv, tokdst=tokdst):
                gp.wait_ge(k.sems[eng].h, depv)
                gp.dma_start(out=dst_tok_dram[t * 128:(t + 1) * 128, :],
                             in_=tokdst).then_inc(dt_s.h, 16)
            dt_s.inc(16)

        if dst_xnT is not None:
            eng, depv = dep_tr
            xpb = k.Pb[0] if (gate or t % 2 == 0) else k.Pb[3]
            xdep = 1 if gate else 2

            @blk.tensor
            def _(te, t=t, eng=eng, depv=depv, tokdst=tokdst, xpb=xpb, xdep=xdep):
                te.wait_ge(k.sems[eng].h, depv)
                te.wait_ge(gp_s.h, k.v_consts_g)
                if t >= xdep:
                    te.wait_ge(vc.h, trev[t - xdep])
                for c in range(NCH):
                    mm = te.transpose(out=xpb[:, c * 128:(c + 1) * 128],
                                      in_=tokdst[:, c * 128:(c + 1) * 128],
                                      identity=k.eye)
                mm.then_inc(tn.h, 1)
            tnv = tn.inc(1)

            @blk.vector
            def _(v, t=t, tnv=tnv, dst_xnT=dst_xnT, xpb=xpb):
                v.wait_ge(tn.h, tnv)
                dst = dst_xnT.rearrange("p (c s) -> p c s", c=NCH)[:, :, t * 128:(t + 1) * 128]
                v.tensor_copy(out=dst,
                              in_=xpb[:, 0:1024].rearrange("p (c j) -> p c j", c=NCH)
                              ).then_inc(vc.h, 1)
            trev[t] = vc.inc(1)
    k.ln_done = dict(v=vc.n, s=sc.n, t=tn.n, g=gp_s.n, dt=dt_s.n)
    if gate:
        k.gate_ready = dict(v=vc.n, g=gp_s.n)
        for nm in ["_gx_tr", "_gx_ev", "_gL_mm", "_gL_ev", "_gL_fin"]:
            if hasattr(k, nm):
                delattr(k, nm)


def emit_moe_l0(k):
    _moe(k, layer=0)


def emit_moe_l1(k):
    g2v = bcast_row(k, k.di["f1_ln2_g"].ap().unsqueeze(0), k.bca[2], 1024, row_off=0)
    b2v = bcast_row(k, k.di["f1_ln2_b"].ap().unsqueeze(0), k.bca[3], 1024, row_off=1024)
    emit_gate_prep(k, 1)
    hn2T = k.V(OFF_BIGB, NCH * S)
    tok = k.V(OFF_BIGA, 16384)
    dep = getattr(k, "ffn1a_done", None)
    pw = [("vc", max(g2v, b2v))]
    if dep is not None:
        pw.append(("dt", dep["dt"]))
    emit_ln(k, src_ap=k.sc_xpr.ap(), dst_tok=tok, dst_xnT=hn2T,
            g_row=k.bca[2], b_row=k.bca[3], gate=True, pre_wait=pw)
    _moe(k, layer=1)


def _moe(k, layer):
    nc, blk = k.nc, k.block
    ld, wq, tn, vc, sc, gp_s, dt_s = (k.sems[n] for n in
                                      ["ld", "wq", "tn", "vc", "sc", "gp", "dt"])
    xT = k.V(OFF_BIGB, NCH * S)
    slab = k.V(OFF_BIGA, 16 * S)
    s1 = k.V(OFF_PAIR, 2048)
    tsb = k.V(OFF_PAIR + 4096, 2048)
    if layer == 0:
        W1DEP = 3
        w1off = [OFF_WS, OFF_WS + 4096, OFF_FSCR + 6400]
        w1x = [k.V(o, 1024, F32) for o in w1off]
        w1b = [k.V(o, 1024) for o in w1off]
    else:
        W1DEP = 4
        w1x = [k.V(OFF_WS + i * 4096, 1024, F32) for i in range(4)]
        w1b = [k.V(o, 1024) for o in
               (OFF_FSCR + 6400, OFF_FSCR + 8448, OFF_PAIR, OFF_PAIR + 2048)]
    w3x = [k.V(OFF_WS + 8192 + i * 4096, 1024, F32) for i in range(2)]
    w3b = [k.V(OFF_WS + 8192 + i * 4096, 1024) for i in range(2)]
    W2DEP = 5
    w2b = [k.V(off, 2048) for off in
           (OFF_REL, OFF_REL + 4096, OFF_BCA, OFF_BCA + 4096, OFF_PAIR + 8192)]
    accb = [k.V(OFF_ATTN + i * 4096, 2048) for i in range(8)]
    wtebuf = [k.V(OFF_WTEB + i * 4096, 2048) for i in range(2)]
    gwT_sb = k.smb[:, 1024:1088]
    gb_b = k.smb[:, 1088:1096]
    L = lambda t: k.sm[:, 96 + t * 8:96 + (t + 1) * 8]
    acc_dram = k.sc_acc0 if layer == 0 else k.sc_moe1
    w1d = k.di["l0_w1P"] if layer == 0 else k.di["f1_w1P"]
    w3d = k.di["l0_w3P"] if layer == 0 else None
    w2d = k.di["l0_w2P"] if layer == 0 else k.di["f1_w2P"]
    ax2, ax3 = k.sems["ax2"], k.sems["ax3"]
    k._moe_pre_tn = k.sems["tn"].n
    k._moe_pre_vc = k.sems["vc"].n
    gwd = k.di["l0_gwT"] if layer == 0 else k.di["f1_gwT"]
    b1c = k.sm[:, 256:512]
    b2c = k.sm[:, 512:576]

    gate_ev = {}
    gr = k.gate_ready
    if layer == 1:
        @blk.gpsimd
        def _(gp):
            gp.dma_start(out=b1c, in_=k.di["f1_b1c"].ap()).then_inc(ld.h, 16)
            gp.dma_start(out=b2c, in_=k.di["f1_b2c"].ap()).then_inc(ld.h, 16)
        ld.inc(32)

    FS = k.fscr
    absl = lambda t: FS[:, 128 + t * 8:128 + (t + 1) * 8]
    nm1 = lambda t: FS[:, 256 + t:257 + t]
    nm2 = lambda t: FS[:, 272 + t:273 + t]
    m1 = lambda t: FS[:, 288 + t:289 + t]
    m2 = lambda t: FS[:, 304 + t:305 + t]
    Z1 = FS[:, 320:336]
    Z2 = FS[:, 336:352]
    mu1 = FS[:, 352:368]
    mu2 = FS[:, 368:384]
    f1_ = lambda t: FS[:, 384 + t * 8:384 + (t + 1) * 8]
    d1_ = lambda t: FS[:, 512 + t * 8:512 + (t + 1) * 8]
    u1_ = lambda t: FS[:, 640 + t * 8:640 + (t + 1) * 8]
    e1_ = lambda t: FS[:, 768 + t * 8:768 + (t + 1) * 8]
    i1_ = lambda t: FS[:, 896 + t * 8:896 + (t + 1) * 8]
    lm_ = lambda t: FS[:, 1024 + t * 8:1024 + (t + 1) * 8]
    nt_ = lambda t: FS[:, 1152 + t * 8:1152 + (t + 1) * 8]
    i2_ = lambda t: FS[:, 1280 + t * 8:1280 + (t + 1) * 8]
    wt_ = lambda t: FS[:, 1408 + t * 8:1408 + (t + 1) * 8]
    tr8 = FS[:, 1536:1544]

    vg = gr["v"]
    if layer == 0:
        @blk.scalar
        def _(s_, vg=vg):
            s_.wait_ge(vc.h, vg)
            s_.wait_ge(gp_s.h, gr["g"])
            s_.activation(out=FS[:, 128:256], in_=k.sm[:, 96:224], func=AF.Abs
                          ).then_inc(sc.h, 1)
        s_abs = sc.inc(1)

        @blk.vector
        def _(v, vg=vg, s_abs=s_abs):
            v.wait_ge(vc.h, vg)
            v.wait_ge(sc.h, s_abs)
            for t in range(NT):
                v.reduce_max(out=nm1(t), in_=L(t), axis=AX.X, negate=True)
            for t in range(NT):
                v.tensor_scalar(out=m1(t), in0=nm1(t), scalar1=-1.0, scalar2=None,
                                op0=ALU.mult)
            for t in range(NT):
                v.tensor_scalar(out=i1_(t), in0=L(t), scalar1=m1(t), scalar2=None,
                                op0=ALU.is_ge)
            for t in range(NT):
                v.tensor_scalar(out=f1_(t), in0=absl(t), scalar1=m1(t), scalar2=0.02,
                                op0=ALU.max, op1=ALU.mult)
            for t in range(NT):
                v.tensor_scalar(out=d1_(t), in0=L(t), scalar1=m1(t), scalar2=-1.0,
                                op0=ALU.subtract, op1=ALU.mult)
            for t in range(NT):
                mm = v.tensor_tensor(out=u1_(t), in0=d1_(t), in1=f1_(t), op=ALU.is_le)
            mm.then_inc(vc.h, 1)
        v_r1 = vc.inc(1)

        @blk.scalar
        def _(s_, v_r1=v_r1):
            s_.wait_ge(vc.h, v_r1)
            for t in range(NT):
                s_.activation(out=e1_(t), in_=L(t), func=AF.Exp, bias=nm1(t))

            s_.activation(out=tr8, in_=L(0), func=AF.Identity).then_inc(sc.h, 1)
        s_r1 = sc.inc(1)

        @blk.vector
        def _(v, s_r1=s_r1):
            v.wait_ge(sc.h, s_r1)
            for t in range(NT):
                v.scalar_tensor_tensor(out=d1_(t), in0=e1_(t), scalar=1.0, in1=u1_(t),
                                       op0=ALU.mult, op1=ALU.mult,
                                       accum_out=Z1[:, t:t + 1])
            for t in range(NT):
                v.scalar_tensor_tensor(out=lm_(t), in0=i1_(t), scalar=-1e30, in1=L(t),
                                       op0=ALU.mult, op1=ALU.add)
            for t in range(NT):
                v.reduce_max(out=nm2(t), in_=lm_(t), axis=AX.X, negate=True)
            for t in range(NT):
                v.tensor_scalar(out=m2(t), in0=nm2(t), scalar1=-1.0, scalar2=None,
                                op0=ALU.mult)
            for t in range(NT):
                v.tensor_scalar(out=nt_(t), in0=i1_(t), scalar1=-1.0, scalar2=1.0,
                                op0=ALU.mult, op1=ALU.add)
            for t in range(NT):
                v.tensor_scalar(out=f1_(t), in0=absl(t), scalar1=m2(t), scalar2=0.02,
                                op0=ALU.max, op1=ALU.mult)
            for t in range(NT):
                v.tensor_scalar(out=d1_(t), in0=L(t), scalar1=m2(t), scalar2=-1.0,
                                op0=ALU.subtract, op1=ALU.mult)
            for t in range(NT):
                v.tensor_tensor(out=u1_(t), in0=d1_(t), in1=f1_(t), op=ALU.is_le)
            for t in range(NT):
                v.tensor_tensor(out=u1_(t), in0=u1_(t), in1=nt_(t), op=ALU.mult)
            for t in range(NT):
                mm = v.tensor_scalar(out=i2_(t), in0=L(t), scalar1=m2(t), scalar2=None,
                                op0=ALU.is_ge)
            mm.then_inc(vc.h, 1)
        v_r2 = vc.inc(1)

        @blk.scalar
        def _(s_, v_r2=v_r2):
            s_.wait_ge(vc.h, v_r2)
            for t in range(NT):
                s_.activation(out=e1_(t), in_=L(t), func=AF.Exp, bias=nm2(t))
            s_.activation(out=tr8, in_=L(0), func=AF.Identity).then_inc(sc.h, 1)
        s_r2 = sc.inc(1)

        @blk.vector
        def _(v, s_r2=s_r2):
            v.wait_ge(sc.h, s_r2)
            for t in range(NT):
                v.scalar_tensor_tensor(out=d1_(t), in0=e1_(t), scalar=1.0, in1=u1_(t),
                                       op0=ALU.mult, op1=ALU.mult,
                                       accum_out=Z2[:, t:t + 1])
            v.reciprocal(out=mu1, in_=Z1)
            v.reciprocal(out=mu2, in_=Z2)
            for t in range(NT):
                v.tensor_tensor(out=i2_(t), in0=i2_(t), in1=nt_(t), op=ALU.mult)
            for t in range(NT):
                v.tensor_scalar(out=i2_(t), in0=i2_(t), scalar1=mu2[:, t:t + 1],
                                scalar2=None, op0=ALU.mult)
            for t in range(NT):
                mm = v.scalar_tensor_tensor(out=wt_(t), in0=i1_(t), scalar=mu1[:, t:t + 1],
                                       in1=i2_(t), op0=ALU.mult, op1=ALU.add)
            mm.then_inc(vc.h, 1)
        v_route = vc.inc(1)
    else:
        @blk.vector
        def _(v, vg=vg):
            v.wait_ge(vc.h, vg)
            v.wait_ge(gp_s.h, gr["g"])
            for t in range(NT):
                v.reduce_max(out=m1(t), in_=L(t), axis=AX.X)
            for t in range(NT):
                mm = v.tensor_scalar(out=wt_(t), in0=L(t), scalar1=m1(t), scalar2=None,
                                op0=ALU.is_ge)
            mm.then_inc(vc.h, 1)
        v_route = vc.inc(1)

    wrow = k.smb[0:1, 0:2048 * 0 + 2048]
    tr_ev = {}
    for t in range(NT):
        @blk.tensor
        def _(te, t=t, v_route=v_route):
            te.wait_ge(vc.h, v_route)
            te.wait_ge(gp_s.h, k.v_consts_g)
            if t >= 1:
                te.wait_ge(vc.h, tr_ev[t - 1])
            mm = te.transpose(out=k.P[3][0:8, (t % 4) * 128:(t % 4) * 128 + 128],
                              in_=wt_(t), identity=k.eyef)
            mm.then_inc(tn.h, 1)
        tnv = tn.inc(1)

        @blk.vector
        def _(v, t=t, tnv=tnv):
            v.wait_ge(tn.h, tnv)
            v.tensor_copy(out=k.smb[0:8, t * 128:(t + 1) * 128],
                          in_=k.P[3][0:8, (t % 4) * 128:(t % 4) * 128 + 128]
                          ).then_inc(vc.h, 1)
        tr_ev[t] = vc.inc(1)
    v_wteT = vc.n

    NFH = 16
    h_ev = {}
    wdma = {}
    first_acc = True
    for e in range(E):
        wb = wtebuf[e % 2]

        @blk.tensor
        def _(te, e=e, v_wteT=v_wteT):
            te.wait_ge(vc.h, v_wteT)
            if e >= 1:
                te.wait_ge(vc.h, k._moe_vwb[e - 1])
                te.wait_ge(vc.h, k._moe_lastout[e - 1])
            if e >= 2:
                te.wait_ge(vc.h, k._moe_lastev[e - 2])
            lhs = k.onesel[:, e * 128:(e + 1) * 128]
            for j in range(2):
                mm = te.matmul(out=k.P[3][:, j * 512:(j + 1) * 512],
                               lhsT=lhs, rhs=k.smb[0:8, j * 512:(j + 1) * 512],
                               start=True, stop=True)
            mm.then_inc(tn.h, 1)
        tnv = tn.inc(1)

        @blk.vector
        def _(v, e=e, tnv=tnv, wb=wb):
            v.wait_ge(tn.h, tnv)
            v.tensor_copy(out=wb[:, 0:1024], in_=k.P[3][:, 0:1024]).then_inc(vc.h, 1)
        v_wb0 = vc.inc(1)

        @blk.tensor
        def _(te, e=e, v_wb0=v_wb0):
            te.wait_ge(vc.h, v_wb0)
            lhs = k.onesel[:, e * 128:(e + 1) * 128]
            for j in range(2, 4):
                mm = te.matmul(out=k.P[3][:, (j - 2) * 512:(j - 1) * 512],
                               lhsT=lhs, rhs=k.smb[0:8, j * 512:(j + 1) * 512],
                               start=True, stop=True)
            mm.then_inc(tn.h, 1)
        tnv2 = tn.inc(1)

        @blk.vector
        def _(v, e=e, tnv2=tnv2, wb=wb):
            v.wait_ge(tn.h, tnv2)
            v.tensor_copy(out=wb[:, 1024:2048], in_=k.P[3][:, 0:1024]).then_inc(vc.h, 1)
        v_wb = vc.inc(1)
        if not hasattr(k, "_moe_vwb"):
            k._moe_vwb = {}
        k._moe_vwb[e] = v_wb

        for Hf in range(2):
            fb = Hf * 2048
            for ft in range(NFH):
                par = ft % 2
                gi = (e * 2 + Hf) * NFH + ft

                widx = gi
                wpar = gi % W1DEP
                w1_on_sync = (layer == 0) or (gi % 2 == 0)
                w1_eng = blk.sync if w1_on_sync else blk.scalar
                w1_sem = ax2 if w1_on_sync else ax3

                @w1_eng
                def _(sy, wpar=wpar, gi=gi, widx=widx, w1_sem=w1_sem):
                    if gi >= W1DEP:
                        sy.wait_ge(tn.h, wdma[gi - W1DEP])
                    else:
                        sy.wait_ge(tn.h, k._moe_pre_tn)
                    sy.dma_start(out=w1x[wpar], in_=w1d.ap()[widx]
                                 ).then_inc(w1_sem.h, 16)
                axv2 = w1_sem.inc(16)
                axv3 = ax3.n
                wqv3 = wq.n
                w3_swdge = (w3d is not None) and (gi % 2 == 1)
                if w3d is not None and not w3_swdge:
                    @blk.scalar
                    def _(s_, par=par, gi=gi, widx=widx):
                        if gi >= 2:
                            s_.wait_ge(tn.h, wdma[gi - 2])
                        else:
                            s_.wait_ge(tn.h, k._moe_pre_tn)
                        s_.dma_start(out=w3x[par], in_=w3d.ap()[widx]
                                     ).then_inc(ax3.h, 16)
                    axv3 = ax3.inc(16)
                elif w3_swdge:
                    @blk.gpsimd
                    def _(gp, par=par, gi=gi, widx=widx):
                        if gi >= 2:
                            gp.wait_ge(tn.h, wdma[gi - 2])
                        else:
                            gp.wait_ge(tn.h, k._moe_pre_tn)
                        gp.dma_start(out=w3b[par], in_=w3d.ap()[widx]
                                     ).then_inc(wq.h, 16)
                    wqv3 = wq.inc(16)

                if layer == 0:
                    @blk.scalar
                    def _(s_, par=par, wpar=wpar, axv2=axv2, axv3=axv3,
                          w1_sem=w1_sem, w3_swdge=w3_swdge):
                        s_.wait_ge(w1_sem.h, axv2)
                        mm = s_.activation(out=w1b[wpar], in_=w1x[wpar],
                                           func=AF.Copy, scale=1.0)
                        if w3d is not None and not w3_swdge:
                            s_.wait_ge(ax3.h, axv3)
                            mm = s_.activation(out=w3b[par], in_=w3x[par],
                                               func=AF.Copy, scale=1.0)
                        mm.then_inc(sc.h, 1)
                    cast_ev = sc.inc(1)
                    cast_sem = sc
                else:
                    @blk.vector
                    def _(v, wpar=wpar, axv2=axv2, gi=gi, w1_sem=w1_sem):
                        v.wait_ge(w1_sem.h, axv2)
                        if gi >= W1DEP:
                            v.wait_ge(tn.h, wdma[gi - W1DEP])
                        v.tensor_copy(out=w1b[wpar],
                                      in_=w1x[wpar]).then_inc(vc.h, 1)
                    cast_ev = vc.inc(1)
                    cast_sem = vc

                @blk.tensor
                def _(te, e=e, ft=ft, par=par, wpar=wpar, gi=gi, cast_ev=cast_ev,
                      cast_sem=cast_sem, v_wb=v_wb, Hf=Hf, w3_swdge=w3_swdge,
                      wqv3=wqv3):
                    te.wait_ge(cast_sem.h, cast_ev)
                    if Hf == 0 and ft == 0:
                        te.wait_ge(vc.h, v_wb)
                    if gi >= 1:
                        te.wait_ge(sc.h, h_ev[(gi - 1, "s")])
                    for c in range(NCH):
                        lhs = w1b[wpar][:, c * 128:(c + 1) * 128]
                        for n_ in range(4):
                            pp = k.P[0] if n_ < 2 else k.P[1]
                            mm = te.matmul(out=pp[:, (n_ % 2) * 512:(n_ % 2 + 1) * 512],
                                           lhsT=lhs,
                                           rhs=xT[:, c * S + n_ * 512:c * S + (n_ + 1) * 512],
                                           start=(c == 0), stop=(c == NCH - 1))
                    mm.then_inc(tn.h, 1)
                    if w3d is not None:
                        if w3_swdge:
                            te.wait_ge(wq.h, wqv3)
                        if gi >= 1:
                            te.wait_ge(vc.h, h_ev[(gi - 1, "v")])
                        for c in range(NCH):
                            lhs = w3b[par][:, c * 128:(c + 1) * 128]
                            for n_ in range(4):
                                pp = k.P[2] if n_ < 2 else k.P[3]
                                mm = te.matmul(out=pp[:, (n_ % 2) * 512:(n_ % 2 + 1) * 512],
                                               lhsT=lhs,
                                               rhs=xT[:, c * S + n_ * 512:c * S + (n_ + 1) * 512],
                                               start=(c == 0), stop=(c == NCH - 1))
                        mm.then_inc(tn.h, 1)
                wdma[gi] = tn.inc(2 if w3d is not None else 1)
                tn_h1 = wdma[gi] - (1 if w3d is not None else 0)

                if layer == 0:
                    @blk.scalar
                    def _(s_, ft=ft, gi=gi, tn_h1=tn_h1):
                        s_.wait_ge(tn.h, tn_h1)
                        if gi >= 1:
                            s_.wait_ge(vc.h, h_ev[(gi - 1, "v")])
                        s_.activation(out=s1[:, 0:1024], in_=k.P[0][:, :], func=AF.Silu)
                        s_.activation(out=s1[:, 1024:2048], in_=k.P[1][:, :],
                                      func=AF.Silu).then_inc(sc.h, 1)
                    h_ev[(gi, "s")] = sc.inc(1)

                    @blk.vector
                    def _(v, ft=ft, gi=gi):
                        v.wait_ge(sc.h, h_ev[(gi, "s")])
                        v.wait_ge(tn.h, wdma[gi])
                        v.tensor_tensor(out=slab[:, ft * S:ft * S + 1024],
                                        in0=s1[:, 0:1024], in1=k.P[2][:, :], op=ALU.mult)
                        v.tensor_tensor(out=slab[:, ft * S + 1024:ft * S + 2048],
                                        in0=s1[:, 1024:2048], in1=k.P[3][:, :],
                                        op=ALU.mult).then_inc(vc.h, 1)
                    h_ev[(gi, "v")] = vc.inc(1)
                else:
                    @blk.scalar
                    def _(s_, e=e, fb=fb, ft=ft, gi=gi, tn_h1=tn_h1):
                        s_.wait_ge(tn.h, tn_h1)
                        if gi >= 1:
                            s_.wait_ge(sc.h, h_ev[(gi - 1, "s")])
                        bia = b1c[:, e * 32 + (fb // 128) + ft:e * 32 + (fb // 128) + ft + 1]
                        s_.activation(out=slab[:, ft * S:ft * S + 1024], in_=k.P[0][:, :],
                                      func=AF.Relu, bias=bia)
                        s_.activation(out=slab[:, ft * S + 1024:ft * S + 2048],
                                      in_=k.P[1][:, :], func=AF.Relu,
                                      bias=bia).then_inc(sc.h, 1)
                    h_ev[(gi, "s")] = sc.inc(1)
                    h_ev[(gi, "v")] = vc.n

            h_last_s = h_ev[((e * 2 + Hf) * NFH + NFH - 1, "s")]
            h_last_v = h_ev[((e * 2 + Hf) * NFH + NFH - 1, "v")]

            for dmt in range(NCH):
                par = dmt % 2
                pj = (k.P[0], k.P[1]) if par == 0 else (k.P[2], k.P[3])
                gj = (e * 2 + Hf) * NCH + dmt

                @blk.gpsimd
                def _(gp, e=e, fb=fb, dmt=dmt, gj=gj):
                    if gj >= W2DEP:
                        gp.wait_ge(tn.h, k._moe_out_mm[gj - W2DEP])
                    else:
                        gp.wait_ge(tn.h, k._moe_pre_tn)
                        gp.wait_ge(vc.h, k._moe_pre_vc)
                    gp.dma_start(out=w2b[gj % W2DEP],
                                 in_=w2d.ap()[(e * 2 + fb // 2048) * NCH + dmt]
                                 ).then_inc(wq.h, 16)
                wq.inc(16)
                wqv = wq.n
                if not hasattr(k, "_moe_out_mm"):
                    k._moe_out_mm = {}

                @blk.tensor
                def _(te, dmt=dmt, par=par, gj=gj, pj=pj, wqv=wqv,
                      h_last_s=h_last_s, h_last_v=h_last_v):
                    te.wait_ge(wq.h, wqv)
                    te.wait_ge(sc.h, h_last_s)
                    te.wait_ge(vc.h, h_last_v)
                    if gj >= 2:
                        te.wait_ge(vc.h, k._moe_out_ev[gj - 2])
                    for ftt in range(16):
                        lhs = w2b[gj % W2DEP][:, ftt * 128:(ftt + 1) * 128]
                        for n_ in range(4):
                            pp = pj[0] if n_ < 2 else pj[1]
                            mm = te.matmul(out=pp[:, (n_ % 2) * 512:(n_ % 2 + 1) * 512],
                                           lhsT=lhs,
                                           rhs=slab[:, ftt * S + n_ * 512:ftt * S + (n_ + 1) * 512],
                                           start=(ftt == 0), stop=(ftt == 15))
                    mm.then_inc(tn.h, 1)
                k._moe_out_mm[gj] = tn.inc(1)
                if not hasattr(k, "_moe_out_ev"):
                    k._moe_out_ev = {}

                @blk.vector
                def _(v, e=e, dmt=dmt, gj=gj, pj=pj, v_wb=v_wb, wb=wb,
                      first=(e == 0 and Hf == 0)):
                    v.wait_ge(tn.h, k._moe_out_mm[gj])
                    v.wait_ge(vc.h, v_wb)
                    dst = accb[dmt] if first else tsb
                    if layer == 0:
                        v.tensor_tensor(out=dst[:, 0:1024], in0=pj[0][:, :],
                                        in1=wb[:, 0:1024], op=ALU.mult)
                        mm = v.tensor_tensor(out=dst[:, 1024:2048], in0=pj[1][:, :],
                                             in1=wb[:, 1024:2048], op=ALU.mult)
                    else:
                        bia = b2c[:, e * 8 + dmt:e * 8 + dmt + 1]
                        v.scalar_tensor_tensor(out=dst[:, 0:1024], in0=pj[0][:, :],
                                               scalar=bia, in1=wb[:, 0:1024],
                                               op0=ALU.add, op1=ALU.mult)
                        mm = v.scalar_tensor_tensor(out=dst[:, 1024:2048],
                                                    in0=pj[1][:, :],
                                                    scalar=bia, in1=wb[:, 1024:2048],
                                                    op0=ALU.add, op1=ALU.mult)
                    if not first:
                        v.tensor_tensor(out=accb[dmt][:, 0:1024],
                                        in0=accb[dmt][:, 0:1024],
                                        in1=tsb[:, 0:1024], op=ALU.add)
                        mm = v.tensor_tensor(out=accb[dmt][:, 1024:2048],
                                             in0=accb[dmt][:, 1024:2048],
                                             in1=tsb[:, 1024:2048], op=ALU.add)
                    mm.then_inc(vc.h, 1)
                k._moe_out_ev[gj] = vc.inc(1)
        if not hasattr(k, "_moe_lastev"):
            k._moe_lastev = {}
            k._moe_lastout = {}
        k._moe_lastev[e] = vc.n
        k._moe_lastout[e] = vc.n

    vlast = vc.n

    @blk.gpsimd
    def _(gp, vlast=vlast):
        gp.wait_ge(vc.h, vlast)
        for dmt in range(8):
            gp.dma_start(out=acc_dram.ap()[dmt, :, :], in_=accb[dmt]
                         ).then_inc(dt_s.h, 16)
    dt_s.inc(128)
    k.moe_done = dict(dt=dt_s.h and dt_s.n, v=vc.n, s=sc.n, t=tn.n)
    for nm in ["_moe_out_mm", "_moe_out_ev", "_moe_acc_t", "_moe_lastev", "_moe_lastout", "_moe_vwb"]:
        if hasattr(k, nm):
            delattr(k, nm)


def emit_attn_l0(k):
    nc, blk = k.nc, k.block
    ld, wq, tn, vc, sc, gp_s, dt_s = (k.sems[n] for n in
                                      ["ld", "wq", "tn", "vc", "sc", "gp", "dt"])
    ax_s = k.sems["ax"]
    xT = k.V(OFF_BIGB, NCH * S)
    comb = k.V(OFF_BIGA, 16384)
    qT = k.V(OFF_PAIR, 2048)
    kT = k.V(OFF_PAIR + 4096, 2048)
    vaug = k.V(OFF_PAIR + 8192, 2080)
    sgm1 = k.V(OFF_WTEB, 2048)
    sgm2 = k.V(OFF_WTEB + 4096, 2048)
    attnA = k.V(OFF_ATTN, 2048)
    attnB = k.V(OFF_ATTN + 4096, 2048)
    attTA = k.V(OFF_ATTNT, 2048)
    attTB = k.V(OFF_ATTNT + 4096, 2048)
    relbA = k.V(OFF_RELB, 2048)
    relbB = k.V(OFF_RELB + 4096, 2048)
    slabA = k.V(OFF_SLAB, 2176)
    slabB = k.V(OFF_SLAB + 4352, 2176)
    slabA1 = k.V(OFF_FSCR + 2048, 2176)
    slabB1 = k.V(OFF_FSCR + 6400, 2176)
    wqb = [k.V(OFF_WS + i * 2048, 1024) for i in range(2)]
    wkb = [k.V(OFF_WS + 4096 + i * 2048, 1024) for i in range(2)]
    wvb = [k.V(OFF_WS + 8192 + i * 2048, 1024) for i in range(2)]
    rbA = k.smb[:, 1152:1280]
    rbB = k.smb[:, 1280:1408]
    czb = k.smb[:, 1408:1416]
    rdA = k.smb[0:1, 0:512]
    rdB = k.smb[0:1, 512:1024]
    rr = k.fscr[:, 0:512]
    nmc = lambda i: k.sm[:, 704 + i:705 + i]
    bqc = k.sm[:, 680:688]
    bkc = k.sm[:, 688:696]
    gate_c = k.sm[:, 712:713]
    omg_c = k.sm[:, 714:715]

    dep = getattr(k, "moe_done", None)

    @blk.gpsimd
    def _(gp):
        gp.dma_start(out=bqc, in_=k.di["l0_bqc"].ap()).then_inc(ld.h, 16)
        gp.dma_start(out=bkc, in_=k.di["l0_bkc"].ap()).then_inc(ld.h, 16)
        gp.dma_start(out=k.relextT[0:64, :],
                     in_=k.di["l0_relextT"].ap()).then_inc(ld.h, 16)
        gp.dma_start(out=k.relextT[64:128, :],
                     in_=k.di["l0_relextT"].ap()).then_inc(ld.h, 16)
        gp.memset(vaug, 1.0).then_inc(gp_s.h, 1)
    ld.inc(64)
    k.relext_ld = ld.n
    v_vaug_init = gp_s.inc(1)
    bvbv = bcast_row(k, k.di["l0_bv"].ap().unsqueeze(0), k.bca[0], 1024, row_off=0)
    bobv = bcast_row(k, k.di["l0_bo"].ap().unsqueeze(0), k.bca[1], 1024, row_off=1024)

    @blk.scalar
    def _(s_):
        s_.wait_ge(ld.h, k.v_consts_ld)
        s_.activation(out=k.sm[0:1, 10:11], in_=k.sm[0:1, 8:9], func=AF.Sigmoid
                      ).then_inc(sc.h, 1)
    s_gate = sc.inc(1)

    @blk.vector
    def _(v, s_gate=s_gate, bobv=bobv):
        v.wait_ge(sc.h, s_gate)
        v.wait_ge(vc.h, bobv)
        v.tensor_copy(out=k.smb[0:1, 1416:1417], in_=k.sm[0:1, 10:11])
        v.tensor_copy(out=czb, in_=k.cz).then_inc(vc.h, 1)
    v_gtrow = vc.inc(1)

    @blk.tensor
    def _(te, v_gtrow=v_gtrow):
        te.wait_ge(vc.h, v_gtrow)
        te.matmul(out=k.P[3][:, 0:1], lhsT=k.ones_row, rhs=k.smb[0:1, 1416:1417],
                  start=True, stop=True).then_inc(tn.h, 1)
    t_gb = tn.inc(1)

    @blk.vector
    def _(v, t_gb=t_gb):
        v.wait_ge(tn.h, t_gb)
        v.tensor_copy(out=gate_c, in_=k.P[3][:, 0:1]).then_inc(vc.h, 1)
    v_g1 = vc.inc(1)

    @blk.gpsimd
    def _(gp, v_g1=v_g1):
        gp.wait_ge(vc.h, v_g1)
        gp.tensor_scalar(out=omg_c, in0=gate_c, scalar1=-1.0, scalar2=1.0,
                         op0=ALU.mult, op1=ALU.add).then_inc(gp_s.h, 1)
    v_gates = gp_s.inc(1)

    proj_ev = {}
    slab_w = {}
    ax2, ax3, ax4 = k.sems["ax2"], k.sems["ax3"], k.sems["ax4"]
    relb_rd = {(0, 0): ax_s.n, (0, 1): ax4.n, (1, 0): ax_s.n, (1, 1): ax4.n}
    pair_done = {}
    projmm = {}

    for pr in range(8):
        par = pr % 2

        @blk.gpsimd
        def _(gp, pr=pr, par=par):
            if pr >= 2:
                gp.wait_ge(tn.h, projmm[pr - 2])
            for wsb, d_ in ((wqb[par], "l0_wqT"), (wkb[par], "l0_wkT"),
                            (wvb[par], "l0_wvT")):
                gp.dma_start(out=wsb.rearrange("p (c j) -> p c j", c=NCH),
                             in_=AP(k.di[d_], pr * 128,
                                    [[D, 128], [128 * D, NCH], [1, 128]])
                             ).then_inc(wq.h, 16)
        wq.inc(48)
        wqv = wq.n

        @blk.tensor
        def _(te, pr=pr, par=par, wqv=wqv):
            te.wait_ge(wq.h, wqv)
            if pr == 0:
                if dep is not None:
                    te.wait_ge(dt_s.h, dep["dt"])
                te.wait_ge(gp_s.h, v_gates)
                te.wait_ge(vc.h, max(bvbv, bobv))
            else:
                te.wait_ge(vc.h, pair_done[pr - 1])
            for c in range(NCH):
                for n_ in range(4):
                    pp = k.P[0] if n_ < 2 else k.P[1]
                    te.matmul(out=pp[:, (n_ % 2) * 512:(n_ % 2 + 1) * 512],
                              lhsT=wqb[par][:, c * 128:(c + 1) * 128],
                              rhs=xT[:, c * S + n_ * 512:c * S + (n_ + 1) * 512],
                              start=(c == 0), stop=(c == NCH - 1))
            for c in range(NCH):
                for n_ in range(4):
                    pp = k.P[2] if n_ < 2 else k.P[3]
                    mm = te.matmul(out=pp[:, (n_ % 2) * 512:(n_ % 2 + 1) * 512],
                                   lhsT=wkb[par][:, c * 128:(c + 1) * 128],
                                   rhs=xT[:, c * S + n_ * 512:c * S + (n_ + 1) * 512],
                                   start=(c == 0), stop=(c == NCH - 1))
            mm.then_inc(tn.h, 1)
        t_qk = tn.inc(1)

        @blk.scalar
        def _(s_, pr=pr, t_qk=t_qk):
            s_.wait_ge(tn.h, t_qk)
            s_.activation(out=qT[:, 0:1024], in_=k.P[0][:, :], func=AF.Identity,
                          bias=bqc[:, pr:pr + 1])
            s_.activation(out=qT[:, 1024:2048], in_=k.P[1][:, :], func=AF.Identity,
                          bias=bqc[:, pr:pr + 1])
            s_.activation(out=kT[:, 0:1024], in_=k.P[2][:, :], func=AF.Identity,
                          bias=bkc[:, pr:pr + 1])
            s_.activation(out=kT[:, 1024:2048], in_=k.P[3][:, :], func=AF.Identity,
                          bias=bkc[:, pr:pr + 1]).then_inc(sc.h, 1)
        proj_ev[pr] = sc.inc(1)

        v_base = vc.n

        @blk.tensor
        def _(te, pr=pr, par=par, v_base=v_base):
            te.wait_ge(sc.h, proj_ev[pr])
            if pr == 0:
                te.wait_ge(gp_s.h, v_vaug_init)
            for kt in range(NT):
                if kt >= 2:
                    te.wait_ge(vc.h, v_base + kt - 1)
                for c in range(NCH):
                    mm = te.matmul(out=k.P[0][:, (kt % 2) * 512:(kt % 2) * 512 + 128],
                                   lhsT=xT[:, c * S + kt * 128:c * S + (kt + 1) * 128],
                                   rhs=wvb[par][:, c * 128:(c + 1) * 128],
                                   start=(c == 0), stop=(c == NCH - 1))
                mm.then_inc(tn.h, 1)
        tn.inc(NT)
        t_vbase = tn.n - NT

        @blk.vector
        def _(v, pr=pr, t_vbase=t_vbase):
            for kt in range(NT):
                v.wait_ge(tn.h, t_vbase + kt + 1)
                dst = vaug[:, kt * 130:(kt + 1) * 130].rearrange(
                    "p (b j) -> p b j", b=2)[:, :, 0:64]
                src = k.P[0][:, (kt % 2) * 512:(kt % 2) * 512 + 128].rearrange(
                    "p (b j) -> p b j", b=2)
                bvs = k.bca[0][:, pr * 128:(pr + 1) * 128].rearrange(
                    "p (b j) -> p b j", b=2)
                v.tensor_tensor(out=dst, in0=src, in1=bvs, op=ALU.add).then_inc(vc.h, 1)
                vc.inc(1)
        v_vaug = vc.n

        bcolsA = k.fscr[:, 2688:2704]
        bcolsB = k.fscr[:, 2704:2720]

        @blk.vector
        def _(v, v_vaug=v_vaug):
            v.wait_ge(vc.h, v_vaug)
            v.tensor_tensor(out=attnA, in0=qT, in1=qT, op=ALU.mult)
            v.tensor_tensor(out=attnB, in0=kT, in1=kT, op=ALU.mult).then_inc(vc.h, 1)
        v_sq = vc.inc(1)

        @blk.tensor
        def _(te, v_sq=v_sq):
            te.wait_ge(vc.h, v_sq)
            onesA = k.ones_all[0:64, 0:1]
            onesB = k.ones_all[64:128, 0:1]
            for j in range(4):
                pq = (k.P[0], k.P[0], k.P[1], k.P[1])[j]
                cs_ = slice((j % 2) * 512, (j % 2) * 512 + 512)
                ts_ = slice(j * 512, (j + 1) * 512)
                te.matmul(out=pq[0:1, cs_], lhsT=onesA, rhs=attnA[0:64, ts_],
                          start=True, stop=True)
                pq2 = (k.P[2], k.P[2], k.P[3], k.P[3])[j]
                te.matmul(out=pq2[0:1, cs_], lhsT=onesB, rhs=attnA[64:128, ts_],
                          start=True, stop=True, tile_position=(64, 0))
                te.matmul(out=pq[64:65, cs_], lhsT=onesA, rhs=attnB[0:64, ts_],
                          start=True, stop=True, tile_position=(0, 64))
                mm = te.matmul(out=pq2[64:65, cs_], lhsT=onesB, rhs=attnB[64:128, ts_],
                               start=True, stop=True, tile_position=(64, 64))
            mm.then_inc(tn.h, 1)
        t_nrm = tn.inc(1)

        @blk.scalar
        def _(s_, t_nrm=t_nrm):
            s_.wait_ge(tn.h, t_nrm)
            s_.activation(out=attTA[0:1, 0:1024], in_=k.P[0][0:1, 0:1024], func=AF.Sqrt)
            s_.activation(out=attTA[0:1, 1024:2048], in_=k.P[1][0:1, 0:1024], func=AF.Sqrt)
            s_.activation(out=attTB[0:1, 0:1024], in_=k.P[2][0:1, 0:1024], func=AF.Sqrt)
            s_.activation(out=attTB[0:1, 1024:2048], in_=k.P[3][0:1, 0:1024],
                          func=AF.Sqrt).then_inc(sc.h, 1)
        s_nrm = sc.inc(1)

        @blk.vector
        def _(v, s_nrm=s_nrm, selfw=vc.n + 1):
            v.wait_ge(sc.h, s_nrm)
            v.reduce_max(out=k.sm[64:65, 704:705], in_=k.P[0][64:65, 0:1024], axis=AX.X)
            v.reduce_max(out=k.sm[64:65, 705:706], in_=k.P[1][64:65, 0:1024],
                         axis=AX.X).then_inc(vc.h, 1)
            v.wait_ge(vc.h, selfw)
            v.tensor_scalar(out=k.sm[64:65, 704:705], in0=k.sm[64:65, 704:705],
                            scalar1=k.sm[64:65, 705:706], scalar2=None, op0=ALU.max)
            v.reduce_max(out=k.sm[64:65, 706:707], in_=k.P[2][64:65, 0:1024], axis=AX.X)
            v.reduce_max(out=k.sm[64:65, 707:708], in_=k.P[3][64:65, 0:1024],
                         axis=AX.X).then_inc(vc.h, 1)
            v.wait_ge(vc.h, selfw + 1)
            v.tensor_scalar(out=k.sm[64:65, 706:707], in0=k.sm[64:65, 706:707],
                            scalar1=k.sm[64:65, 707:708], scalar2=None, op0=ALU.max
                            ).then_inc(vc.h, 1)
            v.tensor_copy(out=k.smb[64:65, 1676:1677], in_=k.sm[64:65, 704:705])
            v.tensor_copy(out=k.smb[64:65, 1677:1678],
                          in_=k.sm[64:65, 706:707]).then_inc(vc.h, 1)
        v_kmax = vc.inc(4)

        @blk.tensor
        def _(te, v_kmax=v_kmax):
            te.wait_ge(vc.h, v_kmax)
            te.matmul(out=k.P[0][:, 0:1], lhsT=k.ones_all[64:65, 0:128],
                      rhs=k.smb[64:65, 1676:1677], start=True, stop=True,
                      tile_position=(64, 0))
            te.matmul(out=k.P[2][:, 0:1], lhsT=k.ones_all[64:65, 0:128],
                      rhs=k.smb[64:65, 1677:1678], start=True, stop=True,
                      tile_position=(64, 0))
            for Q in range(NT):
                te.matmul(out=k.P[1][:, Q:Q + 1],
                          lhsT=attTA[0:1, Q * 128:(Q + 1) * 128],
                          rhs=k.ones_all[0:1, 0:1], start=True, stop=True)
                mm = te.matmul(out=k.P[3][:, Q:Q + 1],
                               lhsT=attTB[0:1, Q * 128:(Q + 1) * 128],
                               rhs=k.ones_all[0:1, 0:1], start=True, stop=True)
            mm.then_inc(tn.h, 1)
        t_bc = tn.inc(1)

        @blk.scalar
        def _(s_, t_bc=t_bc):
            s_.wait_ge(tn.h, t_bc)
            s_.activation(out=k.sm[:, 708:709], in_=k.P[0][:, 0:1], func=AF.Sqrt)
            s_.activation(out=k.sm[:, 709:710], in_=k.P[2][:, 0:1],
                          func=AF.Sqrt).then_inc(sc.h, 1)
        s_mk = sc.inc(1)

        @blk.vector
        def _(v, s_mk=s_mk, selfw=vc.n + 1):
            v.wait_ge(sc.h, s_mk)
            v.tensor_scalar(out=bcolsA, in0=k.P[1][:, 0:16],
                            scalar1=k.sm[:, 708:709], scalar2=-ALPHA,
                            op0=ALU.mult, op1=ALU.mult)
            v.tensor_scalar(out=bcolsB, in0=k.P[3][:, 0:16],
                            scalar1=k.sm[:, 709:710], scalar2=-ALPHA,
                            op0=ALU.mult, op1=ALU.mult).then_inc(vc.h, 1)
            v.wait_ge(vc.h, selfw)
            v.tensor_scalar(out=bcolsA, in0=bcolsA, scalar1=30.0, scalar2=None,
                            op0=ALU.add)
            v.tensor_scalar(out=bcolsB, in0=bcolsB, scalar1=30.0, scalar2=None,
                            op0=ALU.add).then_inc(vc.h, 1)
        b_done = vc.inc(2)

        ext_ev = {}
        sx1_ev = {}
        slbs = ((slabA, slabA1), (slabB, slabB1))
        CH = ((0, k.P[0], k.P[1], (0, 0)), (1, k.P[2], k.P[3], (64, 0)))
        for Q in range(NT):
            w0 = 1920 - 128 * Q
            t_ext1x = {}
            t_ext2x = {}
            for h01, pp0, pp1, tp in CH:
                @blk.tensor
                def _(te, pr=pr, Q=Q, w0=w0, h01=h01, pp0=pp0, pp1=pp1, tp=tp,
                      b_done=b_done):
                    if Q == 0:
                        te.wait_ge(vc.h, b_done)
                        if pr == 0 and h01 == 0:
                            te.wait_ge(ld.h, k.relext_ld)
                        if pr >= 2:
                            te.wait_ge(ax_s.h if h01 == 0 else ax4.h,
                                       relb_rd[(pr - 2, h01)])
                    else:
                        te.wait_ge(sc.h, ext_ev[(h01, Q - 1)])
                    lhs = qT[h01 * 64:h01 * 64 + 64, Q * 128:(Q + 1) * 128]
                    rel = k.relextT[h01 * 64:h01 * 64 + 64, :]
                    for j in range(4):
                        out = (pp0 if j < 2 else pp1)[:, (j % 2) * 512:(j % 2) * 512 + 512]
                        mm = te.matmul(out=out, lhsT=lhs,
                                       rhs=rel[:, w0 + j * 512:w0 + (j + 1) * 512],
                                       start=True, stop=True, tile_position=tp)
                    mm.then_inc(tn.h, 1)
                t_ext1x[h01] = tn.inc(1)

            for h01, pp0, pp1, tp in CH:
                @blk.scalar
                def _(s_, pr=pr, Q=Q, h01=h01, pp0=pp0, pp1=pp1,
                      slb=slbs[h01][Q % 2], t_ext1=t_ext1x[h01]):
                    s_.wait_ge(tn.h, t_ext1)
                    if Q >= 2:
                        s_.wait_ge(ax2.h if h01 == 0 else ax3.h,
                                   slab_w[(pr, h01, Q - 2)])
                    s_.activation(out=slb[:, 0:1024], in_=pp0[:, :], func=AF.Exp,
                                  scale=ALPHA)
                    s_.activation(out=slb[:, 1024:2048], in_=pp1[:, :],
                                  func=AF.Exp, scale=ALPHA).then_inc(sc.h, 1)
                sx1_ev[(h01, Q)] = sc.inc(1)

            for h01, pp0, pp1, tp in CH:
                @blk.tensor
                def _(te, Q=Q, w0=w0, h01=h01, pp0=pp0, tp=tp,
                      sx=sx1_ev[(h01, Q)]):
                    te.wait_ge(sc.h, sx)
                    lhs = qT[h01 * 64:h01 * 64 + 64, Q * 128:(Q + 1) * 128]
                    rel = k.relextT[h01 * 64:h01 * 64 + 64, :]
                    te.matmul(out=pp0[:, 0:127], lhsT=lhs,
                              rhs=rel[:, w0 + 2048:w0 + 2175],
                              start=True, stop=True,
                              tile_position=tp).then_inc(tn.h, 1)
                t_ext2x[h01] = tn.inc(1)

            for h01, pp0, pp1, tp in CH:
                @blk.scalar
                def _(s_, pr=pr, Q=Q, h01=h01, pp0=pp0, slb=slbs[h01][Q % 2],
                      t_ext2=t_ext2x[h01]):
                    s_.wait_ge(tn.h, t_ext2)
                    s_.activation(out=slb[:, 2048:2175], in_=pp0[:, 0:127],
                                  func=AF.Exp, scale=ALPHA).then_inc(sc.h, 1)
                ext_ev[(h01, Q)] = sc.inc(1)
                if h01 == 1:
                    @blk.scalar
                    def _(s_, pr=pr, Q=Q, slb=slbs[1][Q % 2], ev=ext_ev[(1, Q)]):
                        s_.wait_ge(sc.h, ev)
                        s_.dma_start(out=k.sc_slab.ap()[pr % 2, 1, Q, :, 0:2175],
                                     in_=slb[:, 0:2175]).then_inc(ax3.h, 16)
                    slab_w[(pr, 1, Q)] = ax3.inc(16)
                else:
                    @blk.sync
                    def _(sy, pr=pr, Q=Q, slb=slbs[0][Q % 2], ev=ext_ev[(0, Q)]):
                        sy.wait_ge(sc.h, ev)
                        sy.dma_start(out=k.sc_slab.ap()[pr % 2, 0, Q, :, 0:2175],
                                     in_=slb[:, 0:2175]).then_inc(ax2.h, 16)
                    slab_w[(pr, 0, Q)] = ax2.inc(16)

        relused = {}
        ctx_ev = {}
        rd_ev = {}
        for Q in range(NT):
            @blk.gpsimd
            def _(gp, pr=pr, Q=Q):
                gp.wait_ge(ax2.h, slab_w[(pr, 0, Q)])
                if Q >= 1:
                    gp.wait_ge(vc.h, relused[(0, Q - 1)])
                base = ((pr % 2) * 2 + 0) * NT * 128 * 2176 + Q * 128 * 2176
                gp.dma_start(out=relbA,
                             in_=AP(k.sc_slab, base + 127,
                                    [[2175, 128], [1, 2048]])).then_inc(ax_s.h, 16)
            rd_ev[(0, Q)] = ax_s.inc(16)

            eng_b = blk.sync if Q % 2 == 0 else blk.scalar

            @eng_b
            def _(sy, pr=pr, Q=Q):
                sy.wait_ge(ax3.h, slab_w[(pr, 1, Q)])
                if Q >= 1:
                    sy.wait_ge(vc.h, relused[(1, Q - 1)])
                base = ((pr % 2) * 2 + 1) * NT * 128 * 2176 + Q * 128 * 2176
                sy.dma_start(out=relbB,
                             in_=AP(k.sc_slab, base + 127,
                                    [[2175, 128], [1, 2048]])).then_inc(ax4.h, 16)
            rd_ev[(1, Q)] = ax4.inc(16)

            t_scx = {}
            g_nmx = {}
            s_expx = {}
            t_trx = {}
            v_attTx = {}
            t_ctxx = {}
            v_rrx = {}
            t_rbx = {}
            attnXs = (attnA, attnB)
            relbXs = (relbA, relbB)
            attTXs = (attTA, attTB)
            for h01, pp0, pp1, tp in CH:
                @blk.tensor
                def _(te, pr=pr, Q=Q, h01=h01, pp0=pp0, pp1=pp1, tp=tp):
                    if Q == 0:
                        te.wait_ge(sc.h, ext_ev[(h01, NT - 1)])
                    else:
                        te.wait_ge(vc.h, ctx_ev[(h01, Q - 1)])
                    lhs = qT[h01 * 64:h01 * 64 + 64, Q * 128:(Q + 1) * 128]
                    for n_ in range(4):
                        pp = pp0 if n_ < 2 else pp1
                        mm = te.matmul(out=pp[:, (n_ % 2) * 512:(n_ % 2 + 1) * 512],
                                       lhsT=lhs,
                                       rhs=kT[h01 * 64:h01 * 64 + 64,
                                              n_ * 512:(n_ + 1) * 512],
                                       start=True, stop=True, tile_position=tp)
                    mm.then_inc(tn.h, 1)
                t_scx[h01] = tn.inc(1)

            for h01, pp0, pp1, tp in CH:
                @blk.scalar
                def _(s_, Q=Q, h01=h01, pp0=pp0, pp1=pp1, attnX=attnXs[h01],
                      bcol=(bcolsA if h01 == 0 else bcolsB)[:, Q:Q + 1],
                      t_sc=t_scx[h01]):
                    s_.wait_ge(tn.h, t_sc)
                    s_.activation(out=attnX[:, 0:1024], in_=pp0[:, :], func=AF.Exp,
                                  scale=ALPHA, bias=bcol)
                    s_.activation(out=attnX[:, 1024:2048], in_=pp1[:, :],
                                  func=AF.Exp, scale=ALPHA,
                                  bias=bcol).then_inc(sc.h, 1)
                s_expx[h01] = sc.inc(1)

            for h01, pp0, pp1, tp in CH:
                @blk.vector
                def _(v, Q=Q, h01=h01, attnX=attnXs[h01], relbX=relbXs[h01],
                      s_exp=s_expx[h01]):
                    v.wait_ge(sc.h, s_exp)
                    v.wait_ge(ax_s.h if h01 == 0 else ax4.h, rd_ev[(h01, Q)])
                    v.tensor_mul(out=attnX[:, 0:1024], in0=attnX[:, 0:1024],
                                 in1=relbX[:, 0:1024])
                    v.tensor_mul(out=attnX[:, 1024:2048], in0=attnX[:, 1024:2048],
                                 in1=relbX[:, 1024:2048]).then_inc(vc.h, 1)
                relused[(h01, Q)] = vc.inc(1)

            for h01, pp0, pp1, tp in CH:
                @blk.tensor
                def _(te, h01=h01, attnX=attnXs[h01], v_rel=relused[(h01, Q)]):
                    te.wait_ge(vc.h, v_rel)
                    te.wait_ge(gp_s.h, k.v_consts_g)
                    pb = k.Pb[0] if h01 == 0 else k.Pb[2]
                    for kt in range(NT):
                        mm = te.transpose(out=pb[:, kt * 128:(kt + 1) * 128],
                                          in_=attnX[:, kt * 128:(kt + 1) * 128],
                                          identity=k.eye)
                    mm.then_inc(tn.h, 1)
                t_trx[h01] = tn.inc(1)

            for h01, pp0, pp1, tp in CH:
                @blk.vector
                def _(v, h01=h01, attTX=attTXs[h01], t_tr=t_trx[h01]):
                    v.wait_ge(tn.h, t_tr)
                    pb = k.Pb[0] if h01 == 0 else k.Pb[2]
                    v.tensor_copy(out=attTX, in_=pb[:, 0:2048]).then_inc(vc.h, 1)
                v_attTx[h01] = vc.inc(1)

            for h01, pp0, pp1, tp in CH:
                @blk.tensor
                def _(te, h01=h01, attTX=attTXs[h01], v_attT=v_attTx[h01]):
                    te.wait_ge(vc.h, v_attT)
                    if h01 == 0:
                        for kt in range(NT):
                            mm = te.matmul(out=k.P[1][0:65, 0:128],
                                           lhsT=vaug[:, kt * 130:kt * 130 + 65],
                                           rhs=attTX[:, kt * 128:(kt + 1) * 128],
                                           start=(kt == 0), stop=(kt == NT - 1))
                    else:
                        for kt in range(NT):
                            te.matmul(out=k.P[3][64:128, 0:128],
                                      lhsT=vaug[:, kt * 130 + 65:kt * 130 + 129],
                                      rhs=attTX[:, kt * 128:(kt + 1) * 128],
                                      start=(kt == 0), stop=(kt == NT - 1),
                                      tile_position=(0, 64))
                        for kt in range(NT):
                            mm = te.matmul(out=k.P[2][0:1, 0:128],
                                           lhsT=vaug[:, kt * 130 + 129:kt * 130 + 130],
                                           rhs=attTX[:, kt * 128:(kt + 1) * 128],
                                           start=(kt == 0), stop=(kt == NT - 1))
                    mm.then_inc(tn.h, 1)
                t_ctxx[h01] = tn.inc(1)

            for h01, pp0, pp1, tp in CH:
                @blk.vector
                def _(v, h01=h01, t_ctx=t_ctxx[h01], selfw=vc.n + 1):
                    v.wait_ge(tn.h, t_ctx)
                    if h01 == 0:
                        v.reciprocal(out=rr[64:65, 0:128],
                                     in_=k.P[1][64:65, 0:128]).then_inc(vc.h, 1)
                        v.wait_ge(vc.h, selfw)
                        v.tensor_copy(out=k.smb[64:65, 1416:1544],
                                      in_=rr[64:65, 0:128]).then_inc(vc.h, 1)
                    else:
                        v.reciprocal(out=rr[0:1, 128:256],
                                     in_=k.P[2][0:1, 0:128]).then_inc(vc.h, 1)
                        v.wait_ge(vc.h, selfw)
                        v.tensor_copy(out=k.smb[0:1, 1544:1672],
                                      in_=rr[0:1, 128:256]).then_inc(vc.h, 1)
                v_rrx[h01] = vc.inc(2)

            for h01, pp0, pp1, tp in CH:
                @blk.tensor
                def _(te, h01=h01, v_rr=v_rrx[h01]):
                    te.wait_ge(vc.h, v_rr)
                    if h01 == 0:
                        te.matmul(out=k.P[0][0:64, 0:128],
                                  lhsT=k.ones_all[64:65, 0:64],
                                  rhs=k.smb[64:65, 1416:1544], start=True,
                                  stop=True).then_inc(tn.h, 1)
                    else:
                        te.matmul(out=k.P[2][64:128, 0:128],
                                  lhsT=k.ones_all[0:1, 0:64],
                                  rhs=k.smb[0:1, 1544:1672], start=True, stop=True,
                                  tile_position=(0, 64)).then_inc(tn.h, 1)
                t_rbx[h01] = tn.inc(1)

            for h01, pp0, pp1, tp in CH:
                @blk.vector
                def _(v, pr=pr, Q=Q, h01=h01, t_rb=t_rbx[h01]):
                    v.wait_ge(tn.h, t_rb)
                    cs = slice(pr * 2048 + Q * 128, pr * 2048 + (Q + 1) * 128)
                    if h01 == 0:
                        v.tensor_copy(out=rbA[0:64, :], in_=k.P[0][0:64, 0:128])
                        v.tensor_tensor(out=comb[0:64, cs],
                                        in0=k.P[1][0:64, 0:128], in1=rbA[0:64, :],
                                        op=ALU.mult).then_inc(vc.h, 1)
                    else:
                        v.tensor_copy(out=rbB[64:128, :], in_=k.P[2][64:128, 0:128])
                        v.tensor_tensor(out=comb[64:128, cs],
                                        in0=k.P[3][64:128, 0:128],
                                        in1=rbB[64:128, :],
                                        op=ALU.mult).then_inc(vc.h, 1)
                ctx_ev[(h01, Q)] = vc.inc(1)

        relb_rd[(pr, 0)] = rd_ev[(0, NT - 1)]
        relb_rd[(pr, 1)] = rd_ev[(1, NT - 1)]

        @blk.vector
        def _(v, pr=pr, last_ctx=ctx_ev[(1, NT - 1)]):
            v.wait_ge(vc.h, last_ctx)
            v.tensor_scalar_min(out=sgm1, in0=qT, scalar1=0.0).then_inc(vc.h, 1)
        v_sg1 = vc.inc(1)

        @blk.scalar
        def _(s_, v_sg1=v_sg1):
            s_.wait_ge(vc.h, v_sg1)
            s_.activation(out=sgm2, in_=sgm1, func=AF.Exp).then_inc(sc.h, 1)
        s_sg = sc.inc(1)

        @blk.vector
        def _(v, s_sg=s_sg):
            v.wait_ge(sc.h, s_sg)
            v.scalar_tensor_tensor(out=sgm1, in0=qT, scalar=0.0, in1=sgm2,
                                   op0=ALU.max, op1=ALU.add).then_inc(vc.h, 1)
        v_sig = vc.inc(1)

        gchunk_ev = {}
        for n_ in range(4):
            @blk.tensor
            def _(te, pr=pr, n_=n_, v_sig=v_sig):
                if n_ == 0:
                    te.wait_ge(vc.h, v_sig)
                else:
                    te.wait_ge(vc.h, gchunk_ev[n_ - 1])
                te.matmul(out=k.P[0][0:64, 0:512],
                          lhsT=k.cM[0:64, pr * 64:(pr + 1) * 64],
                          rhs=sgm1[0:64, n_ * 512:(n_ + 1) * 512], start=True,
                          stop=True, tile_position=(0, 0))
                te.matmul(out=k.P[2][64:128, 0:512],
                          lhsT=k.cM[64:128, pr * 64:(pr + 1) * 64],
                          rhs=sgm1[64:128, n_ * 512:(n_ + 1) * 512], start=True,
                          stop=True, tile_position=(64, 64))
                te.matmul(out=k.P[1][0:1, 0:512], lhsT=czb[0:64, pr:pr + 1],
                          rhs=sgm1[0:64, n_ * 512:(n_ + 1) * 512], start=True,
                          stop=True, tile_position=(0, 0))
                mm = te.matmul(out=k.P[3][0:1, 0:512], lhsT=czb[64:128, pr:pr + 1],
                               rhs=sgm1[64:128, n_ * 512:(n_ + 1) * 512], start=True,
                               stop=True, tile_position=(64, 0))
                mm.then_inc(tn.h, 1)
            t_g = tn.inc(1)

            @blk.vector
            def _(v, pr=pr, n_=n_, t_g=t_g):
                v.wait_ge(tn.h, t_g)
                v.tensor_scalar(out=rr[0:1, 0:512], in0=k.P[1][0:1, 0:512],
                                scalar1=1e-6, scalar2=None, op0=ALU.add)
                v.tensor_scalar(out=rr[64:65, 0:512], in0=k.P[3][0:1, 0:512],
                                scalar1=1e-6, scalar2=None, op0=ALU.add)
                v.reciprocal(out=rr[0:1, 0:512], in_=rr[0:1, 0:512])
                v.reciprocal(out=rr[64:65, 0:512], in_=rr[64:65, 0:512])
                v.tensor_copy(out=rdA, in_=rr[0:1, 0:512])
                mm = v.tensor_copy(out=k.smb[64:65, 512:1024], in_=rr[64:65, 0:512])
                mm.then_inc(vc.h, 1)
            v_rd = vc.inc(1)

            @blk.tensor
            def _(te, v_rd=v_rd):
                te.wait_ge(vc.h, v_rd)
                te.matmul(out=k.P[1][0:64, 0:512], lhsT=k.ones_all[0:1, 0:64],
                          rhs=rdA, start=True, stop=True)
                te.matmul(out=k.P[3][64:128, 0:512], lhsT=k.ones_all[64:65, 0:64],
                          rhs=k.smb[64:65, 512:1024], start=True, stop=True,
                          tile_position=(64, 64)).then_inc(tn.h, 1)
            t_rdb = tn.inc(1)

            @blk.vector
            def _(v, pr=pr, n_=n_, t_rdb=t_rdb):
                v.wait_ge(tn.h, t_rdb)
                v.tensor_copy(out=attTA[0:64, 0:512], in_=k.P[1][0:64, 0:512])
                v.tensor_copy(out=attTA[64:128, 0:512], in_=k.P[3][64:128, 0:512])
                v.scalar_tensor_tensor(out=attTB[0:64, 0:512], in0=k.P[0][0:64, 0:512],
                                       scalar=omg_c[0:64, :], in1=attTA[0:64, 0:512],
                                       op0=ALU.mult, op1=ALU.mult)
                v.scalar_tensor_tensor(out=attTB[64:128, 0:512],
                                       in0=k.P[2][64:128, 0:512],
                                       scalar=omg_c[64:128, :], in1=attTA[64:128, 0:512],
                                       op0=ALU.mult, op1=ALU.mult)
                cs = comb[:, pr * 2048 + n_ * 512:pr * 2048 + (n_ + 1) * 512]
                mm = v.scalar_tensor_tensor(out=cs, in0=cs, scalar=gate_c,
                                            in1=attTB[:, 0:512],
                                            op0=ALU.mult, op1=ALU.add)
                mm.then_inc(vc.h, 1)
            gchunk_ev[n_] = vc.inc(1)
        pair_done[pr] = gchunk_ev[3]
        projmm[pr] = tn.n

    @blk.gpsimd
    def _(gp):
        gp.wait_ge(tn.h, projmm[7])
        gp.dma_start(out=k.V(OFF_WS, 8192).rearrange("p (c j) -> p c j", c=NCH),
                     in_=k.di["l0_woT"].ap().rearrange("(c p) j -> p c j", p=128)
                     ).then_inc(wq.h, 16)
    wo_ld = wq.inc(16)
    woT_sb = k.V(OFF_WS, 8192)
    st_ev = {}
    vo_ev = {}
    for st in range(NT):
        pj = (k.P[0], k.P[1]) if st % 2 == 0 else (k.P[2], k.P[3])
        ob = k.smb[:, 0:1024] if st % 2 == 0 else k.smb[:, 1024:2048]

        @blk.tensor
        def _(te, st=st, pj=pj, wo_ld=wo_ld):
            if st == 0:
                te.wait_ge(wq.h, wo_ld)
                te.wait_ge(vc.h, pair_done[7])
            if st >= 2:
                te.wait_ge(vc.h, vo_ev[st - 2])
            for c in range(NCH):
                for n_ in range(2):
                    mm = te.matmul(
                        out=pj[n_][:, 0:512],
                        lhsT=comb[:, c * 2048 + st * 128:c * 2048 + (st + 1) * 128],
                        rhs=woT_sb[:, c * 1024 + n_ * 512:c * 1024 + (n_ + 1) * 512],
                        start=(c == 0), stop=(c == NCH - 1))
            mm.then_inc(tn.h, 1)
        t_o = tn.inc(1)

        @blk.vector
        def _(v, st=st, pj=pj, ob=ob, t_o=t_o):
            v.wait_ge(tn.h, t_o)
            if st >= 2:
                v.wait_ge(dt_s.h, st_ev[st - 2])
            v.tensor_tensor(out=ob[:, 0:512], in0=pj[0][:, 0:512],
                            in1=k.bca[1][:, 0:512], op=ALU.add)
            mm = v.tensor_tensor(out=ob[:, 512:1024], in0=pj[1][:, 0:512],
                                 in1=k.bca[1][:, 512:1024], op=ALU.add)
            mm.then_inc(vc.h, 1)
        vo_ev[st] = vc.inc(1)

        @blk.gpsimd
        def _(gp, st=st, ob=ob):
            gp.wait_ge(vc.h, vo_ev[st])
            gp.dma_start(out=k.sc_satt.ap()[st * 128:(st + 1) * 128, :],
                         in_=ob).then_inc(dt_s.h, 16)
        st_ev[st] = dt_s.inc(16)
    k.attn0_done = dict(dt=dt_s.n, v=vc.n, s=sc.n, t=tn.n,
                        ax2=k.sems["ax2"].n, ax3=k.sems["ax3"].n)


def emit_combine_l0(k):
    nc, blk = k.nc, k.block
    ld, tn, vc, sc, gp_s, dt_s = (k.sems[n] for n in
                                  ["ld", "tn", "vc", "sc", "gp", "dt"])
    stage = k.fscr[:, 0:1024]
    stage2 = k.fscr[:, 1024:2048]
    satt_b = [k.V(OFF_ATTN, 1024), k.V(OFF_ATTN + 2048, 1024)]
    accp_b = [k.V(OFF_ATTNT, 1024), k.V(OFF_ATTNT + 2048, 1024)]
    g_b = [k.V(OFF_RELB, 1024), k.V(OFF_RELB + 2048, 1024)]
    adep = getattr(k, "attn0_done", None)
    ev_v = {}
    ev_s = {}
    x1_w = {}

    for t in range(NT):
        par = t % 2
        xt = stage if par == 0 else stage2

        @blk.gpsimd
        def _(gp, t=t, par=par, xt=xt):
            if t == 0 and adep is not None:
                gp.wait_ge(dt_s.h, adep["dt"])
                gp.wait_ge(k.sems["ax2"].h, adep["ax2"])
                gp.wait_ge(k.sems["ax3"].h, adep["ax3"])
            if t >= 2:
                gp.wait_ge(vc.h, ev_v[t - 2])
                gp.wait_ge(dt_s.h, x1_w[t - 2])
            gp.dma_start(out=xt, in_=k.di["x"].ap()[t * 128:(t + 1) * 128, :]
                         ).then_inc(ld.h, 16)
            gp.dma_start(out=satt_b[par], in_=k.sc_satt.ap()[t * 128:(t + 1) * 128, :]
                         ).then_inc(ld.h, 16)
            gp.dma_start(out=accp_b[par].rearrange("p (c j) -> p c j", c=NCH),
                         in_=k.sc_acc0.ap()[:, :, t * 128:(t + 1) * 128]
                         .transpose([1, 0, 2])).then_inc(ld.h, 16)
        ldv = ld.inc(48)

        @blk.tensor
        def _(te, t=t, par=par, ldv=ldv):
            te.wait_ge(ld.h, ldv)
            te.wait_ge(gp_s.h, k.v_consts_g)
            if t >= 2:
                te.wait_ge(sc.h, ev_s[t - 2])
            for c in range(NCH):
                mm = te.transpose(out=k.Pb[2 * par][:, c * 128:(c + 1) * 128],
                                  in_=accp_b[par][:, c * 128:(c + 1) * 128],
                                  identity=k.eye)
            mm.then_inc(tn.h, 1)
        t_tr = tn.inc(1)

        @blk.scalar
        def _(s_, t=t, par=par, t_tr=t_tr):
            s_.wait_ge(tn.h, t_tr)
            s_.activation(out=g_b[par], in_=k.Pb[2 * par][:, 0:1024],
                          func=AF.Silu).then_inc(sc.h, 1)
        ev_s[t] = sc.inc(1)

        @blk.vector
        def _(v, t=t, par=par, xt=xt):
            v.wait_ge(sc.h, ev_s[t])
            v.tensor_mul(out=g_b[par], in0=satt_b[par], in1=g_b[par])
            mm = v.tensor_add(out=xt, in0=xt, in1=g_b[par])
            mm.then_inc(vc.h, 1)
        ev_v[t] = vc.inc(1)

        @blk.gpsimd
        def _(gp, t=t, xt=xt):
            gp.wait_ge(vc.h, ev_v[t])
            gp.dma_start(out=k.sc_x1.ap()[t * 128:(t + 1) * 128, :], in_=xt
                         ).then_inc(dt_s.h, 16)
        x1_w[t] = dt_s.inc(16)

    xn1T = k.V(OFF_BIGB, 16384)
    tok = k.V(OFF_BIGA, 16384)
    emit_ln(k, src_ap=k.sc_x1.ap(), dst_tok=tok, dst_xnT=xn1T,
            dst_tok_dram=k.sc_xn1.ap(), pre_wait=[("dt", dt_s.n)])
    k.comb0_done = dict(v=vc.n, s=sc.n, t=tn.n, dt=k.sems["dt"].n)



def emit_lin_attn_l1(k):
    nc, blk = k.nc, k.block
    ld, wq, tn, vc, sc, gp_s, dt_s = (k.sems[n] for n in
                                      ["ld", "wq", "tn", "vc", "sc", "gp", "dt"])
    xT = k.V(OFF_BIGB, NCH * S)
    QN = k.V(OFF_BIGA, 16384)
    KN = k.V(OFF_BIGA + 32768, 16384)
    wsb = k.V(OFF_WS, 8192)
    AT_v = ([k.V(OFF_ATTN + i * 2048, 1024) for i in range(4)]
            + [k.V(OFF_ATTNT + i * 2048, 1024) for i in range(4)])
    q2b = [k.V(OFF_ATTN, 1024), k.V(OFF_ATTN + 2048, 1024)]
    invb = [k.V(OFF_ATTNT, 1024), k.V(OFF_ATTNT + 2048, 1024)]
    tailb = k.V(OFF_WTEB, 1024)
    disc = k.V(OFF_RELB, 1024)
    xn1b = [k.V(OFF_RELB + 2048, 1024), k.V(OFF_RELB + 4096, 1024)]
    sl1b = [k.V(OFF_SLAB, 1024), k.V(OFF_SLAB + 2048, 1024)]
    bvc = k.sm[:, 680:688]
    rsk_c = lambda t: k.sm[:, 720 + t:721 + t]
    vs_c = lambda t: k.sm[:, 736 + t:737 + t]
    gm_c = k.sm[:, 13:14]
    rskb = k.smb[:, 1024:1040]
    dep = getattr(k, "comb0_done", None)

    bqv = bcast_row(k, k.di["l1_bq"].ap().unsqueeze(0), k.bca[0], 1024, row_off=0)
    bkv = bcast_row(k, k.di["l1_bk"].ap().unsqueeze(0), k.bca[1], 1024, row_off=1024)
    bvv = bcast_row(k, k.di["l1_bv"].ap().unsqueeze(0), k.bca[2], 1024, row_off=0)

    @blk.gpsimd
    def _(gp):
        gp.dma_start(out=bvc, in_=k.di["l1_bvc"].ap()).then_inc(ld.h, 16)
    ld.inc(16)

    @blk.vector
    def _(v, bvv=bvv):
        v.wait_ge(vc.h, bvv)
        v.wait_ge(ld.h, k.v_consts_ld)
        v.tensor_copy(out=k.smb[0:1, 1040:1041], in_=k.sm[0:1, 9:10]).then_inc(vc.h, 1)
    v_gr = vc.inc(1)

    @blk.tensor
    def _(te, v_gr=v_gr):
        te.wait_ge(vc.h, v_gr)
        te.matmul(out=k.P[3][:, 0:1], lhsT=k.ones_row, rhs=k.smb[0:1, 1040:1041],
                  start=True, stop=True).then_inc(tn.h, 1)
    t_gm = tn.inc(1)

    @blk.vector
    def _(v, t_gm=t_gm):
        v.wait_ge(tn.h, t_gm)
        v.tensor_copy(out=gm_c, in_=k.P[3][:, 0:1]).then_inc(vc.h, 1)
    v_gm = vc.inc(1)

    def load_w(dname, after_tn):
        @blk.gpsimd
        def _(gp, dname=dname, after_tn=after_tn):
            gp.wait_ge(tn.h, after_tn)
            gp.dma_start(out=wsb.rearrange("p (c j) -> p c j", c=NCH),
                         in_=k.di[dname].ap().rearrange("(c p) j -> p c j", p=128)
                         ).then_inc(wq.h, 16)
        return wq.inc(16)

    wq_ld = load_w("l1_wqT", 0)
    ev = {}

    def proj_tok(dst, bcab, wld, first_wait=None, accum_vs=False):
        entry_vc = vc.n
        for st in range(NT):
            pj = k.P[0] if st % 2 == 0 else k.P[1]

            @blk.tensor
            def _(te, st=st, pj=pj, wld=wld, first_wait=first_wait, entry_vc=entry_vc):
                te.wait_ge(wq.h, wld)
                if st == 0 and first_wait:
                    for snm, sval in first_wait:
                        te.wait_ge(k.sems[snm].h, sval)
                if st < 2:
                    te.wait_ge(vc.h, entry_vc)
                else:
                    te.wait_ge(vc.h, ev[st - 2])
                for c in range(NCH):
                    for n_ in range(2):
                        mm = te.matmul(
                            out=pj[:, n_ * 512:(n_ + 1) * 512],
                            lhsT=xT[:, c * S + st * 128:c * S + (st + 1) * 128],
                            rhs=wsb[:, c * 1024 + n_ * 512:c * 1024 + (n_ + 1) * 512],
                            start=(c == 0), stop=(c == NCH - 1))
                mm.then_inc(tn.h, 1)
            t_p = tn.inc(1)

            @blk.vector
            def _(v, st=st, pj=pj, t_p=t_p, dst=dst, bcab=bcab, accum_vs=accum_vs):
                v.wait_ge(tn.h, t_p)
                if accum_vs:
                    mm = v.scalar_tensor_tensor(out=disc, in0=pj[:, :], scalar=0.0,
                                                in1=bcab, op0=ALU.add, op1=ALU.add,
                                                accum_out=vs_c(st))
                else:
                    mm = v.tensor_add(out=dst[:, st * 1024:(st + 1) * 1024],
                                      in0=pj[:, :], in1=bcab)
                mm.then_inc(vc.h, 1)
            ev[st] = vc.inc(1)
        return tn.n, vc.n

    fw = [("vc", max(bqv, v_gm))]
    if dep is not None:
        fw.append(("dt", dep["dt"]))
    tq, vq = proj_tok(QN, k.bca[0], wq_ld, first_wait=fw)
    wk_ld = load_w("l1_wkT", tq)
    tk, vk = proj_tok(KN, k.bca[1], wk_ld, first_wait=[("vc", bkv)])
    wv_ld = load_w("l1_wvT", tk)
    tv, vv_ = proj_tok(None, k.bca[2], wv_ld, first_wait=[("vc", bvv)], accum_vs=True)

    def colnorm(src, invbuf, sqpar, t_prev):
        t0v = tn.n
        for st in range(NT):
            @blk.scalar
            def _(s_, st=st, src=src, sqpar=sqpar, t_prev=t_prev):
                if st == 0:
                    s_.wait_ge(vc.h, vc.n)
                if st >= 2:
                    s_.wait_ge(tn.h, k._l1_sqmm[st - 2])
                s_.activation(out=q2b[st % 2], in_=src[:, st * 1024:(st + 1) * 1024],
                              func=AF.Square).then_inc(sc.h, 1)
            s_q2 = sc.inc(1)
            k._l1_sqmm = getattr(k, "_l1_sqmm", {})

            @blk.tensor
            def _(te, st=st, s_q2=s_q2):
                te.wait_ge(sc.h, s_q2)
                te.matmul(out=k.P[2][0:1, 0:512], lhsT=k.ones_col,
                          rhs=q2b[st % 2][:, 0:512], start=(st == 0),
                          stop=(st == NT - 1))
                te.matmul(out=k.P[2][0:1, 512:1024], lhsT=k.ones_col,
                          rhs=q2b[st % 2][:, 512:1024], start=(st == 0),
                          stop=(st == NT - 1)).then_inc(tn.h, 1)
            k._l1_sqmm[st] = tn.inc(1)
        t_sq = tn.n

        @blk.scalar
        def _(s_, t_sq=t_sq):
            s_.wait_ge(tn.h, t_sq)
            s_.activation(out=k.fscr[0:1, 0:1024], in_=k.P[2][0:1, 0:1024],
                          func=AF.Sqrt).then_inc(sc.h, 1)
        s_sq = sc.inc(1)

        @blk.vector
        def _(v, s_sq=s_sq):
            v.wait_ge(sc.h, s_sq)
            v.reciprocal(out=k.fscr[0:1, 1024:2048], in_=k.fscr[0:1, 0:1024])
            mm = v.tensor_copy(out=k.smb[0:1, 0:1024], in_=k.fscr[0:1, 1024:2048])
            mm.then_inc(vc.h, 1)
        v_iv = vc.inc(1)

        @blk.tensor
        def _(te, v_iv=v_iv):
            te.wait_ge(vc.h, v_iv)
            te.matmul(out=k.P[2][:, 0:512], lhsT=k.ones_row, rhs=k.smb[0:1, 0:512],
                      start=True, stop=True)
            te.matmul(out=k.P[2][:, 512:1024], lhsT=k.ones_row,
                      rhs=k.smb[0:1, 512:1024], start=True, stop=True).then_inc(tn.h, 1)
        t_iv = tn.inc(1)

        @blk.vector
        def _(v, t_iv=t_iv, invbuf=invbuf):
            v.wait_ge(tn.h, t_iv)
            v.tensor_copy(out=invbuf, in_=k.P[2][:, 0:1024]).then_inc(vc.h, 1)
        v_ib = vc.inc(1)

        for st in range(NT):
            @blk.vector
            def _(v, st=st, src=src, invbuf=invbuf, v_ib=v_ib):
                if st == 0:
                    v.wait_ge(vc.h, v_ib)
                mm = v.tensor_mul(out=src[:, st * 1024:(st + 1) * 1024],
                                  in0=src[:, st * 1024:(st + 1) * 1024], in1=invbuf)
                mm.then_inc(vc.h, 1)
            vc.inc(1)
        return vc.n

    vqn = colnorm(QN, invb[0], 0, tq)
    vkn = colnorm(KN, invb[1], 1, tk)

    for st in range(NT):
        @blk.vector
        def _(v, st=st, vkn=vkn):
            if st == 0:
                v.wait_ge(vc.h, vkn)
            mm = v.tensor_scalar(out=disc, in0=KN[:, st * 1024:(st + 1) * 1024],
                                 scalar1=1.0, scalar2=1e-6, op0=ALU.mult,
                                 op1=ALU.add, accum_out=rsk_c(st))
            mm.then_inc(vc.h, 1)
        vc.inc(1)

    @blk.vector
    def _(v):
        v.wait_ge(vc.h, vc.n)
        v.tensor_copy(out=rskb, in_=k.sm[:, 720:736]).then_inc(vc.h, 1)
    v_rskb = vc.inc(1)

    @blk.tensor
    def _(te, v_rskb=v_rskb):
        te.wait_ge(vc.h, v_rskb)
        for st in range(NT):
            for n_ in range(2):
                mm = te.matmul(out=k.P[2][0:1, n_ * 512:(n_ + 1) * 512],
                               lhsT=rskb[:, st:st + 1],
                               rhs=QN[:, st * 1024 + n_ * 512:st * 1024 + (n_ + 1) * 512],
                               start=(st == 0), stop=(st == NT - 1))
        mm.then_inc(tn.h, 1)
    t_tp = tn.inc(1)

    @blk.scalar
    def _(s_, t_tp=t_tp):
        s_.wait_ge(tn.h, t_tp)
        s_.activation(out=k.fscr[0:1, 0:1024], in_=k.P[2][0:1, 0:1024],
                      func=AF.Identity, bias=k.sm[0:1, 14:15]).then_inc(sc.h, 1)
    s_tl = sc.inc(1)

    @blk.vector
    def _(v, s_tl=s_tl):
        v.wait_ge(sc.h, s_tl)
        v.reciprocal(out=k.fscr[0:1, 1024:2048], in_=k.fscr[0:1, 0:1024])
        mm = v.tensor_copy(out=k.smb[0:1, 0:1024], in_=k.fscr[0:1, 1024:2048])
        mm.then_inc(vc.h, 1)
    v_tl = vc.inc(1)

    @blk.tensor
    def _(te, v_tl=v_tl):
        te.wait_ge(vc.h, v_tl)
        te.matmul(out=k.P[2][:, 0:512], lhsT=k.ones_row, rhs=k.smb[0:1, 0:512],
                  start=True, stop=True)
        te.matmul(out=k.P[2][:, 512:1024], lhsT=k.ones_row,
                  rhs=k.smb[0:1, 512:1024], start=True, stop=True).then_inc(tn.h, 1)
    t_tlb = tn.inc(1)

    @blk.vector
    def _(v, t_tlb=t_tlb):
        v.wait_ge(tn.h, t_tlb)
        v.tensor_copy(out=tailb, in_=k.P[2][:, 0:1024]).then_inc(vc.h, 1)
    v_tlb = vc.inc(1)

    at_ev = {}
    for dt_ in range(NCH):
        pj = k.P[0] if dt_ % 2 == 0 else k.P[1]

        @blk.tensor
        def _(te, dt_=dt_, pj=pj, v_tlb=v_tlb):
            if dt_ == 0:
                te.wait_ge(vc.h, v_tlb)
            if dt_ >= 2:
                te.wait_ge(sc.h, at_ev[dt_ - 2])
            for st in range(NT):
                for n_ in range(2):
                    mm = te.matmul(
                        out=pj[:, n_ * 512:(n_ + 1) * 512],
                        lhsT=KN[:, st * 1024 + dt_ * 128:st * 1024 + dt_ * 128 + 128],
                        rhs=QN[:, st * 1024 + n_ * 512:st * 1024 + (n_ + 1) * 512],
                        start=(st == 0), stop=(st == NT - 1))
            mm.then_inc(tn.h, 1)
        t_at = tn.inc(1)

        @blk.scalar
        def _(s_, dt_=dt_, pj=pj, t_at=t_at):
            s_.wait_ge(tn.h, t_at)
            s_.activation(out=AT_v[dt_], in_=pj[:, :],
                          func=AF.Identity).then_inc(sc.h, 1)
        at_ev[dt_] = sc.inc(1)

    VT = QN
    vt_ev = {}
    for dt_ in range(NCH):
        pj2 = (k.P[0], k.P[1]) if dt_ % 2 == 0 else (k.P[2], k.P[3])

        @blk.tensor
        def _(te, dt_=dt_, pj2=pj2):
            te.wait_ge(sc.h, at_ev[NCH - 1])
            if dt_ >= 2:
                te.wait_ge(sc.h, vt_ev[dt_ - 2])
            for c in range(NCH):
                for n_ in range(4):
                    pp = pj2[0] if n_ < 2 else pj2[1]
                    mm = te.matmul(
                        out=pp[:, (n_ % 2) * 512:(n_ % 2 + 1) * 512],
                        lhsT=wsb[:, c * 1024 + dt_ * 128:c * 1024 + dt_ * 128 + 128],
                        rhs=xT[:, c * S + n_ * 512:c * S + (n_ + 1) * 512],
                        start=(c == 0), stop=(c == NCH - 1))
            mm.then_inc(tn.h, 1)
        t_vt = tn.inc(1)

        @blk.scalar
        def _(s_, dt_=dt_, pj2=pj2, t_vt=t_vt):
            s_.wait_ge(tn.h, t_vt)
            s_.activation(out=VT[:, dt_ * 2048:dt_ * 2048 + 1024], in_=pj2[0][:, :],
                          func=AF.Identity, bias=bvc[:, dt_:dt_ + 1])
            s_.activation(out=VT[:, dt_ * 2048 + 1024:dt_ * 2048 + 2048],
                          in_=pj2[1][:, :], func=AF.Identity,
                          bias=bvc[:, dt_:dt_ + 1]).then_inc(sc.h, 1)
        vt_ev[dt_] = sc.inc(1)

    for ct in range(NT):
        pj = k.P[0] if ct % 2 == 0 else k.P[1]
        par = ct % 2

        @blk.gpsimd
        def _(gp, ct=ct, par=par):
            if ct >= 2:
                gp.wait_ge(vc.h, k._l1_slev[ct - 2])
            gp.dma_start(out=xn1b[par], in_=k.sc_xn1.ap()[ct * 128:(ct + 1) * 128, :]
                         ).then_inc(ld.h, 16)
        ld_x = ld.inc(16)
        k._l1_slev = getattr(k, "_l1_slev", {})

        @blk.tensor
        def _(te, ct=ct, pj=pj, ld_x=ld_x):
            if ct == 0:
                te.wait_ge(sc.h, vt_ev[NCH - 1])
                te.wait_ge(sc.h, vt_ev[NCH - 2])
            if ct >= 2:
                te.wait_ge(vc.h, k._l1_slev[ct - 2])
            for dc in range(NCH):
                for n_ in range(2):
                    mm = te.matmul(
                        out=pj[:, n_ * 512:(n_ + 1) * 512],
                        lhsT=VT[:, dc * 2048 + ct * 128:dc * 2048 + ct * 128 + 128],
                        rhs=AT_v[dc][:, n_ * 512:(n_ + 1) * 512],
                        start=(dc == 0), stop=(dc == NCH - 1))
            mm.then_inc(tn.h, 1)
        t_tm = tn.inc(1)

        @blk.vector
        def _(v, ct=ct, pj=pj, par=par, t_tm=t_tm, ld_x=ld_x):
            v.wait_ge(tn.h, t_tm)
            v.wait_ge(ld.h, ld_x)
            if ct >= 2:
                v.wait_ge(dt_s.h, k._l1_stw[ct - 2])
            v.scalar_tensor_tensor(out=disc, in0=pj[:, :], scalar=vs_c(ct),
                                   in1=tailb, op0=ALU.add, op1=ALU.mult)
            mm = v.scalar_tensor_tensor(out=sl1b[par], in0=disc, scalar=gm_c,
                                        in1=xn1b[par], op0=ALU.mult, op1=ALU.add)
            mm.then_inc(vc.h, 1)
        v_sl = vc.inc(1)
        k._l1_slev[ct] = v_sl

        @blk.gpsimd
        def _(gp, ct=ct, par=par, v_sl=v_sl):
            gp.wait_ge(vc.h, v_sl)
            gp.dma_start(out=k.sc_sl1.ap()[ct * 128:(ct + 1) * 128, :],
                         in_=sl1b[par]).then_inc(dt_s.h, 16)
        k._l1_stw = getattr(k, "_l1_stw", {})
        k._l1_stw[ct] = dt_s.inc(16)
    for nm in ["_l1_sqmm", "_l1_slev", "_l1_stw"]:
        if hasattr(k, nm):
            delattr(k, nm)
    k.lin1_done = dict(v=vc.n, s=sc.n, t=tn.n, dt=dt_s.n)



def emit_ffn1_attn(k):
    nc, blk = k.nc, k.block
    ld, wq, tn, vc, sc, gp_s, dt_s = (k.sems[n] for n in
                                      ["ld", "wq", "tn", "vc", "sc", "gp", "dt"])
    ax_s = k.sems["ax"]

    g1v = bcast_row(k, k.di["f1_ln1_g"].ap().unsqueeze(0), k.bca[2], 1024, row_off=0)
    b1v = bcast_row(k, k.di["f1_ln1_b"].ap().unsqueeze(0), k.bca[3], 1024, row_off=1024)
    hnT = k.V(OFF_BIGB, NCH * S)
    tok = k.V(OFF_BIGA, 16384)
    dep = getattr(k, "lin1_done", None)
    pw = [("vc", max(g1v, b1v))]
    if dep is not None:
        pw.append(("dt", dep["dt"]))
    emit_ln(k, src_ap=k.sc_xn1.ap(), dst_tok=tok, dst_xnT=hnT,
            g_row=k.bca[2], b_row=k.bca[3], pre_wait=pw)

    xT = hnT
    comb = k.V(OFF_BIGA, 16384)
    qT = k.V(OFF_PAIR, 2048)
    kT = k.V(OFF_PAIR + 4096, 2048)
    vaug = k.V(OFF_PAIR + 8192, 2080)
    attnA = k.V(OFF_ATTN, 2048)
    attnB = k.V(OFF_ATTN + 4096, 2048)
    attTA = k.V(OFF_ATTNT, 2048)
    attTB = k.V(OFF_ATTNT + 4096, 2048)
    wqb = [k.V(OFF_WS + i * 2048, 1024) for i in range(2)]
    wkb = [k.V(OFF_WS + 4096 + i * 2048, 1024) for i in range(2)]
    wvb = [k.V(OFF_WS + 8192 + i * 2048, 1024) for i in range(2)]
    rbA = k.smb[:, 1152:1280]
    rbB = k.smb[:, 1280:1408]
    rr = k.fscr[:, 0:512]
    nmc = lambda i: k.sm[:, 704 + i:705 + i]
    bqc = k.sm[:, 680:688]
    bkc = k.sm[:, 688:696]

    @blk.gpsimd
    def _(gp):
        gp.dma_start(out=bqc, in_=k.di["f1_bqc"].ap()).then_inc(ld.h, 16)
        gp.dma_start(out=bkc, in_=k.di["f1_bkc"].ap()).then_inc(ld.h, 16)
        gp.memset(vaug, 1.0).then_inc(gp_s.h, 1)
    ld.inc(32)
    v_vaug_init = gp_s.inc(1)
    bvv = bcast_row(k, k.di["f1_bv"].ap().unsqueeze(0), k.bca[0], 1024, row_off=0)
    bov = bcast_row(k, k.di["f1_bo"].ap().unsqueeze(0), k.bca[1], 1024, row_off=1024)

    proj_ev = {}
    pair_done = {}
    projmm = {}

    for pr in range(8):
        par = pr % 2

        @blk.gpsimd
        def _(gp, pr=pr, par=par):
            if pr >= 2:
                gp.wait_ge(tn.h, projmm[pr - 2])
            for wsb, d_ in ((wqb[par], "f1_wq"), (wkb[par], "f1_wk"),
                            (wvb[par], "f1_wv")):
                gp.dma_start(out=wsb.rearrange("p (c j) -> p c j", c=NCH),
                             in_=AP(k.di[d_], pr * 128,
                                    [[D, 128], [128 * D, NCH], [1, 128]])
                             ).then_inc(wq.h, 16)
        wq.inc(48)
        wqv = wq.n

        @blk.tensor
        def _(te, pr=pr, par=par, wqv=wqv):
            te.wait_ge(wq.h, wqv)
            if pr == 0:
                te.wait_ge(vc.h, max(bvv, bov))
                te.wait_ge(gp_s.h, v_vaug_init)
                te.wait_ge(k.sems["vc"].h, k.ln_done["v"])
                te.wait_ge(k.sems["sc"].h, k.ln_done["s"])
            else:
                te.wait_ge(vc.h, pair_done[pr - 1])
            for c in range(NCH):
                for n_ in range(4):
                    pp = k.P[0] if n_ < 2 else k.P[1]
                    te.matmul(out=pp[:, (n_ % 2) * 512:(n_ % 2 + 1) * 512],
                              lhsT=wqb[par][:, c * 128:(c + 1) * 128],
                              rhs=xT[:, c * S + n_ * 512:c * S + (n_ + 1) * 512],
                              start=(c == 0), stop=(c == NCH - 1))
            for c in range(NCH):
                for n_ in range(4):
                    pp = k.P[2] if n_ < 2 else k.P[3]
                    mm = te.matmul(out=pp[:, (n_ % 2) * 512:(n_ % 2 + 1) * 512],
                                   lhsT=wkb[par][:, c * 128:(c + 1) * 128],
                                   rhs=xT[:, c * S + n_ * 512:c * S + (n_ + 1) * 512],
                                   start=(c == 0), stop=(c == NCH - 1))
            mm.then_inc(tn.h, 1)
        t_qk = tn.inc(1)

        @blk.scalar
        def _(s_, pr=pr, t_qk=t_qk):
            s_.wait_ge(tn.h, t_qk)
            s_.activation(out=qT[:, 0:1024], in_=k.P[0][:, :], func=AF.Identity,
                          bias=bqc[:, pr:pr + 1])
            s_.activation(out=qT[:, 1024:2048], in_=k.P[1][:, :], func=AF.Identity,
                          bias=bqc[:, pr:pr + 1])
            s_.activation(out=kT[:, 0:1024], in_=k.P[2][:, :], func=AF.Identity,
                          bias=bkc[:, pr:pr + 1])
            s_.activation(out=kT[:, 1024:2048], in_=k.P[3][:, :], func=AF.Identity,
                          bias=bkc[:, pr:pr + 1]).then_inc(sc.h, 1)
        proj_ev[pr] = sc.inc(1)

        v_base = vc.n

        @blk.tensor
        def _(te, pr=pr, par=par, v_base=v_base):
            te.wait_ge(sc.h, proj_ev[pr])
            for kt in range(NT):
                if kt >= 2:
                    te.wait_ge(vc.h, v_base + kt - 1)
                for c in range(NCH):
                    mm = te.matmul(out=k.P[0][:, (kt % 2) * 512:(kt % 2) * 512 + 128],
                                   lhsT=xT[:, c * S + kt * 128:c * S + (kt + 1) * 128],
                                   rhs=wvb[par][:, c * 128:(c + 1) * 128],
                                   start=(c == 0), stop=(c == NCH - 1))
                mm.then_inc(tn.h, 1)
        tn.inc(NT)
        t_vbase = tn.n - NT

        @blk.vector
        def _(v, pr=pr, t_vbase=t_vbase):
            for kt in range(NT):
                v.wait_ge(tn.h, t_vbase + kt + 1)
                dst = vaug[:, kt * 130:(kt + 1) * 130].rearrange(
                    "p (b j) -> p b j", b=2)[:, :, 0:64]
                src = k.P[0][:, (kt % 2) * 512:(kt % 2) * 512 + 128].rearrange(
                    "p (b j) -> p b j", b=2)
                bvs = k.bca[0][:, pr * 128:(pr + 1) * 128].rearrange(
                    "p (b j) -> p b j", b=2)
                v.tensor_tensor(out=dst, in0=src, in1=bvs, op=ALU.add).then_inc(vc.h, 1)
                vc.inc(1)
        v_vaug = vc.n

        bcolsA = k.fscr[:, 2688:2704]
        bcolsB = k.fscr[:, 2704:2720]

        @blk.vector
        def _(v, v_vaug=v_vaug):
            v.wait_ge(vc.h, v_vaug)
            v.tensor_tensor(out=attnA, in0=qT, in1=qT, op=ALU.mult)
            v.tensor_tensor(out=attnB, in0=kT, in1=kT, op=ALU.mult).then_inc(vc.h, 1)
        v_sq = vc.inc(1)

        @blk.tensor
        def _(te, v_sq=v_sq):
            te.wait_ge(vc.h, v_sq)
            onesA = k.ones_all[0:64, 0:1]
            onesB = k.ones_all[64:128, 0:1]
            for j in range(4):
                pq = (k.P[0], k.P[0], k.P[1], k.P[1])[j]
                cs_ = slice((j % 2) * 512, (j % 2) * 512 + 512)
                ts_ = slice(j * 512, (j + 1) * 512)
                te.matmul(out=pq[0:1, cs_], lhsT=onesA, rhs=attnA[0:64, ts_],
                          start=True, stop=True)
                pq2 = (k.P[2], k.P[2], k.P[3], k.P[3])[j]
                te.matmul(out=pq2[0:1, cs_], lhsT=onesB, rhs=attnA[64:128, ts_],
                          start=True, stop=True, tile_position=(64, 0))
                te.matmul(out=pq[64:65, cs_], lhsT=onesA, rhs=attnB[0:64, ts_],
                          start=True, stop=True, tile_position=(0, 64))
                mm = te.matmul(out=pq2[64:65, cs_], lhsT=onesB, rhs=attnB[64:128, ts_],
                               start=True, stop=True, tile_position=(64, 64))
            mm.then_inc(tn.h, 1)
        t_nrm = tn.inc(1)

        @blk.scalar
        def _(s_, t_nrm=t_nrm):
            s_.wait_ge(tn.h, t_nrm)
            s_.activation(out=attTA[0:1, 0:1024], in_=k.P[0][0:1, 0:1024], func=AF.Sqrt)
            s_.activation(out=attTA[0:1, 1024:2048], in_=k.P[1][0:1, 0:1024], func=AF.Sqrt)
            s_.activation(out=attTB[0:1, 0:1024], in_=k.P[2][0:1, 0:1024], func=AF.Sqrt)
            s_.activation(out=attTB[0:1, 1024:2048], in_=k.P[3][0:1, 0:1024],
                          func=AF.Sqrt).then_inc(sc.h, 1)
        s_nrm = sc.inc(1)

        @blk.vector
        def _(v, s_nrm=s_nrm, selfw=vc.n + 1):
            v.wait_ge(sc.h, s_nrm)
            v.reduce_max(out=k.sm[64:65, 704:705], in_=k.P[0][64:65, 0:1024], axis=AX.X)
            v.reduce_max(out=k.sm[64:65, 705:706], in_=k.P[1][64:65, 0:1024],
                         axis=AX.X).then_inc(vc.h, 1)
            v.wait_ge(vc.h, selfw)
            v.tensor_scalar(out=k.sm[64:65, 704:705], in0=k.sm[64:65, 704:705],
                            scalar1=k.sm[64:65, 705:706], scalar2=None, op0=ALU.max)
            v.reduce_max(out=k.sm[64:65, 706:707], in_=k.P[2][64:65, 0:1024], axis=AX.X)
            v.reduce_max(out=k.sm[64:65, 707:708], in_=k.P[3][64:65, 0:1024],
                         axis=AX.X).then_inc(vc.h, 1)
            v.wait_ge(vc.h, selfw + 1)
            v.tensor_scalar(out=k.sm[64:65, 706:707], in0=k.sm[64:65, 706:707],
                            scalar1=k.sm[64:65, 707:708], scalar2=None, op0=ALU.max
                            ).then_inc(vc.h, 1)
            v.tensor_copy(out=k.smb[64:65, 1676:1677], in_=k.sm[64:65, 704:705])
            v.tensor_copy(out=k.smb[64:65, 1677:1678],
                          in_=k.sm[64:65, 706:707]).then_inc(vc.h, 1)
        v_kmax = vc.inc(4)

        @blk.tensor
        def _(te, v_kmax=v_kmax):
            te.wait_ge(vc.h, v_kmax)
            te.matmul(out=k.P[0][:, 0:1], lhsT=k.ones_all[64:65, 0:128],
                      rhs=k.smb[64:65, 1676:1677], start=True, stop=True,
                      tile_position=(64, 0))
            te.matmul(out=k.P[2][:, 0:1], lhsT=k.ones_all[64:65, 0:128],
                      rhs=k.smb[64:65, 1677:1678], start=True, stop=True,
                      tile_position=(64, 0))
            for Q in range(NT):
                te.matmul(out=k.P[1][:, Q:Q + 1],
                          lhsT=attTA[0:1, Q * 128:(Q + 1) * 128],
                          rhs=k.ones_all[0:1, 0:1], start=True, stop=True)
                mm = te.matmul(out=k.P[3][:, Q:Q + 1],
                               lhsT=attTB[0:1, Q * 128:(Q + 1) * 128],
                               rhs=k.ones_all[0:1, 0:1], start=True, stop=True)
            mm.then_inc(tn.h, 1)
        t_bc = tn.inc(1)

        @blk.scalar
        def _(s_, t_bc=t_bc):
            s_.wait_ge(tn.h, t_bc)
            s_.activation(out=k.sm[:, 708:709], in_=k.P[0][:, 0:1], func=AF.Sqrt)
            s_.activation(out=k.sm[:, 709:710], in_=k.P[2][:, 0:1],
                          func=AF.Sqrt).then_inc(sc.h, 1)
        s_mk = sc.inc(1)

        @blk.vector
        def _(v, s_mk=s_mk, selfw=vc.n + 1):
            v.wait_ge(sc.h, s_mk)
            v.tensor_scalar(out=bcolsA, in0=k.P[1][:, 0:16],
                            scalar1=k.sm[:, 708:709], scalar2=-ALPHA,
                            op0=ALU.mult, op1=ALU.mult)
            v.tensor_scalar(out=bcolsB, in0=k.P[3][:, 0:16],
                            scalar1=k.sm[:, 709:710], scalar2=-ALPHA,
                            op0=ALU.mult, op1=ALU.mult).then_inc(vc.h, 1)
            v.wait_ge(vc.h, selfw)
            v.tensor_scalar(out=bcolsA, in0=bcolsA, scalar1=30.0, scalar2=None,
                            op0=ALU.add)
            v.tensor_scalar(out=bcolsB, in0=bcolsB, scalar1=30.0, scalar2=None,
                            op0=ALU.add).then_inc(vc.h, 1)
        b_done = vc.inc(2)

        ctx_ev = {}
        CH5 = ((0, k.P[0], k.P[1], (0, 0)), (1, k.P[2], k.P[3], (64, 0)))
        attnXs = (attnA, attnB)
        attTXs = (attTA, attTB)
        bcolsXs = (bcolsA, bcolsB)
        for Q in range(NT):
            t_scx = {}
            g_nmx = {}
            s_expx = {}
            t_trx = {}
            v_attTx = {}
            t_ctxx = {}
            v_rrx = {}
            t_rbx = {}
            for h01, pp0, pp1, tp in CH5:
                @blk.tensor
                def _(te, pr=pr, Q=Q, h01=h01, pp0=pp0, pp1=pp1, tp=tp,
                      b_done=b_done):
                    if Q == 0:
                        te.wait_ge(vc.h, b_done)
                    else:
                        te.wait_ge(vc.h, ctx_ev[(h01, Q - 1)])
                    lhs = qT[h01 * 64:h01 * 64 + 64, Q * 128:(Q + 1) * 128]
                    for n_ in range(4):
                        pp = pp0 if n_ < 2 else pp1
                        mm = te.matmul(out=pp[:, (n_ % 2) * 512:(n_ % 2 + 1) * 512],
                                       lhsT=lhs,
                                       rhs=kT[h01 * 64:h01 * 64 + 64,
                                              n_ * 512:(n_ + 1) * 512],
                                       start=True, stop=True, tile_position=tp)
                    mm.then_inc(tn.h, 1)
                t_scx[h01] = tn.inc(1)

            for h01, pp0, pp1, tp in CH5:
                @blk.scalar
                def _(s_, Q=Q, h01=h01, pp0=pp0, pp1=pp1, attnX=attnXs[h01],
                      bcol=bcolsXs[h01][:, Q:Q + 1], t_sc=t_scx[h01]):
                    s_.wait_ge(tn.h, t_sc)
                    s_.activation(out=attnX[:, 0:1024], in_=pp0[:, :], func=AF.Exp,
                                  scale=ALPHA, bias=bcol)
                    s_.activation(out=attnX[:, 1024:2048], in_=pp1[:, :],
                                  func=AF.Exp, scale=ALPHA,
                                  bias=bcol).then_inc(sc.h, 1)
                s_expx[h01] = sc.inc(1)

            for h01, pp0, pp1, tp in CH5:
                @blk.tensor
                def _(te, h01=h01, attnX=attnXs[h01], s_exp=s_expx[h01]):
                    te.wait_ge(sc.h, s_exp)
                    te.wait_ge(gp_s.h, k.v_consts_g)
                    pb = k.Pb[0] if h01 == 0 else k.Pb[2]
                    for kt in range(NT):
                        mm = te.transpose(out=pb[:, kt * 128:(kt + 1) * 128],
                                          in_=attnX[:, kt * 128:(kt + 1) * 128],
                                          identity=k.eye)
                    mm.then_inc(tn.h, 1)
                t_trx[h01] = tn.inc(1)

            for h01, pp0, pp1, tp in CH5:
                @blk.vector
                def _(v, h01=h01, attTX=attTXs[h01], t_tr=t_trx[h01]):
                    v.wait_ge(tn.h, t_tr)
                    pb = k.Pb[0] if h01 == 0 else k.Pb[2]
                    v.tensor_copy(out=attTX, in_=pb[:, 0:2048]).then_inc(vc.h, 1)
                v_attTx[h01] = vc.inc(1)

            for h01, pp0, pp1, tp in CH5:
                @blk.tensor
                def _(te, h01=h01, attTX=attTXs[h01], v_attT=v_attTx[h01]):
                    te.wait_ge(vc.h, v_attT)
                    if h01 == 0:
                        for kt in range(NT):
                            mm = te.matmul(out=k.P[1][0:65, 0:128],
                                           lhsT=vaug[:, kt * 130:kt * 130 + 65],
                                           rhs=attTX[:, kt * 128:(kt + 1) * 128],
                                           start=(kt == 0), stop=(kt == NT - 1))
                    else:
                        for kt in range(NT):
                            te.matmul(out=k.P[3][64:128, 0:128],
                                      lhsT=vaug[:, kt * 130 + 65:kt * 130 + 129],
                                      rhs=attTX[:, kt * 128:(kt + 1) * 128],
                                      start=(kt == 0), stop=(kt == NT - 1),
                                      tile_position=(0, 64))
                        for kt in range(NT):
                            mm = te.matmul(out=k.P[2][0:1, 0:128],
                                           lhsT=vaug[:, kt * 130 + 129:kt * 130 + 130],
                                           rhs=attTX[:, kt * 128:(kt + 1) * 128],
                                           start=(kt == 0), stop=(kt == NT - 1))
                    mm.then_inc(tn.h, 1)
                t_ctxx[h01] = tn.inc(1)

            for h01, pp0, pp1, tp in CH5:
                @blk.vector
                def _(v, h01=h01, t_ctx=t_ctxx[h01], selfw=vc.n + 1):
                    v.wait_ge(tn.h, t_ctx)
                    if h01 == 0:
                        v.reciprocal(out=rr[64:65, 0:128],
                                     in_=k.P[1][64:65, 0:128]).then_inc(vc.h, 1)
                        v.wait_ge(vc.h, selfw)
                        v.tensor_copy(out=k.smb[64:65, 1416:1544],
                                      in_=rr[64:65, 0:128]).then_inc(vc.h, 1)
                    else:
                        v.reciprocal(out=rr[0:1, 128:256],
                                     in_=k.P[2][0:1, 0:128]).then_inc(vc.h, 1)
                        v.wait_ge(vc.h, selfw)
                        v.tensor_copy(out=k.smb[0:1, 1544:1672],
                                      in_=rr[0:1, 128:256]).then_inc(vc.h, 1)
                v_rrx[h01] = vc.inc(2)

            for h01, pp0, pp1, tp in CH5:
                @blk.tensor
                def _(te, h01=h01, v_rr=v_rrx[h01]):
                    te.wait_ge(vc.h, v_rr)
                    if h01 == 0:
                        te.matmul(out=k.P[0][0:64, 0:128],
                                  lhsT=k.ones_all[64:65, 0:64],
                                  rhs=k.smb[64:65, 1416:1544], start=True,
                                  stop=True).then_inc(tn.h, 1)
                    else:
                        te.matmul(out=k.P[2][64:128, 0:128],
                                  lhsT=k.ones_all[0:1, 0:64],
                                  rhs=k.smb[0:1, 1544:1672], start=True, stop=True,
                                  tile_position=(0, 64)).then_inc(tn.h, 1)
                t_rbx[h01] = tn.inc(1)

            for h01, pp0, pp1, tp in CH5:
                @blk.vector
                def _(v, pr=pr, Q=Q, h01=h01, t_rb=t_rbx[h01]):
                    v.wait_ge(tn.h, t_rb)
                    cs = slice(pr * 2048 + Q * 128, pr * 2048 + (Q + 1) * 128)
                    if h01 == 0:
                        v.tensor_copy(out=rbA[0:64, :], in_=k.P[0][0:64, 0:128])
                        v.tensor_tensor(out=comb[0:64, cs],
                                        in0=k.P[1][0:64, 0:128], in1=rbA[0:64, :],
                                        op=ALU.mult).then_inc(vc.h, 1)
                    else:
                        v.tensor_copy(out=rbB[64:128, :], in_=k.P[2][64:128, 0:128])
                        v.tensor_tensor(out=comb[64:128, cs],
                                        in0=k.P[3][64:128, 0:128],
                                        in1=rbB[64:128, :],
                                        op=ALU.mult).then_inc(vc.h, 1)
                ctx_ev[(h01, Q)] = vc.inc(1)
        pair_done[pr] = ctx_ev[(1, NT - 1)]
        projmm[pr] = tn.n

    @blk.gpsimd
    def _(gp):
        gp.wait_ge(tn.h, projmm[7])
        gp.dma_start(out=k.V(OFF_WS, 8192).rearrange("p (c j) -> p c j", c=NCH),
                     in_=k.di["f1_wo"].ap().rearrange("(c p) j -> p c j", p=128)
                     ).then_inc(wq.h, 16)
    wo_ld = wq.inc(16)
    woT_sb = k.V(OFF_WS, 8192)
    xn1b = [k.V(OFF_RELB, 1024), k.V(OFF_RELB + 2048, 1024)]
    xprb = [k.fscr[:, 0:1024], k.fscr[:, 1024:2048]]
    st_ev = {}
    vo_ev = {}
    for st in range(NT):
        pj = (k.P[0], k.P[1]) if st % 2 == 0 else (k.P[2], k.P[3])
        par = st % 2

        @blk.gpsimd
        def _(gp, st=st, par=par):
            if st >= 2:
                gp.wait_ge(vc.h, vo_ev[st - 2])
            gp.dma_start(out=xn1b[par], in_=k.sc_xn1.ap()[st * 128:(st + 1) * 128, :]
                         ).then_inc(ld.h, 16)
        ld_x = ld.inc(16)

        @blk.tensor
        def _(te, st=st, pj=pj, wo_ld=wo_ld):
            if st == 0:
                te.wait_ge(wq.h, wo_ld)
                te.wait_ge(vc.h, pair_done[7])
            if st >= 2:
                te.wait_ge(vc.h, vo_ev[st - 2])
            for c in range(NCH):
                for n_ in range(2):
                    mm = te.matmul(
                        out=pj[n_][:, 0:512],
                        lhsT=comb[:, c * 2048 + st * 128:c * 2048 + (st + 1) * 128],
                        rhs=woT_sb[:, c * 1024 + n_ * 512:c * 1024 + (n_ + 1) * 512],
                        start=(c == 0), stop=(c == NCH - 1))
            mm.then_inc(tn.h, 1)
        t_o = tn.inc(1)

        @blk.vector
        def _(v, st=st, pj=pj, par=par, t_o=t_o, ld_x=ld_x):
            v.wait_ge(tn.h, t_o)
            v.wait_ge(ld.h, ld_x)
            if st >= 2:
                v.wait_ge(dt_s.h, st_ev[st - 2])
            v.tensor_add(out=k.smb[:, 0:512], in0=pj[0][:, 0:512],
                         in1=k.bca[1][:, 0:512])
            v.tensor_add(out=k.smb[:, 512:1024], in0=pj[1][:, 0:512],
                         in1=k.bca[1][:, 512:1024])
            mm = v.tensor_add(out=xprb[par], in0=k.smb[:, 0:1024], in1=xn1b[par])
            mm.then_inc(vc.h, 1)
        vo_ev[st] = vc.inc(1)

        @blk.gpsimd
        def _(gp, st=st, par=par):
            gp.wait_ge(vc.h, vo_ev[st])
            gp.dma_start(out=k.sc_xpr.ap()[st * 128:(st + 1) * 128, :],
                         in_=xprb[par]).then_inc(dt_s.h, 16)
        st_ev[st] = dt_s.inc(16)
    k.ffn1a_done = dict(v=vc.n, s=sc.h and sc.n, t=tn.n, dt=dt_s.n)



def emit_final(k):
    nc, blk = k.nc, k.block
    ld, tn, vc, sc, gp_s, dt_s, st_s = (k.sems[n] for n in
                                        ["ld", "tn", "vc", "sc", "gp", "dt", "st"])
    x1b = [k.fscr[:, 0:1024], k.fscr[:, 1024:2048]]
    tmpb = [k.V(OFF_WTEB + 4096, 1024), k.V(OFF_WTEB + 6144, 1024)]
    xpr0 = k.V(OFF_RELB, 1024, F32)
    xpr1 = k.V(OFF_SLAB, 1024, F32)
    xprs = [xpr0, xpr1]
    sl1b = [k.V(OFF_ATTN, 1024), k.V(OFF_ATTN + 2048, 1024)]
    mpb = [k.V(OFF_ATTNT, 1024), k.V(OFF_ATTNT + 2048, 1024)]
    gb = [k.V(OFF_WTEB, 1024), k.V(OFF_WTEB + 2048, 1024)]
    dep = getattr(k, "moe_done", None)
    ev_v = {}
    ev_s = {}
    ev_v0 = {}
    st_w = {}

    for t in range(NT):
        par = t % 2

        @blk.gpsimd
        def _(gp, t=t, par=par):
            if t == 0 and dep is not None:
                gp.wait_ge(dt_s.h, dep["dt"])
            if t >= 2:
                gp.wait_ge(vc.h, ev_v[t - 2])
                gp.wait_ge(st_s.h, st_w[t - 2])
            gp.dma_start(out=x1b[par], in_=k.sc_x1.ap()[t * 128:(t + 1) * 128, :]
                         ).then_inc(ld.h, 16)
            gp.dma_start(out=xprs[par], in_=k.sc_xpr.ap()[t * 128:(t + 1) * 128, :]
                         ).then_inc(ld.h, 16)
            gp.dma_start(out=sl1b[par], in_=k.sc_sl1.ap()[t * 128:(t + 1) * 128, :]
                         ).then_inc(ld.h, 16)
            gp.dma_start(out=mpb[par].rearrange("p (c j) -> p c j", c=NCH),
                         in_=k.sc_moe1.ap()[:, :, t * 128:(t + 1) * 128]
                         .transpose([1, 0, 2])).then_inc(ld.h, 16)
        ldv = ld.inc(64)

        @blk.tensor
        def _(te, t=t, par=par, ldv=ldv):
            te.wait_ge(ld.h, ldv)
            te.wait_ge(gp_s.h, k.v_consts_g)
            if t == 0 and dep is not None:
                te.wait_ge(vc.h, dep["v"])
                te.wait_ge(sc.h, dep["s"])
            if t >= 2:
                te.wait_ge(sc.h, ev_s[t - 2])
            if t >= 1:
                te.wait_ge(vc.h, ev_v0[t - 1])
            for c in range(NCH):
                mm = te.transpose(out=k.Pb[2 * par][:, c * 128:(c + 1) * 128],
                                  in_=mpb[par][:, c * 128:(c + 1) * 128],
                                  identity=k.eye)
            mm.then_inc(tn.h, 1)
        t_tr = tn.inc(1)

        @blk.vector
        def _(v, t=t, par=par, t_tr=t_tr):
            v.wait_ge(tn.h, t_tr)
            mm = v.tensor_add(out=tmpb[par], in0=k.Pb[2 * par][:, 0:1024],
                              in1=xprs[par])
            mm.then_inc(vc.h, 1)
        v_m = vc.inc(1)
        ev_v0[t] = v_m

        @blk.scalar
        def _(s_, t=t, par=par, v_m=v_m):
            s_.wait_ge(vc.h, v_m)
            s_.activation(out=gb[par], in_=tmpb[par], func=AF.Silu).then_inc(sc.h, 1)
        ev_s[t] = sc.inc(1)

        @blk.vector
        def _(v, t=t, par=par):
            v.wait_ge(sc.h, ev_s[t])
            if t >= 2:
                v.wait_ge(st_s.h, st_w[t - 2])
            v.tensor_mul(out=gb[par], in0=sl1b[par], in1=gb[par])
            mm = v.tensor_add(out=x1b[par], in0=x1b[par], in1=gb[par])
            mm.then_inc(vc.h, 1)
        ev_v[t] = vc.inc(1)

        @blk.gpsimd
        def _(gp, t=t, par=par):
            gp.wait_ge(vc.h, ev_v[t])
            gp.dma_start(out=k.out.ap()[t * 128:(t + 1) * 128, :], in_=x1b[par]
                         ).then_inc(st_s.h, 16)
        st_w[t] = st_s.inc(16)

